# version 08
# baseline (speedup 1.0000x reference)
import numpy as np
import ml_dtypes

import concourse.bass as bass
import concourse.mybir as mybir
from concourse.bass_utils import run_bass_kernel_spmd

F32 = mybir.dt.float32
BF16 = mybir.dt.bfloat16
AF = mybir.ActivationFunctionType
ALU = mybir.AluOpType
AX = mybir.AxisListType
ts = bass.ts

D = 1024
TOK = 128
NT_OWN = 16
NT_EXT = 18
HEADS = 16
WIN = 384
DFF = 4096
NEXP = 8
DEXP = 512
EPS = 1e-5
INV_D = 1.0 / D
QSCALE = 0.125

ENGS = ("sync", "tensor", "vector", "scalar", "gpsimd")
DBG_SKIP = set()


class Sem:
    def __init__(self, h):
        self.h = h
        self.n = 0
        self.base = 0

    def bump(self, k=1):
        self.n += k
        return (self, k)


class Emitter:
    def __init__(self):
        self.streams = {e: [] for e in ENGS}

    def on(self, eng, fn):
        self.streams[eng].append(fn)

    def wait(self, eng, sem, val):
        if val > 0:
            self.on(eng, lambda e, s=sem.h, v=val: e.wait_ge(s, v))

    def op(self, eng, fn, inc=None):
        if inc is None:
            self.on(eng, fn)
        else:
            sem, k = inc
            self.on(eng, lambda e, f=fn, s=sem.h, kk=k: f(e).then_inc(s, kk))


class Arena:

    def __init__(self, nc, base, top):
        b0 = (base + 31) & ~31
        nbytes = ((top - b0) // 32) * 32
        big = nc.alloc_sbuf_tensor_at("arena", [128, nbytes], mybir.dt.uint8,
                                      offset=b0)
        self.views = {F32: big.bitcast(F32).ap(), BF16: big.bitcast(BF16).ap()}
        self.base = 0
        self.top = nbytes
        self.off = 0

    def alloc(self, name, shape, dtype):
        esz = 2 if dtype == BF16 else 4
        elems = int(np.prod(shape[1:]))
        off = (self.off + 31) & ~31
        assert off + elems * esz <= self.top, (
            f"SBUF arena overflow: {name} {elems * esz}B at {off}")
        c0 = off // esz
        self.off = off + elems * esz
        return self.views[dtype][0:shape[0], c0:c0 + elems]


def build_program(taps=(), nphases=6, a2_tiles=NT_OWN, a2_nopf=99):
    nc = bass.Bass("TRN2", target_bir_lowering=False, detect_race_conditions=False)

    def din(name, shape, dtype=F32):
        return nc.dram_tensor(name, list(shape), dtype, kind="ExternalInput").ap()

    x_ext = din("x_ext", [NT_EXT * TOK, D])
    wqkvT = din("wqkvT", [D, 3 * D])
    wprojT = din("wprojT", [D, D])
    wfc1T = din("wfc1T", [D, DFF])
    wfc2T = din("wfc2T", [DFF, D])
    w1_in = din("w1_in", [NEXP, D, DEXP])
    w2_in = din("w2_in", [NEXP, DEXP, D])
    wmod = din("wmod", [D, 6 * D])
    wrouteT = din("wrouteT", [D, NEXP])
    cent_in = din("cent_in", [NEXP, D])
    b2_in = din("b2_in", [NEXP, D])
    qkvb_row = din("qkvb_row", [1, 3 * D])
    projb_row = din("projb_row", [1, D])
    fc1b_fm_in = din("fc1b_fm_in", [128, 32])
    fc2b_row = din("fc2b_row", [1, D])
    b1_row = din("b1_row", [1, NEXP * DEXP])
    lnrow = din("lnrow", [1, 4 * D])
    c_row8 = din("c_row8", [8, 128])
    maskrow = din("maskrow", [1, NT_OWN * WIN])
    ident_bf_in = din("ident_bf_in", [128, 128], BF16)
    ident32_in = din("ident32_in", [128, 128])

    def scr(name, shape, dtype=F32):
        kind = "ExternalOutput" if name in taps else "Internal"
        return nc.dram_tensor(name, list(shape), dtype, kind=kind).ap()

    s_h = scr("s_h", [NT_OWN, TOK, D])
    s_x1 = scr("s_x1", [NT_OWN, TOK, D])
    s_sxa = scr("s_sxa", [NT_OWN, TOK, D])
    s_s2 = scr("s_s2", [NT_OWN, TOK, D])
    s_modrow = scr("s_modrow", [1, 6 * D])
    s_h2T = scr("s_h2T", [NT_OWN, TOK, D], BF16)
    s_dbg = scr("s_dbg", [128, 4 * D])
    s_dbg2 = scr("s_dbg2", [NT_EXT, 128, 8])
    s_dbgao = scr("s_dbgao", [NT_OWN, 128, D], BF16)
    q_scr = scr("q_scr", [8, 128, NT_EXT * TOK], BF16)
    k_scr = scr("k_scr", [8, 128, NT_EXT * TOK], BF16)
    v_scr = scr("v_scr", [NT_EXT, 128, D], BF16)
    y_out = nc.dram_tensor("y", [NT_OWN, TOK, D], F32, kind="ExternalOutput").ap()

    def sb(name, shape, dtype):
        return nc.alloc_sbuf_tensor(name, list(shape), dtype).ap()

    ones_bf = sb("ones_bf", [1, 512], BF16)
    ones32 = sb("ones32", [1, 128], F32)
    ident_bf = sb("ident_bf", [128, 128], BF16)
    ident32 = sb("ident32", [128, 128], F32)
    b2_bf = sb("b2_bf", [NEXP, D], BF16)
    cnT_bf = sb("cnT_bf", [128, 64], BF16)
    wrT_bf = sb("wrT_bf", [128, 64], BF16)
    repl = sb("repl", [128, 4 * D], F32)
    stats_ln2 = sb("stats_ln2", [128, 2 * NT_OWN], F32)
    stats_s2 = sb("stats_s2", [128, 2 * NT_OWN], F32)

    def rs(i):
        return repl[:, i * D:(i + 1) * D]

    A = Arena(nc, nc.sbuf_base, nc.sbuf_top)
    banks = [nc.alloc_psum_tensor(f"bank{i}", [128, 512], F32) for i in range(8)]

    def bf(i):
        return banks[i].ap()

    def bb(i):
        return banks[i].bitcast(BF16).ap()

    E = Emitter()
    sem_stack = []
    sem_pool = []
    store_pool = []
    pool_state = {"i": 0, "j": 0}

    def _alloc_sem(name):
        cm = nc.semaphore(name)
        h = cm.__enter__()
        sem_stack.append(cm)
        return Sem(h)

    def mksem(name):
        i = pool_state["i"]
        if i >= len(sem_pool):
            sem_pool.append(_alloc_sem(f"pool{i}"))
        pool_state["i"] = i + 1
        return sem_pool[i]

    def mkstore(name):
        j = pool_state["j"]
        if j >= len(store_pool):
            store_pool.append(_alloc_sem(f"spool{j}"))
        pool_state["j"] = j + 1
        return store_pool[j]

    hw_pool = []

    def mkdma(name):
        k = pool_state.get("k", 0)
        if k >= len(hw_pool):
            hw_pool.append(_alloc_sem(f"hpool{k}"))
        pool_state["k"] = k + 1
        return hw_pool[k]

    def pool_reset():
        pool_state["i"] = 0
        pool_state["j"] = 0
        pool_state["k"] = 0
        for s in sem_pool + store_pool + hw_pool:
            s.base = s.n

    def waitr(eng, sem, rel):
        if rel > 0:
            E.wait(eng, sem, sem.base + rel)

    dv = {}

    def dma(eng, fn, sem, label=None):
        E.op(eng, fn, inc=sem.bump(16))
        if label is not None:
            dv[label] = (sem, sem.n)

    def wdma(eng, label):
        sv = dv.get(label)
        if sv is not None:
            E.wait(eng, sv[0], sv[1])

    BAR = _alloc_sem("BAR")

    def finalize():
        with nc.Block() as block:
            for eng in ENGS:
                getattr(block, eng)(
                    (lambda fns: (lambda e: [f(e) for f in fns]))(
                        E.streams[eng]))
        return nc

    def barrier(clear=True):
        for eng in ENGS:
            E.op(eng, lambda e: e.drain(), inc=BAR.bump())
        tgt = BAR.n
        for eng in ENGS:
            E.wait(eng, BAR, tgt)
        del clear

    def emit_lnstats(X, sm, sems, rstd_out, nmr_out, sq_trash,
                     reduce_inc=None):
        SC = sems["SC"]
        if reduce_inc is None:
            reduce_inc = SC.bump()
        rsem = reduce_inc[0]
        E.op("vector", lambda e: e.reduce_sum(sm["s1"], X, axis=AX.X),
             inc=reduce_inc)
        E.op("scalar", lambda e: e.activation(sq_trash, X, AF.Square,
                                              accum_out=sm["varb"]),
             inc=sems["SB"].bump())
        E.wait("vector", rsem, rsem.n)
        E.op("vector", lambda e: e.tensor_scalar(sm["mu"], sm["s1"], INV_D,
                                                 None, ALU.mult),
             inc=SC.bump())
        E.wait("vector", sems["SB"], sems["SB"].n)
        E.wait("vector", SC, SC.n)
        E.op("vector", lambda e: e.tensor_mul(sm["mu2"], sm["mu"], sm["mu"]),
             inc=SC.bump())
        E.wait("vector", SC, SC.n)
        E.op("vector", lambda e: e.scalar_tensor_tensor(
            sm["var"], sm["varb"], INV_D, sm["mu2"], ALU.mult, ALU.subtract),
            inc=SC.bump())
        E.wait("vector", SC, SC.n)
        E.op("vector", lambda e: e.tensor_scalar(sm["var"], sm["var"], EPS,
                                                 None, ALU.add),
             inc=SC.bump())
        E.wait("scalar", SC, SC.n)
        E.op("scalar", lambda e: e.activation(sm["lnv"], sm["var"], AF.Ln),
             inc=sems["SD"].bump())
        E.wait("scalar", sems["SD"], sems["SD"].n)
        E.op("scalar", lambda e, r=rstd_out: e.activation(r, sm["lnv"], AF.Exp,
                                                          scale=-0.5),
             inc=sems["SD"].bump())
        E.wait("vector", sems["SD"], sems["SD"].n)
        E.op("vector", lambda e, r=rstd_out, o=nmr_out: e.tensor_scalar(
            o, sm["mu"], r, -1.0, ALU.mult, ALU.mult), inc=sems["SE"].bump())

    RMM = _alloc_sem("RMM")
    REV = _alloc_sem("REV")

    def replicate_row(src_row, dst, b0, b1, add_one=False):
        E.wait("tensor", REV, REV.n)
        E.op("tensor", lambda e: e.matmul(bf(b0)[:, :], ones32[0:1, 0:128],
                                          src_row[0:1, 0:512],
                                          start=True, stop=True))
        E.op("tensor", lambda e: e.matmul(bf(b1)[:, :], ones32[0:1, 0:128],
                                          src_row[0:1, 512:1024],
                                          start=True, stop=True),
             inc=RMM.bump())
        E.wait("scalar", RMM, RMM.n)
        bias = 1.0 if add_one else 0.0
        E.op("scalar", lambda e: e.activation(dst[:, 0:512], bf(b0)[:, :],
                                              AF.Identity, bias=bias))
        E.op("scalar", lambda e: e.activation(dst[:, 512:1024], bf(b1)[:, :],
                                              AF.Identity, bias=bias),
             inc=REV.bump())

    LD0 = mkdma("LD0")
    LDmod = [mkdma("LDmod0"), mkdma("LDmod1")]
    LDrow = mkdma("LDrow")
    CAST0 = mksem("CAST0")
    MODMM = mksem("MODMM")
    MODEV = mksem("MODEV")
    P0A = mksem("P0A")
    P0B = mksem("P0B")
    P0C = mksem("P0C")
    ST0 = mkstore("ST0")

    wstage = [A.alloc(f"wst0_{i}", [128, 3072], F32) for i in range(2)]
    wmod_bf = A.alloc("wmod_bf", [128, 8 * 6144], BF16)
    modrow = A.alloc("modrow", [1, 6144], F32)
    c8 = A.alloc("c8", [8, 128], F32)
    cfm_bf = A.alloc("cfm_bf", [128, 8], BF16)
    cent = A.alloc("cent", [8, D], F32)
    centsq = A.alloc("centsq", [8, D], F32)
    css = A.alloc("css", [8, 2], F32)
    cnorm_bf = A.alloc("cnorm_bf", [8, D], BF16)
    wr_st = A.alloc("wr_st", [128, 64], F32)
    b2_st = A.alloc("b2_st", [8, D], F32)
    row_a = A.alloc("row_a", [1, D], F32)
    trow = [A.alloc(f"trow{i}", [1, D], F32) for i in range(2)]

    for dst, src in [(ident_bf[:, :], ident_bf_in[:, :]),
                     (ident32[:, :], ident32_in[:, :]),
                     (c8[:, :], c_row8[:, :]),
                     (cent[:, :], cent_in[:, :]),
                     (b2_st[:, :], b2_in[:, :])]:
        E.op("sync", lambda e, d=dst, s=src: e.dma_start(d, s), inc=LD0.bump(16))
    wr_src = bass.AP(wrouteT.tensor, 0, [[8, 128], [1024, 8], [1, 8]])
    E.op("sync", lambda e, s=wr_src: e.dma_start(
        wr_st.rearrange("p (a b) -> p a b", a=8), s), inc=LD0.bump(16))
    ld_pre = LD0.n

    E.op("vector", lambda e: e.memset(ones_bf[:, :], 1.0))
    E.op("vector", lambda e: e.memset(ones32[:, :], 1.0), inc=P0A.bump())
    E.wait("vector", LD0, ld_pre)
    E.op("vector", lambda e: e.tensor_copy(wrT_bf[:, :], wr_st[:, :]))
    E.op("vector", lambda e: e.tensor_copy(b2_bf[:, :], b2_st[:, :]))

    E.wait("scalar", LD0, ld_pre)
    E.op("scalar", lambda e: e.activation(c8[:, :], c8[:, :], AF.Silu),
         inc=P0B.bump())
    E.wait("tensor", P0B, P0B.n)
    E.wait("tensor", LD0, ld_pre)
    E.op("tensor", lambda e: e.transpose(bf(6)[0:128, 0:8], c8[0:8, 0:128],
                                         ident32[0:8, 0:8]), inc=MODMM.bump())
    E.wait("scalar", MODMM, MODMM.n)
    E.op("scalar", lambda e: e.activation(cfm_bf[:, :], bf(6)[0:128, 0:8],
                                          AF.Copy), inc=MODEV.bump())

    for k in range(8):
        for h in range(2):
            i = k * 2 + h
            if i >= 2:
                waitr("sync", CAST0, i - 1)
            E.op("sync", lambda e, kk=k, hh=h, ii=i: e.dma_start(
                wstage[ii % 2][:, :],
                wmod[ts(kk, 128), hh * 3072:(hh + 1) * 3072]),
                inc=LDmod[i % 2].bump(16))
            E.wait("vector", LDmod[i % 2], LDmod[i % 2].n)
            E.op("vector", lambda e, kk=k, hh=h, ii=i: e.tensor_copy(
                wmod_bf[:, kk * 6144 + hh * 3072: kk * 6144 + (hh + 1) * 3072],
                wstage[ii % 2][:, :]), inc=CAST0.bump())
    cast_all = CAST0.n

    E.wait("tensor", CAST0, cast_all)
    E.wait("tensor", MODEV, MODEV.n)
    for half in range(2):
        if half == 1:
            E.wait("tensor", MODEV, MODEV.n)
        for k in range(8):
            for nch in range(6):
                col = half * 3072 + nch * 512
                E.op("tensor", lambda e, kk=k, nn=nch, cc=col: e.matmul(
                    bf(nn)[0:1, :], cfm_bf[:, kk:kk + 1],
                    wmod_bf[:, kk * 6144 + cc: kk * 6144 + cc + 512],
                    start=(kk == 0), stop=(kk == 7)))
        E.op("tensor", lambda e: e.drain(), inc=MODMM.bump())
        E.wait("scalar", MODMM, MODMM.n)
        for nch in range(6):
            col = half * 3072 + nch * 512
            E.op("scalar", lambda e, nn=nch, cc=col: e.activation(
                modrow[0:1, cc:cc + 512], bf(nn)[0:1, :], AF.Identity),
                inc=MODEV.bump())

    E.wait("vector", MODEV, MODEV.n)
    E.op("vector", lambda e: e.tensor_scalar_add(modrow[0:1, 1024:2048],
                                                 modrow[0:1, 1024:2048], 1.0))
    E.op("vector", lambda e: e.tensor_scalar_add(modrow[0:1, 4096:5120],
                                                 modrow[0:1, 4096:5120], 1.0))
    for gc in (2048, 5120):
        g = modrow[0:1, gc:gc + 1024]
        E.op("vector", lambda e, gg=g: e.tensor_scalar(trow[0][0:1, :], gg,
                                                       -1.0, None, ALU.mult),
             inc=P0C.bump())
        E.wait("vector", P0C, P0C.n)
        E.op("vector", lambda e, gg=g: e.tensor_max(trow[0][0:1, :],
                                                    trow[0][0:1, :], gg),
             inc=P0C.bump())
        E.wait("vector", P0C, P0C.n)
        E.op("vector", lambda e: e.tensor_scalar_max(trow[0][0:1, :],
                                                     trow[0][0:1, :], 0.25),
             inc=P0C.bump())
        E.wait("vector", P0C, P0C.n)
        E.op("vector", lambda e, gg=g: e.tensor_scalar(trow[1][0:1, :], gg, 0.0,
                                                       None, ALU.is_ge),
             inc=P0C.bump())
        E.wait("vector", P0C, P0C.n)
        E.op("vector", lambda e: e.tensor_scalar(trow[1][0:1, :],
                                                 trow[1][0:1, :], 2.0, -1.0,
                                                 ALU.mult, ALU.add),
             inc=P0C.bump())
        E.wait("vector", P0C, P0C.n)
        E.op("vector", lambda e, gg=g: e.tensor_mul(gg, trow[1][0:1, :],
                                                    trow[0][0:1, :]),
             inc=P0C.bump())
        E.wait("vector", P0C, P0C.n)
    E.op("vector", lambda e: e.tensor_copy(trow[0][0:1, 0:8], modrow[0:1, 0:8]),
         inc=P0C.bump())
    E.wait("scalar", P0C, P0C.n)
    E.op("scalar", lambda e: e.dma_start(s_modrow[0:1, :], modrow[0:1, :]),
         inc=ST0.bump(16))

    E.wait("tensor", P0C, P0C.n)
    replicate_row(modrow[0:1, 0:1024], rs(0), 0, 1)
    replicate_row(modrow[0:1, 1024:2048], rs(1), 2, 3)
    replicate_row(modrow[0:1, 2048:3072], rs(2), 4, 5)
    E.op("sync", lambda e: e.dma_start(row_a[0:1, :], lnrow[0:1, 0:1024]),
         inc=LDrow.bump(16))
    E.wait("tensor", LDrow, LDrow.n)
    replicate_row(row_a[0:1, :], rs(3), 0, 1)

    E.wait("scalar", LD0, ld_pre)
    E.op("scalar", lambda e: e.activation(centsq[0:8, :], cent[0:8, :],
                                          AF.Square, accum_out=css[0:8, 0:1]),
         inc=P0B.bump())
    E.wait("scalar", P0B, P0B.n)
    E.op("scalar", lambda e: e.activation(css[0:8, 1:2], css[0:8, 0:1], AF.Ln),
         inc=P0B.bump())
    E.wait("scalar", P0B, P0B.n)
    E.op("scalar", lambda e: e.activation(css[0:8, 0:1], css[0:8, 1:2], AF.Exp,
                                          scale=0.5), inc=P0B.bump())
    E.wait("vector", P0B, P0B.n)
    E.op("vector", lambda e: e.tensor_scalar_max(css[0:8, 0:1], css[0:8, 0:1],
                                                 1e-12), inc=P0C.bump())
    E.wait("vector", P0C, P0C.n)
    E.op("vector", lambda e: e.reciprocal(css[0:8, 1:2], css[0:8, 0:1]),
         inc=P0C.bump())
    E.wait("vector", P0C, P0C.n)
    E.op("vector", lambda e: e.tensor_scalar(cnorm_bf[0:8, :], cent[0:8, :],
                                             css[0:8, 1:2], None, ALU.mult),
         inc=P0C.bump())
    E.wait("tensor", P0C, P0C.n)
    E.wait("tensor", REV, REV.n)
    for c in range(8):
        E.op("tensor", lambda e, cc=c: e.transpose(
            bb(6)[0:128, cc * 8:(cc + 1) * 8], cnorm_bf[0:8, ts(cc, 128)],
            ident_bf[0:8, 0:8]), inc=MODMM.bump() if c == 7 else None)
    E.wait("vector", MODMM, MODMM.n)
    E.op("vector", lambda e: e.tensor_copy(cnT_bf[:, :], bb(6)[0:128, 0:64]))

    if "s_dbg" in taps:
        E.wait("scalar", REV, REV.n)
        E.op("scalar", lambda e: e.dma_start(s_dbg[:, :], repl[:, :]),
             inc=ST0.bump(16))
    E.wait("scalar", ST0, ST0.n)
    barrier()
    if nphases == 1:
        return finalize()

    pool_reset()
    A.off = A.base
    wproj_bf = A.alloc("wproj_bf", [128, 8 * D], BF16)
    projb_bf = A.alloc("projb_bf", [1, D], BF16)
    maskrow_bf = A.alloc("maskrow_bf", [1, NT_OWN * WIN], BF16)
    ckptA = A.off
    wqkv_bf = A.alloc("wqkv_bf", [128, 8 * 3072], BF16)
    wstA = [A.alloc(f"wstA{i}", [128, 1536], F32) for i in range(2)]
    qkvb_bf = A.alloc("qkvb_bf", [1, 3 * D], BF16)
    rowstA = A.alloc("rowstA", [1, 3072], F32)
    lnb1_repl = A.alloc("lnb1_repl", [128, D], F32)
    x_ring = [A.alloc(f"xr{i}", [128, D], F32) for i in range(2)]
    h_ring = [A.alloc(f"hr{i}", [128, D], F32) for i in range(2)]
    u_buf = A.alloc("u_buf", [128, D], F32)
    u1_buf = A.alloc("u1_buf", [128, D], F32)
    h1_buf = A.alloc("h1_buf", [128, D], F32)
    a1_buf = A.alloc("a1_buf", [128, D], F32)
    abf_ring = [A.alloc(f"abf{i}", [128, D], BF16) for i in range(2)]
    sq_tr = A.alloc("sq_tr", [128, D], F32)
    aT_gr = [A.alloc(f"aT_g{i}", [128, 8 * 512], BF16) for i in range(2)]
    qk_st = [A.alloc(f"qkst{i}", [128, 512], BF16) for i in range(2)]
    v_st = [A.alloc(f"vst{i}", [128, D], BF16) for i in range(2)]
    smA = {k: A.alloc(f"smA_{k}", [128, 1], F32) for k in
           ("s1", "mu", "mu2", "varb", "var", "lnv", "rstd", "nmr",
            "s1b", "mub", "mu2b", "varbb", "varB", "lnvb", "rstdb", "nmrb")}
    sm1 = {k: smA[k] for k in ("s1", "mu", "mu2", "varb", "var", "lnv")}
    sm2 = {"s1": smA["s1b"], "mu": smA["mub"], "mu2": smA["mu2b"],
           "varb": smA["varbb"], "var": smA["varB"], "lnv": smA["lnvb"]}

    LDWs = [mkdma("LDWa"), mkdma("LDWb")]
    LDROW = mkdma("LDROW")
    WCAST = _alloc_sem("WCAST")
    LDXs = [mkdma("LDXa"), mkdma("LDXb")]
    SXR = mksem("SXR")
    SUX = mksem("SUX")
    SUH = mksem("SUH")
    SH = mksem("SH")
    SABF = mksem("SABF")
    STR = mksem("STR")
    SATEV = mksem("SATEV")
    SQMM = mksem("SQMM")
    SQEV = mksem("SQEV")
    SVMM = mksem("SVMM")
    SVEV = mksem("SVEV")
    STHs = [mkstore("STHa"), mkstore("STHb")]
    STQs = [mkstore("STQa"), mkstore("STQb")]
    STVs = [mkstore("STVa"), mkstore("STVb")]
    SLN1 = {k: mksem(f"A1a{k}") for k in ("SB", "SC", "SD", "SE")}
    SLN2 = {k: mksem(f"A1b{k}") for k in ("SB", "SC", "SD", "SE")}
    CH1 = mksem("CH1")

    dma("sync", lambda e: e.dma_start(rowstA[0:1, 0:1024],
                                      lnrow[0:1, 1024:2048]), LDROW)
    E.wait("tensor", LDROW, LDROW.n)
    replicate_row(rowstA[0:1, 0:1024], lnb1_repl, 0, 1)
    repl_a_done = REV.n

    E.wait("sync", RMM, RMM.n)
    E.op("sync", lambda e: e.dma_start(rowstA[0:1, :], qkvb_row[0:1, :]),
         inc=LDROW.bump(16))
    E.wait("vector", LDROW, LDROW.n)
    E.op("vector", lambda e: e.tensor_copy(qkvb_bf[0:1, :], rowstA[0:1, :]),
         inc=WCAST.bump())
    wc0 = WCAST.n
    for p in range(3):
        for k in range(8):
            i = p * 8 + k
            if i >= 2:
                E.wait("sync", WCAST, wc0 + i - 1)
            sW = LDWs[i % 2]
            E.op("sync", lambda e, kk=k, pp=p, ii=i: e.dma_start(
                wstA[ii % 2][:, 0:1024],
                wqkvT[ts(kk, 128), pp * 1024:(pp + 1) * 1024]),
                inc=sW.bump(16))
            E.wait("vector", sW, sW.n)
            E.op("vector", lambda e, kk=k, pp=p, ii=i: e.tensor_copy(
                wqkv_bf[:, kk * 3072 + pp * 1024: kk * 3072 + (pp + 1) * 1024],
                wstA[ii % 2][:, 0:1024]), inc=WCAST.bump())
        if p == 0:
            wcast_q = WCAST.n
        elif p == 1:
            wcast_k = WCAST.n
    wcast_v = WCAST.n
    E.wait("sync", WCAST, WCAST.n)
    E.op("sync", lambda e: e.dma_start(rowstA[0:1, 0:3072],
                                       maskrow[0:1, 0:3072]), inc=LDROW.bump(16))
    E.wait("vector", LDROW, LDROW.n)
    E.op("vector", lambda e: e.tensor_copy(maskrow_bf[0:1, 0:3072],
                                           rowstA[0:1, 0:3072]),
         inc=WCAST.bump())
    E.wait("sync", WCAST, WCAST.n)
    E.op("sync", lambda e: e.dma_start(rowstA[0:1, 0:3072],
                                       maskrow[0:1, 3072:6144]),
         inc=LDROW.bump(16))
    E.wait("vector", LDROW, LDROW.n)
    E.op("vector", lambda e: e.tensor_copy(maskrow_bf[0:1, 3072:6144],
                                           rowstA[0:1, 0:3072]),
         inc=WCAST.bump())
    wcast_qkv = WCAST.n

    def own_count(t):
        return max(0, min(t, 16))

    dma("sync", lambda e: e.dma_start(x_ring[0][:, :], x_ext[ts(0, 128), :]),
        LDXs[0], label=("a1x", 0))

    groups = [(0, 4), (4, 8), (8, 12), (12, 16), (16, 18)]

    def prep_tile_a1(t, lt, slot):
        xb = x_ring[t % 2]
        hb = h_ring[t % 2]
        ab = abf_ring[t % 2]
        if t + 1 < NT_EXT:
            waitr("sync", SXR, t)
            waitr("sync", SUX, t)
            dma("sync", lambda e, tt=t + 1: e.dma_start(
                x_ring[tt % 2][:, :], x_ext[ts(tt, 128), :]),
                LDXs[(t + 1) % 2], label=("a1x", t + 1))
        wdma("vector", ("a1x", t))
        wdma("scalar", ("a1x", t))
        emit_lnstats(xb[:, :], sm1, SLN1, smA["rstd"], smA["nmr"], sq_tr,
                     reduce_inc=SXR.bump())
        E.wait("scalar", SLN1["SE"], SLN1["SE"].n)
        waitr("scalar", SH, t)
        E.op("scalar", lambda e, x=xb: e.activation(
            u_buf[:, :], x[:, :], AF.Identity, scale=smA["rstd"],
            bias=smA["nmr"]), inc=SUX.bump())
        E.wait("vector", SUX, SUX.n)
        if t == 0:
            E.wait("vector", REV, repl_a_done)
        if t >= 2:
            wdma("vector", ("a1h", t - 2))
            waitr("vector", SUH, t - 1)
        E.op("vector", lambda e: e.scalar_tensor_tensor(
            h1_buf[:, :], u_buf[:, :], 1.0, rs(1), ALU.mult, ALU.mult),
            inc=CH1.bump())
        E.wait("vector", CH1, CH1.n)
        E.op("vector", lambda e, h=hb: e.scalar_tensor_tensor(
            h[:, :], h1_buf[:, :], 1.0, rs(0), ALU.mult, ALU.add),
            inc=SH.bump())
        if 1 <= t <= 16:
            E.wait("scalar", SH, SH.n)
            dma("scalar", lambda e, tt=t, h=hb: e.dma_start(
                s_h[tt - 1, :, :], h[:, :]), STHs[t % 2], label=("a1h", t))
        E.wait("scalar", SH, SH.n)
        E.wait("vector", SH, SH.n)
        emit_lnstats(hb[:, :], sm2, SLN2, smA["rstdb"], smA["nmrb"], sq_tr)
        E.wait("scalar", SLN2["SE"], SLN2["SE"].n)
        waitr("scalar", SABF, t)
        E.op("scalar", lambda e, h=hb: e.activation(
            u1_buf[:, :], h[:, :], AF.Identity, scale=smA["rstdb"],
            bias=smA["nmrb"]), inc=SUH.bump())
        E.wait("vector", SUH, SUH.n)
        waitr("vector", STR, t - 1)
        E.op("vector", lambda e: e.scalar_tensor_tensor(
            a1_buf[:, :], u1_buf[:, :], 1.0, rs(3), ALU.mult, ALU.mult),
            inc=CH1.bump())
        E.wait("vector", CH1, CH1.n)
        E.op("vector", lambda e, a=ab: e.scalar_tensor_tensor(
            a[:, :], a1_buf[:, :], 1.0, lnb1_repl, ALU.mult, ALU.add),
            inc=SABF.bump())
        E.wait("tensor", SABF, SABF.n)
        waitr("tensor", SATEV, t - 1)
        for c in range(8):
            E.op("tensor", lambda e, cc=c, a=ab, bk=t % 2: e.transpose(
                bb(bk)[:, ts(cc, 128)], a[:, ts(cc, 128)], ident_bf[:, :]),
                inc=STR.bump() if c == 7 else None)
        E.wait("vector", STR, STR.n)
        dstA = aT_gr[slot].rearrange("p (c w) -> p c w", c=8)[
            :, :, lt * 128:(lt + 1) * 128]
        E.op("vector", lambda e, d=dstA, bk=t % 2: e.tensor_copy(
            d, bb(bk).rearrange("p (c w) -> p c w", c=8)),
            inc=SATEV.bump())

    for t in range(4):
        prep_tile_a1(t, t, 0)
    qk_use = 0
    for gi, (t0, t1) in enumerate(groups):
        ng = (t1 - t0) * 128
        nxt = groups[gi + 1] if gi + 1 < len(groups) else None
        aTb = aT_gr[gi % 2]
        E.wait("tensor", SATEV, SATEV.n)
        for m in range(16):
            if gi == 0 and m == 0:
                E.wait("tensor", WCAST, wcast_q)
            if gi == 0 and m == 8:
                E.wait("tensor", WCAST, wcast_k)
            is_q = m < 8
            mm8 = m % 8
            colw = mm8 * 128 if is_q else 1024 + mm8 * 128
            bk = 2 + (qk_use % 2)
            waitr("tensor", SQEV, qk_use - 1)
            for k in range(8):
                E.op("tensor", lambda e, kk=k, cw=colw, b=bk, nn=ng, aT=aTb:
                     e.matmul(
                         bf(b)[:, 0:nn],
                         wqkv_bf[:, kk * 3072 + cw: kk * 3072 + cw + 128],
                         aT[:, kk * 512: kk * 512 + nn],
                         start=(kk == 0), stop=False))
            E.op("tensor", lambda e, cw=colw, b=bk, nn=ng: e.matmul(
                bf(b)[:, 0:nn], qkvb_bf[0:1, cw:cw + 128], ones_bf[0:1, 0:nn],
                start=False, stop=True), inc=SQMM.bump())
            E.wait("scalar", SQMM, SQMM.n)
            wdma("scalar", ("a1qk", qk_use - 2))
            sc = QSCALE if is_q else 1.0
            E.op("scalar", lambda e, b=bk, s=sc, st=qk_st[qk_use % 2], nn=ng:
                 e.activation(st[:, 0:nn], bf(b)[:, 0:nn], AF.Copy, scale=s),
                 inc=SQEV.bump())
            E.wait("scalar", SQEV, SQEV.n)
            dscr = q_scr if is_q else k_scr
            dma("scalar", lambda e, d=dscr, mmm=mm8, st=qk_st[qk_use % 2],
                c0=t0 * 128, nn=ng: e.dma_start(d[mmm, :, c0:c0 + nn],
                                                st[:, 0:nn]),
                STQs[qk_use % 2], label=("a1qk", qk_use))
            qk_use += 1
            if nxt is not None and m % 4 == 3:
                j = m // 4
                if nxt[0] + j < nxt[1]:
                    prep_tile_a1(nxt[0] + j, j, (gi + 1) % 2)
        if gi == 0:
            E.wait("tensor", WCAST, wcast_v)
        for t in range(t0, t1):
            lt = t - t0
            bks = (4, 5) if (t % 2 == 0) else (6, 7)
            E.wait("tensor", SVEV, max(0, SVMM.n - 1))
            for n in range(2):
                for k in range(8):
                    E.op("tensor", lambda e, kk=k, nn=n, b=bks[n], l=lt,
                         aT=aTb: e.matmul(
                             bf(b)[:, :],
                             aT[:, kk * 512 + l * 128: kk * 512 + l * 128 + 128],
                             wqkv_bf[:, kk * 3072 + 2048 + nn * 512:
                                     kk * 3072 + 2048 + nn * 512 + 512],
                             start=(kk == 0), stop=False))
                E.op("tensor", lambda e, nn=n, b=bks[n]: e.matmul(
                    bf(b)[:, :], ones_bf[0:1, 0:128],
                    qkvb_bf[0:1, 2048 + nn * 512: 2048 + nn * 512 + 512],
                    start=False, stop=True),
                    inc=SVMM.bump() if n == 1 else None)
            E.wait("scalar", SVMM, SVMM.n)
            wdma("scalar", ("a1v", t - 2))
            E.op("scalar", lambda e, b=bks[0], st=v_st[t % 2]: e.activation(
                st[:, 0:512], bf(b)[:, :], AF.Copy))
            E.op("scalar", lambda e, b=bks[1], st=v_st[t % 2]: e.activation(
                st[:, 512:1024], bf(b)[:, :], AF.Copy), inc=SVEV.bump())
            E.wait("scalar", SVEV, SVEV.n)
            dma("scalar", lambda e, tt=t, st=v_st[t % 2]: e.dma_start(
                v_scr[tt, :, :], st[:, :]), STVs[t % 2], label=("a1v", t))

    for k in range(8):
        E.wait("sync", WCAST, wcast_v if k < 2 else wcast_qkv + k - 1)
        sW = LDWs[k % 2]
        E.op("sync", lambda e, kk=k: e.dma_start(
            wstA[kk % 2][:, 0:1024], wprojT[ts(kk, 128), :]), inc=sW.bump(16))
        E.wait("vector", sW, sW.n)
        E.op("vector", lambda e, kk=k: e.tensor_copy(
            wproj_bf[:, kk * 1024:(kk + 1) * 1024], wstA[kk % 2][:, 0:1024]),
            inc=WCAST.bump())
    E.wait("sync", WCAST, WCAST.n)
    E.op("sync", lambda e: e.dma_start(rowstA[0:1, 0:1024], projb_row[0:1, :]),
         inc=LDROW.bump(16))
    E.wait("vector", LDROW, LDROW.n)
    E.op("vector", lambda e: e.tensor_copy(projb_bf[0:1, :], rowstA[0:1, 0:1024]),
         inc=WCAST.bump())
    wcast_proj = WCAST.n

    for s in STHs + STQs + STVs:
        E.wait("scalar", s, s.n)
    barrier()
    if nphases == 2:
        return finalize()

    pool_reset()
    A.off = ckptA
    k_ring = [A.alloc(f"kring{i}", [128, 8 * 128], BF16) for i in range(4)]
    v_ring = [A.alloc(f"vring{i}", [128, D], BF16) for i in range(4)]
    q_ring = [A.alloc(f"qring{i}", [128, 8 * 512], BF16) for i in range(4)]
    h2_ring = [A.alloc(f"h2r{i}", [128, D], F32) for i in range(2)]
    x2_ring = [A.alloc(f"x2r{i}", [128, D], F32) for i in range(2)]
    p_sb = [A.alloc(f"psb{i}", [128, WIN], BF16) for i in range(3)]
    pT_sb = [A.alloc(f"pTsb{i}", [128, WIN], BF16) for i in range(3)]
    ao_sb = A.alloc("ao_sb", [128, D], BF16)
    aoT_sb = A.alloc("aoT_sb", [128, D], BF16)
    x1_sb = [A.alloc(f"x1sb{i}", [128, D], F32) for i in range(2)]
    sxa_sb = [A.alloc(f"sxasb{i}", [128, D], F32) for i in range(2)]
    sxa_t1 = A.alloc("sxa_t1", [128, D], F32)
    ssum = A.alloc("ssum", [128, HEADS], F32)
    rrb = A.alloc("rrb", [128, HEADS], F32)
    sq_tr2 = A.alloc("sq_tr2", [128, D], F32)
    s1A2 = A.alloc("s1A2", [128, NT_OWN], F32)
    vbA2 = A.alloc("vbA2", [128, NT_OWN], F32)
    muA2 = A.alloc("muA2", [128, NT_OWN], F32)
    t1A2 = A.alloc("t1A2", [128, NT_OWN], F32)

    LKVs = [mkdma(f"LKV{i}") for i in range(4)]
    LVVs = [mkdma(f"LVV{i}") for i in range(4)]
    LQs = [mkdma(f"LQ{i}") for i in range(2)]
    LHs = [mkdma(f"LH{i}") for i in range(2)]
    LX2s = [mkdma(f"LX2{i}") for i in range(2)]
    SSC = mksem("SSC")
    SEXP = mksem("SEXP")
    SRR = mksem("SRR")
    SPT = mksem("SPT")
    SPTE = mksem("SPTE")
    SVM2 = mksem("SVM2")
    SOEV = mksem("SOEV")
    SAOT = mksem("SAOT")
    SAOE = mksem("SAOE")
    SPRJ = mksem("SPRJ")
    SX1 = mksem("SX1")
    SSXA = mksem("SSXA")
    STX1s = [mkstore("STX1a"), mkstore("STX1b")]
    STSXAs = [mkstore("STSXAa"), mkstore("STSXAb")]
    SLNA2 = {k: mksem(f"A2c{k}") for k in ("SB", "SC", "SD", "SE")}
    CH2 = mksem("CH2")

    def kload(slot_ext):
        src = bass.AP(k_scr.tensor, slot_ext * 128,
                      [[NT_EXT * TOK, 128], [128 * NT_EXT * TOK, 8], [1, 128]])
        dst = k_ring[slot_ext % 4].rearrange("p (c w) -> p c w", c=8)
        return dst, src

    for s in range(3):
        dst, src = kload(s)
        dma("sync", lambda e, d=dst, ss=src: e.dma_start(d, ss),
            LKVs[s % 4], label=("kv", s))
        dma("sync", lambda e, d=v_ring[s % 4], ss=s: e.dma_start(
            d[:, :], v_scr[ss, :, :]), LVVs[s % 4], label=("vv", s))
    qsrc0 = bass.AP(q_scr.tensor, 128,
                    [[NT_EXT * TOK, 128], [128 * NT_EXT * TOK, 8], [1, 512]])
    dma("sync", lambda e, s=qsrc0: e.dma_start(
        q_ring[0].rearrange("p (c w) -> p c w", c=8), s), LQs[0],
        label=("qg", 0))
    dma("sync", lambda e: e.dma_start(h2_ring[0][:, :], s_h[0, :, :]),
        LHs[0], label=("h", 0))
    dma("sync", lambda e: e.dma_start(x2_ring[0][:, :], x_ext[ts(1, 128), :]),
        LX2s[0], label=("x2", 0))

    for t in range(min(NT_OWN, a2_tiles)):
        g = t // 4
        if t + 3 < NT_EXT and t < a2_nopf:
            waitr("sync", SVM2, HEADS * t)
            dst, src = kload(t + 3)
            dma("sync", lambda e, d=dst, ss=src: e.dma_start(d, ss),
                LKVs[(t + 3) % 4], label=("kv", t + 3))
            dma("sync", lambda e, d=v_ring[(t + 3) % 4], ss=t + 3:
                e.dma_start(d[:, :], v_scr[ss, :, :]), LVVs[(t + 3) % 4],
                label=("vv", t + 3))
        if t % 4 == 0 and g + 1 < 4 and t < a2_nopf:
            waitr("sync", SSC, HEADS * 4 * g)
            c0 = (g + 1) * 512 + 128
            nq = min(512, NT_EXT * TOK - c0)
            qsrc = bass.AP(q_scr.tensor, c0,
                           [[NT_EXT * TOK, 128], [128 * NT_EXT * TOK, 8],
                            [1, nq]])
            dma("sync", lambda e, d=q_ring[(g + 1) % 4], ss=qsrc, nn=nq:
                e.dma_start(d.rearrange("p (c w) -> p c w", c=8)[:, :, 0:nn],
                            ss), LQs[(g + 1) % 2], label=("qg", g + 1))
        if t + 1 < min(NT_OWN, a2_nopf):
            waitr("sync", SX1, t)
            dma("sync", lambda e, tt=t + 1: e.dma_start(
                h2_ring[tt % 2][:, :], s_h[tt, :, :]), LHs[(t + 1) % 2],
                label=("h", t + 1))
            waitr("sync", SSXA, t)
            dma("sync", lambda e, tt=t + 1: e.dma_start(
                x2_ring[tt % 2][:, :], x_ext[ts(tt + 1, 128), :]),
                LX2s[(t + 1) % 2], label=("x2", t + 1))

        qg = q_ring[g % 4]
        qcol0 = (t % 4) * 128
        masked = t in (0, NT_OWN - 1)

        def emit_tr_copy(idx):
            tb = 2 if idx % 2 == 0 else 5
            waitr("tensor", SEXP, idx + 1)
            waitr("tensor", SPTE, idx - 1)
            for j in range(3):
                E.op("tensor", lambda e, jj=j, pp=p_sb[idx % 3], b=tb:
                     e.transpose(
                         bb(b)[:, jj * 128:(jj + 1) * 128],
                         pp[:, jj * 128:(jj + 1) * 128], ident_bf[:, :]),
                     inc=SPT.bump() if j == 2 else None)
            E.wait("vector", SPT, SPT.n)
            E.op("vector", lambda e, pt=pT_sb[idx % 3], b=tb: e.tensor_copy(
                pt[:, :], bb(b)[:, 0:WIN]), inc=SPTE.bump())

        def emit_v(idx):
            hh = idx % HEADS
            tt = idx // HEADS
            hf = hh // 8
            so = hh % 8
            waitr("tensor", SPTE, idx + 1)
            if so == 0:
                waitr("tensor", SOEV, 8 * ((tt * 2 + hf) - 1))
            for j in range(3):
                E.op("tensor", lambda e, jj=j, sl=(tt + j) % 4, h2=hh, hf2=hf,
                     pt=pT_sb[idx % 3]: e.matmul(
                         bf(3 + hf2)[:, (h2 % 8) * 64:((h2 % 8) + 1) * 64],
                         pt[:, jj * 128:(jj + 1) * 128],
                         v_ring[sl][:, h2 * 64:(h2 + 1) * 64],
                         start=(jj == 0), stop=(jj == 2)),
                     inc=SVM2.bump() if j == 2 else None)
            if so == 7:
                E.wait("vector", SVM2, SVM2.n)
                E.wait("vector", SRR, SRR.n)
                if tt > 0 and hf == 0:
                    waitr("vector", SAOT, tt)
                    wdma("vector", ("dbgao", tt - 1))
                for h2i in range(hf * 8, hf * 8 + 8):
                    E.op("vector", lambda e, hh2=h2i, hf2=hf: e.tensor_scalar(
                        ao_sb[:, hh2 * 64:(hh2 + 1) * 64],
                        bf(3 + hf2)[:, (hh2 % 8) * 64:((hh2 % 8) + 1) * 64],
                        rrb[:, hh2:hh2 + 1], None, ALU.mult),
                        inc=SOEV.bump())

        for h in range(HEADS):
            idx = t * HEADS + h
            plane = h // 2
            prow = (h % 2) * 64
            scb = idx % 2
            if h == 0:
                for w in range(3):
                    wdma("tensor", ("kv", t + w))
                    wdma("tensor", ("vv", t + w))
                wdma("tensor", ("qg", g))
            waitr("tensor", SEXP, idx - 1)
            for j in range(3):
                E.op("tensor", lambda e, jj=j, sl=(t + j) % 4, b=scb, pl=plane,
                     pr=prow, q0=qcol0, qgb=qg, mk=masked: e.matmul(
                         bf(b)[:, jj * 128:(jj + 1) * 128],
                         qgb[pr:pr + 64, pl * 512 + q0: pl * 512 + q0 + 128],
                         k_ring[sl][pr:pr + 64, pl * 128:(pl + 1) * 128],
                         start=(jj == 0), stop=(jj == 2 and not mk)),
                     inc=SSC.bump() if (j == 2 and not masked) else None)
            if masked:
                E.op("tensor", lambda e, b=scb, tt=t: e.matmul(
                    bf(b)[:, 0:WIN], ones_bf[0:1, 0:128],
                    maskrow_bf[0:1, tt * WIN:(tt + 1) * WIN],
                    start=False, stop=True), inc=SSC.bump())
            E.wait("scalar", SSC, SSC.n)
            E.op("scalar", lambda e, b=scb, hh=h, pp=p_sb[idx % 3]:
                 e.activation(pp[:, :], bf(b)[:, 0:WIN], AF.Exp,
                              accum_out=ssum[:, hh:hh + 1]), inc=SEXP.bump())
            E.wait("vector", SEXP, SEXP.n)
            E.op("vector", lambda e, hh=h: e.reciprocal(rrb[:, hh:hh + 1],
                                                        ssum[:, hh:hh + 1]),
                 inc=SRR.bump())
            if h > 0:
                emit_tr_copy(idx - 1)
            if h > 1:
                emit_v(idx - 2)
        emit_tr_copy(t * HEADS + HEADS - 1)
        emit_v(t * HEADS + HEADS - 2)
        emit_v(t * HEADS + HEADS - 1)
        if "s_dbgao" in taps:
            E.wait("scalar", SOEV, SOEV.n)
            dma("scalar", lambda e, tt=t: e.dma_start(
                s_dbgao[tt, :, :], ao_sb[:, :]), ST0, label=("dbgao", t))
        E.wait("tensor", SOEV, SOEV.n)
        waitr("tensor", SAOE, t)
        for c in range(8):
            E.op("tensor", lambda e, cc=c: e.transpose(
                bb(5)[:, ts(cc, 128)], ao_sb[:, ts(cc, 128)], ident_bf[:, :]),
                inc=SAOT.bump() if c == 7 else None)
        E.wait("vector", SAOT, SAOT.n)
        E.op("vector", lambda e: e.tensor_copy(aoT_sb[:, :], bb(5)[:, 0:D]),
             inc=SAOE.bump())
        E.wait("tensor", SAOE, SAOE.n)
        waitr("tensor", SX1, t)
        E.wait("tensor", WCAST, wcast_proj)
        for n in range(2):
            for k in range(8):
                E.op("tensor", lambda e, kk=k, nn=n: e.matmul(
                    bf(6 + nn)[:, :], aoT_sb[:, ts(kk, 128)],
                    wproj_bf[:, kk * 1024 + nn * 512: kk * 1024 + nn * 512 + 512],
                    start=(kk == 0), stop=False))
            E.op("tensor", lambda e, nn=n: e.matmul(
                bf(6 + nn)[:, :], ones_bf[0:1, 0:128],
                projb_bf[0:1, nn * 512:(nn + 1) * 512], start=False, stop=True),
                inc=SPRJ.bump() if n == 1 else None)
        E.wait("vector", SPRJ, SPRJ.n)
        wdma("vector", ("h", t))
        wdma("vector", ("stx1", t - 2))
        waitr("vector", SLNA2["SB"], t - 1)
        E.op("vector", lambda e, tt=t: e.tensor_add(
            x1_sb[tt % 2][:, 0:512], bf(6)[:, :], h2_ring[tt % 2][:, 0:512]))
        E.op("vector", lambda e, tt=t: e.tensor_add(
            x1_sb[tt % 2][:, 512:1024], bf(7)[:, :],
            h2_ring[tt % 2][:, 512:1024]), inc=SX1.bump())
        E.wait("scalar", SX1, SX1.n)
        dma("scalar", lambda e, tt=t: e.dma_start(
            s_x1[tt, :, :], x1_sb[tt % 2][:, :]), STX1s[t % 2],
            label=("stx1", t))
        wdma("vector", ("x2", t))
        wdma("vector", ("stsxa", t - 2))
        E.wait("vector", SX1, SX1.n)
        E.op("vector", lambda e, tt=t: e.scalar_tensor_tensor(
            sxa_t1[:, :], x1_sb[tt % 2][:, :], 1.0, rs(2), ALU.mult, ALU.mult),
            inc=CH2.bump())
        E.wait("vector", CH2, CH2.n)
        E.op("vector", lambda e, tt=t: e.tensor_add(
            sxa_sb[tt % 2][:, :], sxa_t1[:, :], x2_ring[tt % 2][:, :]),
            inc=SSXA.bump())
        E.wait("scalar", SSXA, SSXA.n)
        dma("scalar", lambda e, tt=t: e.dma_start(
            s_sxa[tt, :, :], sxa_sb[tt % 2][:, :]), STSXAs[t % 2],
            label=("stsxa", t))
        E.wait("scalar", SX1, SX1.n)
        E.wait("vector", SX1, SX1.n)
        E.op("vector", lambda e, tt=t: e.reduce_sum(
            s1A2[:, tt:tt + 1], x1_sb[tt % 2][:, :], axis=AX.X),
            inc=SLNA2["SC"].bump())
        waitr("scalar", SLNA2["SB"], t)
        E.op("scalar", lambda e, tt=t: e.activation(
            sq_tr2[:, :], x1_sb[tt % 2][:, :], AF.Square,
            accum_out=vbA2[:, tt:tt + 1]), inc=SLNA2["SB"].bump())

    _ps = stats_ln2.ap[0][0]
    stats_ev = bass.AP(stats_ln2.tensor, stats_ln2.offset, [[_ps, 128], [2, 16]])
    stats_od = bass.AP(stats_ln2.tensor, stats_ln2.offset + 1,
                       [[_ps, 128], [2, 16]])
    E.wait("vector", SLNA2["SB"], SLNA2["SB"].n)
    E.wait("vector", SLNA2["SC"], SLNA2["SC"].n)
    E.op("vector", lambda e: e.tensor_scalar(muA2[:, :], s1A2[:, :], INV_D,
                                             None, ALU.mult), inc=CH2.bump())
    E.wait("vector", CH2, CH2.n)
    E.op("vector", lambda e: e.tensor_mul(t1A2[:, :], muA2[:, :], muA2[:, :]),
         inc=CH2.bump())
    E.wait("vector", CH2, CH2.n)
    E.op("vector", lambda e: e.scalar_tensor_tensor(
        t1A2[:, :], vbA2[:, :], INV_D, t1A2[:, :], ALU.mult, ALU.subtract),
        inc=CH2.bump())
    E.wait("vector", CH2, CH2.n)
    E.op("vector", lambda e: e.tensor_scalar_add(t1A2[:, :], t1A2[:, :], EPS),
         inc=CH2.bump())
    E.wait("scalar", CH2, CH2.n)
    E.op("scalar", lambda e: e.activation(s1A2[:, :], t1A2[:, :], AF.Ln),
         inc=CH2.bump())
    E.wait("scalar", CH2, CH2.n)
    E.op("scalar", lambda e, ev=stats_ev: e.activation(ev, s1A2[:, :], AF.Exp,
                                                       scale=-0.5),
         inc=CH2.bump())
    E.wait("vector", CH2, CH2.n)
    E.op("vector", lambda e, ev=stats_ev: e.tensor_mul(t1A2[:, :], muA2[:, :],
                                                       ev), inc=CH2.bump())
    E.wait("vector", CH2, CH2.n)
    E.op("vector", lambda e, ov=stats_od: e.tensor_scalar(
        ov, t1A2[:, :], -1.0, None, ALU.mult), inc=CH2.bump())

    for s in STX1s + STSXAs:
        E.wait("scalar", s, s.n)
    barrier()
    if nphases == 3:
        return finalize()

    pool_reset()
    A.off = A.base
    wfc1_bf = A.alloc("wfc1_bf", [128, 8 * DFF], BF16)
    wfc2_bf = A.alloc("wfc2_bf", [128, 32 * D], BF16)
    wstM = [A.alloc(f"wstM{i}", [128, 512], F32) for i in range(2)]
    wstM2 = [A.alloc(f"wstM2{i}", [128, 512], F32) for i in range(2)]
    fc1b_fm = A.alloc("fc1b_fm", [128, 32], F32)
    fc2b_bf = A.alloc("fc2b_bf", [1, D], BF16)
    lnw2_repl = A.alloc("lnw2_repl", [128, D], F32)
    lnb2_repl = A.alloc("lnb2_repl", [128, D], F32)
    x1_ring = [A.alloc(f"x1rM{i}", [128, D], F32) for i in range(3)]
    u2_buf = A.alloc("u2_buf", [128, D], F32)
    zbf_buf = A.alloc("zbf_buf", [128, D], BF16)
    zT_gr = [A.alloc(f"zT_g{i}", [128, 8 * 256], BF16) for i in range(2)]
    tT_r = [A.alloc(f"tTr{i}", [128, 256], BF16) for i in range(2)]
    s2_sb = [A.alloc(f"s2sb{i}", [128, D], F32) for i in range(2)]
    s1M = A.alloc("s1M", [128, NT_OWN], F32)
    vbM = A.alloc("vbM", [128, NT_OWN], F32)
    muM = A.alloc("muM", [128, NT_OWN], F32)
    t1M = A.alloc("t1M", [128, NT_OWN], F32)

    LWMs = [mkdma("LWMa"), mkdma("LWMb")]
    LWM2s = [_alloc_sem("LWM2a"), _alloc_sem("LWM2b")]
    for _s in LWM2s:
        _s.base = 0
    LROWM = mkdma("LROWM")
    LFCB = mkdma("LFCB")
    WCM = mksem("WCM")
    WCM2 = mksem("WCM2")
    WCB = mksem("WCB")
    LX1Ms = [mkdma(f"LX1M{i}") for i in range(3)]
    LXMs = [mkdma("LXMa"), mkdma("LXMb")]
    SUZ = mksem("SUZ")
    SZ = mksem("SZ")
    SZT = mksem("SZT")
    SZTE = mksem("SZTE")
    SF1 = mksem("SF1")
    SGE = mksem("SGE")
    SF2C = mksem("SF2C")
    SF2T = mksem("SF2T")
    SS2 = mksem("SS2")
    STS2s = [mkstore("STS2a"), mkstore("STS2b")]
    SLNM = {k: mksem(f"Mc{k}") for k in ("SB", "SC", "SD", "SE")}
    CHM = mksem("CHM")

    E.op("sync", lambda e: e.dma_start(u2_buf[0:1, :], lnrow[0:1, 2048:3072]),
         inc=LROWM.bump(16))
    E.wait("tensor", LROWM, LROWM.n)
    replicate_row(u2_buf[0:1, :], lnw2_repl, 0, 1)
    E.wait("sync", RMM, RMM.n)
    E.op("sync", lambda e: e.dma_start(u2_buf[0:1, :], lnrow[0:1, 3072:4096]),
         inc=LROWM.bump(16))
    E.wait("tensor", LROWM, LROWM.n)
    replicate_row(u2_buf[0:1, :], lnb2_repl, 2, 3)
    repl_m_done = REV.n

    for t in range(2):
        dma("sync", lambda e, tt=t: e.dma_start(
            x1_ring[tt % 3][:, :], s_x1[tt, :, :]), LX1Ms[t % 3],
            label=("x1m", t))
        dma("sync", lambda e, tt=t: e.dma_start(
            s2_sb[tt % 2][:, :], s_sxa[tt, :, :]), LXMs[t % 2],
            label=("xm", t))

    E.op("sync", lambda e: e.dma_start(fc1b_fm[:, :], fc1b_fm_in[:, :]),
         inc=LFCB.bump(16))
    E.wait("sync", RMM, RMM.n)
    E.op("sync", lambda e: e.dma_start(u2_buf[0:1, :], fc2b_row[0:1, :]),
         inc=LROWM.bump(16))
    for i in range(64):
        k1, q1 = i % 8, i // 8
        k2, q2 = i // 2, i % 2
        if i >= 2:
            waitr("sync", WCM, i - 1)
            waitr("gpsimd", WCM2, i - 1)
        E.op("sync", lambda e, kk=k1, qq=q1, ii=i: e.dma_start(
            wstM[ii % 2][:, :], wfc1T[ts(kk, 128), qq * 512:(qq + 1) * 512]),
            inc=LWMs[i % 2].bump(16))
        E.op("gpsimd", lambda e, kk=k2, qq=q2, ii=i: e.dma_start(
            wstM2[ii % 2][:, :], wfc2T[ts(kk, 128), qq * 512:(qq + 1) * 512]),
            inc=LWM2s[i % 2].bump(16))
    E.wait("scalar", LFCB, LFCB.n)
    lwm_n = {0: LWMs[0].base, 1: LWMs[1].base}
    lwm2_n = {0: LWM2s[0].base, 1: LWM2s[1].base}

    def emit_casts(i0, i1):
        for i in range(i0, i1):
            k1, q1 = i % 8, i // 8
            k2, q2 = i // 2, i % 2
            lwm_n[i % 2] += 16
            E.wait("vector", LWMs[i % 2], lwm_n[i % 2])
            E.op("vector", lambda e, kk=k1, qq=q1, ii=i: e.tensor_copy(
                wfc1_bf[:, kk * DFF + qq * 512: kk * DFF + (qq + 1) * 512],
                wstM[ii % 2][:, :]), inc=WCM.bump())
            lwm2_n[i % 2] += 16
            E.wait("vector", LWM2s[i % 2], lwm2_n[i % 2])
            E.op("vector", lambda e, kk=k2, qq=q2, ii=i: e.tensor_copy(
                wfc2_bf[:, kk * 1024 + qq * 512: kk * 1024 + (qq + 1) * 512],
                wstM2[ii % 2][:, :]), inc=WCM2.bump())

    emit_casts(0, 2)
    E.wait("vector", LROWM, LROWM.n)
    E.op("vector", lambda e: e.tensor_copy(fc2b_bf[0:1, :], u2_buf[0:1, :]),
         inc=WCB.bump())

    obA = (4, 5)
    obB = (6, 7)

    def prep_tile(t, slot, lt):
        if t + 2 < NT_OWN:
            waitr("sync", SUZ, t)
            dma("sync", lambda e, tt=t + 2: e.dma_start(
                x1_ring[tt % 3][:, :], s_x1[tt, :, :]), LX1Ms[(t + 2) % 3],
                label=("x1m", t + 2))
        wdma("scalar", ("x1m", t))
        if t == 0:
            E.wait("scalar", REV, repl_m_done)
            waitr("scalar", WCB, 1)
        waitr("scalar", SZ, t)
        waitr("scalar", SS2, 2 * (t // 2) - 2)
        E.op("scalar", lambda e, tt=t: e.activation(
            u2_buf[:, :], x1_ring[tt % 3][:, :], AF.Identity,
            scale=stats_ln2[:, 2 * tt:2 * tt + 1],
            bias=stats_ln2[:, 2 * tt + 1:2 * tt + 2]), inc=SUZ.bump())
        E.wait("vector", SUZ, SUZ.n)
        if t == 0:
            E.wait("vector", REV, repl_m_done)
        waitr("vector", SZT, t)
        E.op("vector", lambda e: e.scalar_tensor_tensor(
            u2_buf[:, :], u2_buf[:, :], 1.0, lnw2_repl, ALU.mult, ALU.mult),
            inc=CHM.bump())
        E.wait("vector", CHM, CHM.n)
        E.op("vector", lambda e: e.scalar_tensor_tensor(
            zbf_buf[:, :], u2_buf[:, :], 1.0, lnb2_repl, ALU.mult, ALU.add),
            inc=SZ.bump())
        E.wait("tensor", SZ, SZ.n)
        waitr("tensor", SZTE, t - 1)
        for c in range(8):
            E.op("tensor", lambda e, cc=c, bk=t % 2: e.transpose(
                bb(bk)[:, ts(cc, 128)], zbf_buf[:, ts(cc, 128)],
                ident_bf[:, :]), inc=SZT.bump() if c == 7 else None)
        E.wait("vector", SZT, SZT.n)
        waitr("vector", SF1, 32 * (t // 2 - 1))
        dstz = zT_gr[slot].rearrange("p (c w) -> p c w", c=8)[
            :, :, lt * 128:(lt + 1) * 128]
        E.op("vector", lambda e, d=dstz, bk=t % 2: e.tensor_copy(
            d, bb(bk).rearrange("p (c w) -> p c w", c=8)), inc=SZTE.bump())

    def emit_fc2(g, m):
        mu = g * 32 + m
        if g == 0:
            waitr("tensor", WCM2, 2 * (m + 1))
        waitr("tensor", SGE, mu + 1)
        for lt in range(2):
            ob = obA if lt == 0 else obB
            for n in range(2):
                E.op("tensor", lambda e, nn=n, l=lt, mm=m, b=ob[n],
                     st=tT_r[mu % 2]: e.matmul(
                         bf(b)[:, :], st[:, l * 128:(l + 1) * 128],
                         wfc2_bf[:, mm * 1024 + nn * 512: mm * 1024 + nn * 512 + 512],
                         start=(mm == 0), stop=False),
                     inc=SF2C.bump() if (lt == 1 and n == 1) else None)

    prep_tile(0, 0, 0)
    prep_tile(1, 0, 1)
    emit_casts(2, 64)
    for g in range(8):
        tiles = (2 * g, 2 * g + 1)
        E.wait("tensor", SZTE, SZTE.n)
        waitr("tensor", SS2, 2 * g)
        for m in range(32):
            mu = g * 32 + m
            fb = 2 + (mu % 2)
            if g == 0:
                waitr("tensor", WCM, 8 * (m // 4) + 8)
            waitr("tensor", SGE, mu - 1)
            for k in range(8):
                E.op("tensor", lambda e, kk=k, mm=m, b=fb, zt=zT_gr[g % 2]:
                     e.matmul(
                         bf(b)[:, 0:256],
                         wfc1_bf[:, kk * DFF + mm * 128: kk * DFF + mm * 128 + 128],
                         zt[:, kk * 256:(kk + 1) * 256],
                         start=(kk == 0), stop=(kk == 7)),
                     inc=SF1.bump() if k == 7 else None)
            E.wait("scalar", SF1, SF1.n)
            waitr("scalar", SF2C, mu - 1)
            E.op("scalar", lambda e, mm=m, b=fb, st=tT_r[mu % 2]: e.activation(
                st[:, :], bf(b)[:, 0:256], AF.Gelu, bias=fc1b_fm[:, mm:mm + 1]),
                inc=SGE.bump())
            if m >= 1:
                emit_fc2(g, m - 1)
        emit_fc2(g, 31)
        for lt, t in enumerate(tiles):
            ob = obA if lt == 0 else obB
            if g == 0 and lt == 0:
                waitr("tensor", WCB, 1)
            for n in range(2):
                E.op("tensor", lambda e, nn=n, b=ob[n]: e.matmul(
                    bf(b)[:, :], ones_bf[0:1, 0:128],
                    fc2b_bf[0:1, nn * 512:(nn + 1) * 512],
                    start=False, stop=True),
                    inc=SF2T.bump() if n == 1 else None)
        if g + 1 < 8:
            prep_tile(2 * g + 2, (g + 1) % 2, 0)
            prep_tile(2 * g + 3, (g + 1) % 2, 1)
        for lt, t in enumerate(tiles):
            ob = obA if lt == 0 else obB
            E.wait("vector", SF2T, SF2T.n)
            wdma("vector", ("xm", t))
            waitr("vector", SLNM["SB"], t)
            E.op("vector", lambda e, b=ob[0]: e.scalar_tensor_tensor(
                u2_buf[:, 0:512], bf(b)[:, :], 1.0, rs(2)[:, 0:512],
                ALU.mult, ALU.mult))
            E.op("vector", lambda e, b=ob[1]: e.scalar_tensor_tensor(
                u2_buf[:, 512:1024], bf(b)[:, :], 1.0, rs(2)[:, 512:1024],
                ALU.mult, ALU.mult), inc=CHM.bump())
            E.wait("vector", CHM, CHM.n)
            E.op("vector", lambda e, tt=t: e.tensor_add(
                s2_sb[tt % 2][:, :], u2_buf[:, :], s2_sb[tt % 2][:, :]),
                inc=SS2.bump())
            E.wait("scalar", SS2, SS2.n)
            dma("scalar", lambda e, tt=t: e.dma_start(
                s_s2[tt, :, :], s2_sb[tt % 2][:, :]), STS2s[t % 2],
                label=("sts2", t))
            E.wait("scalar", SS2, SS2.n)
            E.wait("vector", SS2, SS2.n)
            E.op("vector", lambda e, tt=t: e.reduce_sum(
                s1M[:, tt:tt + 1], s2_sb[tt % 2][:, :], axis=AX.X),
                inc=SLNM["SC"].bump())
            waitr("scalar", SLNM["SB"], t)
            E.op("scalar", lambda e, tt=t: e.activation(
                u2_buf[:, :], s2_sb[tt % 2][:, :], AF.Square,
                accum_out=vbM[:, tt:tt + 1]), inc=SLNM["SB"].bump())
            if t + 2 < NT_OWN:
                wdma("sync", ("sts2", t))
                waitr("sync", SLNM["SB"], t + 1)
                waitr("sync", SLNM["SC"], t + 1)
                dma("sync", lambda e, tt=t + 2: e.dma_start(
                    s2_sb[tt % 2][:, :], s_sxa[tt, :, :]), LXMs[t % 2],
                    label=("xm", t + 2))

    _ps2 = stats_s2.ap[0][0]
    sts2_ev = bass.AP(stats_s2.tensor, stats_s2.offset, [[_ps2, 128], [2, 16]])
    sts2_od = bass.AP(stats_s2.tensor, stats_s2.offset + 1,
                      [[_ps2, 128], [2, 16]])
    E.wait("vector", SLNM["SB"], SLNM["SB"].n)
    E.wait("vector", SLNM["SC"], SLNM["SC"].n)
    E.op("vector", lambda e: e.tensor_scalar(muM[:, :], s1M[:, :], INV_D,
                                             None, ALU.mult), inc=CHM.bump())
    E.wait("vector", CHM, CHM.n)
    E.op("vector", lambda e: e.tensor_mul(t1M[:, :], muM[:, :], muM[:, :]),
         inc=CHM.bump())
    E.wait("vector", CHM, CHM.n)
    E.op("vector", lambda e: e.scalar_tensor_tensor(
        t1M[:, :], vbM[:, :], INV_D, t1M[:, :], ALU.mult, ALU.subtract),
        inc=CHM.bump())
    E.wait("vector", CHM, CHM.n)
    E.op("vector", lambda e: e.tensor_scalar_add(t1M[:, :], t1M[:, :], EPS),
         inc=CHM.bump())
    E.wait("scalar", CHM, CHM.n)
    E.op("scalar", lambda e: e.activation(s1M[:, :], t1M[:, :], AF.Ln),
         inc=CHM.bump())
    E.wait("scalar", CHM, CHM.n)
    E.op("scalar", lambda e, ev=sts2_ev: e.activation(ev, s1M[:, :], AF.Exp,
                                                      scale=-0.5),
         inc=CHM.bump())
    E.wait("vector", CHM, CHM.n)
    E.op("vector", lambda e, ev=sts2_ev: e.tensor_mul(t1M[:, :], muM[:, :],
                                                      ev), inc=CHM.bump())
    E.wait("vector", CHM, CHM.n)
    E.op("vector", lambda e, ov=sts2_od: e.tensor_scalar(
        ov, t1M[:, :], -1.0, None, ALU.mult), inc=CHM.bump())

    for s in STS2s:
        E.wait("scalar", s, s.n)
    barrier()
    if nphases == 4:
        return finalize()

    pool_reset()
    A.off = A.base
    gates = A.alloc("gates", [128, NEXP * NT_OWN], F32)
    gatesT_bf = A.alloc("gatesT_bf", [8, NT_OWN * TOK], BF16)
    r2_all = A.alloc("r2_all", [128, NT_OWN], F32)
    ck5 = A.off
    s2E_ring = [A.alloc(f"s2E{i}", [128, D], F32) for i in range(2)]
    u3_buf = A.alloc("u3_buf", [128, D], F32)
    h1E_buf = A.alloc("h1E_buf", [128, D], F32)
    h2bf_buf = A.alloc("h2bf_buf", [128, D], BF16)
    h2st = [A.alloc(f"h2st{i}", [128, D], BF16) for i in range(2)]
    rowstE = A.alloc("rowstE", [1, D], F32)
    cos_all = A.alloc("cos_all", [128, NEXP * NT_OWN], F32)
    route_all = A.alloc("route_all", [128, NEXP * NT_OWN], F32)
    u_all = A.alloc("u_all", [128, NEXP * NT_OWN], F32)
    gtub_all = A.alloc("gtub_all", [128, NEXP * NT_OWN], F32)
    gmask_all = A.alloc("gmask_all", [128, NEXP * NT_OWN], F32)
    gE_all = A.alloc("gE_all", [128, NEXP * NT_OWN], F32)
    g16 = {k: A.alloc(f"g16_{k}", [128, NT_OWN], F32) for k in
           ("m1", "m2", "rsum", "rinv", "r2s")}

    def bc16(a):
        return bass.AP(a.tensor, a.offset, [[a.ap[0][0], 128], [1, 16], [0, 8]])

    def c16(a):
        return bass.AP(a.tensor, a.offset, [[a.ap[0][0], 128], [1, 16], [1, 1]])

    def g3(a):
        return a.rearrange("p (t e) -> p t e", t=16)
    assert A.off <= ck5 + 33 * 1024, f"5a scratch too big: {A.off - ck5}"
    A.off = ck5 + 33 * 1024
    w1_bf = A.alloc("w1_bf", [128, NEXP * 8 * DEXP], BF16)
    w2_bf = A.alloc("w2_bf", [128, NEXP * 4 * D], BF16)
    wstE = [A.alloc(f"wstE{i}", [128, 512], F32) for i in range(2)]
    wstE2 = [A.alloc(f"wstE2{i}", [128, 512], F32) for i in range(2)]
    b1_bf = A.alloc("b1_bf", [1, NEXP * DEXP], BF16)

    LWEs = [mkdma("LWEa"), mkdma("LWEb")]
    LWE2s = [_alloc_sem("LWE2a"), _alloc_sem("LWE2b")]
    for _s in LWE2s:
        _s.base = 0
    LROWE = mkdma("LROWE")
    WCE = mksem("WCE")
    WCE2 = mksem("WCE2")
    LS2s = [mkdma("LS2a"), mkdma("LS2b")]
    SU3 = mksem("SU3")
    SH2 = mksem("SH2")
    SH2T = mksem("SH2T")
    SH2E = mksem("SH2E")
    STH2s = [mkstore("STH2a"), mkstore("STH2b")]
    SLG = mksem("SLG")
    SLGE = mksem("SLGE")
    SGA = mksem("SGA")
    SGDONE = mksem("SGDONE")
    SGT = mksem("SGT")
    SR2 = mksem("SR2")
    CHE = mksem("CHE")

    E.op("sync", lambda e: e.dma_start(rowstE[0:1, :], s_modrow[0:1, 3072:4096]),
         inc=LROWE.bump(16))
    E.wait("tensor", LROWE, LROWE.n)
    replicate_row(rowstE[0:1, :], rs(0), 0, 1)
    E.wait("sync", RMM, RMM.n)
    E.op("sync", lambda e: e.dma_start(rowstE[0:1, :], s_modrow[0:1, 4096:5120]),
         inc=LROWE.bump(16))
    E.wait("tensor", LROWE, LROWE.n)
    replicate_row(rowstE[0:1, :], rs(1), 2, 3)
    E.wait("sync", RMM, RMM.n)
    E.op("sync", lambda e: e.dma_start(rowstE[0:1, :], s_modrow[0:1, 5120:6144]),
         inc=LROWE.bump(16))
    E.wait("tensor", LROWE, LROWE.n)
    replicate_row(rowstE[0:1, :], rs(2), 4, 5)
    repl_e_done = REV.n

    lw1n = {0: LWEs[0].base, 1: LWEs[1].base}
    lw2n = {0: 0, 1: 0}

    def emit_w1_loads(i0, i1):
        for i in range(i0, i1):
            e_, k = divmod(i, 8)
            w1e = w1_in[e_].rearrange("(kc p) n -> kc p n", p=128)
            if i >= 2:
                waitr("sync", WCE, i - 1)
            E.op("sync", lambda e, s=w1e, kk=k, ii=i: e.dma_start(
                wstE[ii % 2][:, :], s[kk]), inc=LWEs[i % 2].bump(16))

    def emit_w1_casts(i0, i1):
        for i in range(i0, i1):
            e_, k = divmod(i, 8)
            lw1n[i % 2] += 16
            E.wait("vector", LWEs[i % 2], lw1n[i % 2])
            col = (e_ * 8 + k) * 512
            E.op("vector", lambda e, cc=col, ii=i: e.tensor_copy(
                w1_bf[:, cc:cc + 512], wstE[ii % 2][:, :]), inc=WCE.bump())

    def emit_w2_loads(i0, i1):
        for i in range(i0, i1):
            e_, k = divmod(i, 8)
            w2e = w2_in[e_].rearrange("(kc p) n -> kc p n", p=128)
            src = w2e[k // 2][:, (k % 2) * 512:(k % 2) * 512 + 512]
            if i >= 2:
                waitr("gpsimd", WCE2, i - 1)
            E.op("gpsimd", lambda e, s=src, ii=i: e.dma_start(
                wstE2[ii % 2][:, :], s), inc=LWE2s[i % 2].bump(16))

    def emit_w2_casts(i0, i1):
        for i in range(i0, i1):
            e_, k = divmod(i, 8)
            lw2n[i % 2] += 16
            E.wait("scalar", LWE2s[i % 2], lw2n[i % 2])
            col = e_ * 4096 + (k // 2) * 1024 + (k % 2) * 512
            E.op("scalar", lambda e, cc=col, ii=i: e.activation(
                w2_bf[:, cc:cc + 512], wstE2[ii % 2][:, :], AF.Copy),
                inc=WCE2.bump())

    emit_w2_loads(0, 64)

    for t in range(2):
        dma("sync", lambda e, tt=t: e.dma_start(
            s2E_ring[tt % 2][:, :], s_s2[tt, :, :]), LS2s[t % 2],
            label=("s2e", t))
    emit_w1_loads(0, 2)
    emit_w1_casts(0, 2)
    emit_w2_casts(0, 2)

    for t in range(NT_OWN):
        if t + 2 < NT_OWN:
            waitr("sync", SU3, t + 1)
            dma("sync", lambda e, tt=t + 2: e.dma_start(
                s2E_ring[tt % 2][:, :], s_s2[tt, :, :]), LS2s[t % 2],
                label=("s2e", t + 2))
        emit_w1_loads(2 + 4 * t, min(2 + 4 * (t + 1), 64))
        wdma("scalar", ("s2e", t))
        waitr("scalar", SR2, t)
        E.op("scalar", lambda e, tt=t: e.activation(
            u3_buf[:, :], s2E_ring[tt % 2][:, :], AF.Identity,
            scale=stats_s2[:, 2 * tt:2 * tt + 1],
            bias=stats_s2[:, 2 * tt + 1:2 * tt + 2]), inc=SU3.bump())
        E.wait("vector", SU3, SU3.n)
        if t == 0:
            E.wait("vector", REV, repl_e_done)
        waitr("vector", SR2, t)
        waitr("vector", SH2T, t)
        E.op("vector", lambda e: e.scalar_tensor_tensor(
            h1E_buf[:, :], u3_buf[:, :], 1.0, rs(1), ALU.mult, ALU.mult),
            inc=CHE.bump())
        E.wait("vector", CHE, CHE.n)
        E.op("vector", lambda e: e.scalar_tensor_tensor(
            h2bf_buf[:, :], h1E_buf[:, :], 1.0, rs(0), ALU.mult, ALU.add),
            inc=SH2.bump())
        E.wait("scalar", SH2, SH2.n)
        E.op("scalar", lambda e, tt=t: e.activation(
            u3_buf[:, :], h2bf_buf[:, :], AF.Square,
            accum_out=r2_all[:, tt:tt + 1]), inc=SR2.bump())
        E.wait("tensor", SH2, SH2.n)
        waitr("tensor", SH2E, t - 1)
        for c in range(8):
            E.op("tensor", lambda e, cc=c, bk=t % 2: e.transpose(
                bb(bk)[:, ts(cc, 128)], h2bf_buf[:, ts(cc, 128)],
                ident_bf[:, :]), inc=SH2T.bump() if c == 7 else None)
        E.wait("vector", SH2T, SH2T.n)
        wdma("vector", ("sth2", t - 2))
        waitr("vector", SLG, t - 1)
        E.op("vector", lambda e, bk=t % 2, st=h2st[t % 2]: e.tensor_copy(
            st[:, :], bb(bk)[:, 0:D]), inc=SH2E.bump())
        E.wait("scalar", SH2E, SH2E.n)
        dma("scalar", lambda e, tt=t, st=h2st[t % 2]: e.dma_start(
            s_h2T[tt, :, :], st[:, :]), STH2s[t % 2], label=("sth2", t))
        E.wait("tensor", SH2E, SH2E.n)
        waitr("tensor", SLGE, t)
        for k in range(8):
            E.op("tensor", lambda e, kk=k, st=h2st[t % 2]: e.matmul(
                bf(2)[:, 0:8], st[:, ts(kk, 128)], cnT_bf[:, kk * 8:(kk + 1) * 8],
                start=(kk == 0), stop=(kk == 7)))
        for k in range(8):
            E.op("tensor", lambda e, kk=k, st=h2st[t % 2]: e.matmul(
                bf(2)[:, 256:264], st[:, ts(kk, 128)],
                wrT_bf[:, kk * 8:(kk + 1) * 8], start=(kk == 0), stop=(kk == 7)),
                inc=SLG.bump() if k == 7 else None)
        E.wait("vector", SLG, SLG.n)
        E.op("vector", lambda e, tt=t: e.tensor_copy(
            cos_all[:, tt * 8:(tt + 1) * 8], bf(2)[:, 0:8]))
        E.op("vector", lambda e, tt=t: e.tensor_copy(
            route_all[:, tt * 8:(tt + 1) * 8], bf(2)[:, 256:264]),
            inc=SLGE.bump())
        emit_w1_casts(2 + 4 * t, min(2 + 4 * (t + 1), 64))
        emit_w2_casts(2 + 4 * t, min(2 + 4 * (t + 1), 64))

    E.wait("sync", RMM, RMM.n)
    for q in range(4):
        if q > 0:
            E.wait("sync", WCE, WCE.n)
        E.op("sync", lambda e, qq=q: e.dma_start(
            rowstE[0:1, :], b1_row[0:1, qq * 1024:(qq + 1) * 1024]),
            inc=LROWE.bump(16))
        E.wait("vector", LROWE, LROWE.n)
        E.op("vector", lambda e, qq=q: e.tensor_copy(
            b1_bf[0:1, qq * 1024:(qq + 1) * 1024], rowstE[0:1, :]),
            inc=WCE.bump())
    wce_total = WCE.n
    wce2_total = WCE2.n

    def rtg(eng="vector"):
        E.wait(eng, SGA, SGA.n)

    E.wait("scalar", SR2, SR2.n)
    E.op("scalar", lambda e: e.activation(g16["m1"], r2_all[:, 0:NT_OWN],
                                          AF.Ln), inc=SGA.bump())
    rtg("scalar")
    E.op("scalar", lambda e: e.activation(g16["r2s"], g16["m1"], AF.Exp,
                                          scale=0.5), inc=SGA.bump())
    rtg()
    E.op("vector", lambda e: e.tensor_scalar_max(g16["r2s"], g16["r2s"],
                                                 1e-12), inc=SGA.bump())
    rtg()
    E.op("vector", lambda e: e.reciprocal(g16["rinv"], g16["r2s"]),
         inc=SGA.bump())
    rtg()
    E.wait("vector", SLGE, SLGE.n)
    E.op("vector", lambda e: e.tensor_tensor(
        g3(cos_all), g3(cos_all), bc16(g16["rinv"]), ALU.mult),
        inc=SGA.bump())
    rtg()
    E.op("vector", lambda e: e.tensor_add(cos_all[:, :], cos_all[:, :],
                                          route_all[:, :]), inc=SGA.bump())
    rtg()
    E.op("vector", lambda e: e.reduce_max(c16(g16["m1"]), g3(cos_all),
                                          axis=AX.X), inc=SGA.bump())
    rtg()
    E.op("vector", lambda e: e.tensor_tensor(
        g3(u_all), g3(cos_all), bc16(g16["m1"]), ALU.subtract),
        inc=SGA.bump())
    E.op("vector", lambda e: e.tensor_tensor(
        g3(gmask_all), g3(cos_all), bc16(g16["m1"]), ALU.is_ge),
        inc=SGA.bump())
    rtg("scalar")
    E.op("scalar", lambda e: e.activation(gE_all[:, :], u_all[:, :], AF.Exp),
         inc=SGA.bump())
    rtg()
    E.op("vector", lambda e: e.scalar_tensor_tensor(
        gtub_all[:, :], gmask_all[:, :], -1e30, cos_all[:, :],
        ALU.mult, ALU.add), inc=SGA.bump())
    rtg()
    E.op("vector", lambda e: e.reduce_max(c16(g16["m2"]), g3(gtub_all),
                                          axis=AX.X), inc=SGA.bump())
    rtg()
    E.op("vector", lambda e: e.tensor_tensor(
        g3(gmask_all), g3(cos_all), bc16(g16["m2"]), ALU.is_ge),
        inc=SGA.bump())
    rtg()
    E.op("vector", lambda e: e.tensor_mul(gE_all[:, :], gE_all[:, :],
                                          gmask_all[:, :]), inc=SGA.bump())
    rtg()
    E.op("vector", lambda e: e.reduce_sum(c16(g16["rsum"]), g3(gE_all),
                                          axis=AX.X), inc=SGA.bump())
    rtg()
    E.op("vector", lambda e: e.reciprocal(g16["rinv"], g16["rsum"]),
         inc=SGA.bump())
    rtg()
    E.op("vector", lambda e: e.tensor_tensor(
        g3(gates[:, 0:NEXP * NT_OWN]), g3(gE_all), bc16(g16["rinv"]),
        ALU.mult), inc=SGDONE.bump())
    E.wait("tensor", SGDONE, SGDONE.n)
    for b in range(4):
        waitr("tensor", SGT, 2 * b)
        for ti in range(4):
            E.op("tensor", lambda e, tt=b * 4 + ti, cc=ti: e.transpose(
                bf(3)[0:8, cc * 128:(cc + 1) * 128],
                gates[:, tt * 8:(tt + 1) * 8], ident32[:, :]),
                inc=SGT.bump() if ti == 3 else None)
        E.wait("scalar", SGT, SGT.n)
        E.op("scalar", lambda e, bb_=b: e.activation(
            gatesT_bf[0:8, bb_ * 512:(bb_ + 1) * 512], bf(3)[0:8, 0:512],
            AF.Copy), inc=SGT.bump())

    for s in STH2s:
        E.wait("scalar", s, s.n)
    barrier(clear=False)
    if nphases == 5:
        return finalize()

    A.off = ck5
    h2T_ring = [A.alloc(f"h2Tr{i}", [128, D], BF16) for i in range(2)]
    s2b_ring = [A.alloc(f"s2b{i}", [128, D], F32) for i in range(2)]
    tE_f = A.alloc("tE_f", [128, DEXP], F32)
    tEg_bf = [A.alloc(f"tEg{i}", [128, DEXP], BF16) for i in range(2)]
    tT_bf = [A.alloc(f"tTbf{i}", [128, DEXP], BF16) for i in range(3)]
    yo_t1 = A.alloc("yo_t1", [128, D], F32)
    yo_sb = [A.alloc(f"yo{i}", [128, D], F32) for i in range(2)]
    assert A.off <= ck5 + 33 * 1024, f"5b scratch too big: {A.off - ck5}"

    LH2Bs = [mkdma("LH2Ba"), mkdma("LH2Bb")]
    LS2Bs = [mkdma("LS2Ba"), mkdma("LS2Bb")]
    STE = mksem("STE")
    SGEL = mksem("SGEL")
    SSCL = mksem("SSCL")
    STT = mksem("STT")
    STTE = mksem("STTE")
    SOUT = mksem("SOUT")
    SYEV = mksem("SYEV")
    STYs = [mkstore("STYa"), mkstore("STYb")]

    for t in range(2):
        dma("sync", lambda e, tt=t: e.dma_start(
            h2T_ring[tt % 2][:, :], s_h2T[tt, :, :]), LH2Bs[t % 2],
            label=("h2b", t))
        dma("sync", lambda e, tt=t: e.dma_start(
            s2b_ring[tt % 2][:, :], s_s2[tt, :, :]), LS2Bs[t % 2],
            label=("s2b", t))

    E.wait("tensor", WCE, wce_total)
    E.wait("tensor", WCE2, wce2_total)
    for t in range(NT_OWN):
        if t + 2 < NT_OWN:
            waitr("sync", STE, NEXP * (t + 1))
            waitr("sync", SYEV, t + 1)
            dma("sync", lambda e, tt=t + 2: e.dma_start(
                h2T_ring[tt % 2][:, :], s_h2T[tt, :, :]), LH2Bs[t % 2],
                label=("h2b", t + 2))
            dma("sync", lambda e, tt=t + 2: e.dma_start(
                s2b_ring[tt % 2][:, :], s_s2[tt, :, :]), LS2Bs[t % 2],
                label=("s2b", t + 2))
        obks = (4, 5) if (t % 2 == 0) else (6, 7)

        def tail_trc(idx):
            waitr("tensor", SSCL, idx + 1)
            waitr("tensor", STTE, idx - 1)
            for c in range(4):
                E.op("tensor", lambda e, cc=c, ii=idx: e.transpose(
                    bb(2 + ii % 2)[:, ts(cc, 128)],
                    tEg_bf[ii % 2][:, ts(cc, 128)], ident_bf[:, :]),
                    inc=STT.bump() if c == 3 else None)
            E.wait("scalar", STT, STT.n)
            E.op("scalar", lambda e, ii=idx: e.activation(
                tT_bf[ii % 3][:, :], bb(2 + ii % 2)[:, 0:512], AF.Copy),
                inc=STTE.bump())

        def tail_fc2(idx):
            ee = idx % NEXP
            waitr("tensor", STTE, idx + 1)
            if ee == 0:
                waitr("tensor", SYEV, t - 1)
            for n in range(2):
                for k in range(4):
                    E.op("tensor", lambda e, kk=k, nn=n, ee2=ee, ii=idx,
                         b=obks[n]: e.matmul(
                             bf(b)[:, :], tT_bf[ii % 3][:, ts(kk, 128)],
                             w2_bf[:, (ee2 * 4 + kk) * 1024 + nn * 512:
                                   (ee2 * 4 + kk) * 1024 + nn * 512 + 512],
                             start=(ee2 == 0 and kk == 0), stop=False))

        for e_ in range(NEXP):
            idx = t * NEXP + e_
            bk = idx % 2
            if e_ == 0:
                wdma("tensor", ("h2b", t))
            waitr("tensor", SGEL, idx - 1)
            for k in range(8):
                E.op("tensor", lambda e, kk=k, ee=e_, b=bk, ht=h2T_ring[t % 2]:
                     e.matmul(bf(b)[:, :], ht[:, ts(kk, 128)],
                              w1_bf[:, (ee * 8 + kk) * 512:(ee * 8 + kk) * 512 + 512],
                              start=(kk == 0), stop=False))
            E.op("tensor", lambda e, ee=e_, b=bk: e.matmul(
                bf(b)[:, :], ones_bf[0:1, 0:128],
                b1_bf[0:1, ee * 512:(ee + 1) * 512], start=False, stop=True),
                inc=STE.bump())
            E.wait("scalar", STE, STE.n)
            waitr("scalar", SSCL, idx)
            E.op("scalar", lambda e, b=bk: e.activation(
                tE_f[:, :], bf(b)[:, :], AF.Gelu), inc=SGEL.bump())
            E.wait("vector", SGEL, SGEL.n)
            waitr("vector", STT, idx - 1)
            E.op("vector", lambda e, tt=t, ee=e_, ii=idx: e.tensor_scalar(
                tEg_bf[ii % 2][:, :], tE_f[:, :],
                gates[:, tt * 8 + ee: tt * 8 + ee + 1], None, ALU.mult),
                inc=SSCL.bump())
            if e_ > 0:
                tail_trc(idx - 1)
            if e_ > 1:
                tail_fc2(idx - 2)
        tail_trc(t * NEXP + NEXP - 1)
        tail_fc2(t * NEXP + NEXP - 2)
        tail_fc2(t * NEXP + NEXP - 1)
        for n in range(2):
            E.op("tensor", lambda e, nn=n, tt=t, b=obks[n]: e.matmul(
                bf(b)[:, :], gatesT_bf[0:8, tt * 128:(tt + 1) * 128],
                b2_bf[0:8, nn * 512:(nn + 1) * 512], start=False, stop=True),
                inc=SOUT.bump() if n == 1 else None)
        E.wait("vector", SOUT, SOUT.n)
        wdma("vector", ("s2b", t))
        wdma("vector", ("sty", t - 2))
        E.op("vector", lambda e, b=obks[0]: e.scalar_tensor_tensor(
            yo_t1[:, 0:512], bf(b)[:, :], 1.0, rs(2)[:, 0:512],
            ALU.mult, ALU.mult), inc=CHE.bump())
        E.op("vector", lambda e, b=obks[1]: e.scalar_tensor_tensor(
            yo_t1[:, 512:1024], bf(b)[:, :], 1.0, rs(2)[:, 512:1024],
            ALU.mult, ALU.mult), inc=CHE.bump())
        E.wait("vector", CHE, CHE.n)
        E.op("vector", lambda e, tt=t: e.tensor_add(
            yo_sb[tt % 2][:, :], yo_t1[:, :], s2b_ring[tt % 2][:, :]),
            inc=SYEV.bump())
        E.wait("scalar", SYEV, SYEV.n)
        dma("scalar", lambda e, tt=t: e.dma_start(
            y_out[tt, :, :], yo_sb[tt % 2][:, :]), STYs[t % 2],
            label=("sty", t))

    for s in STYs:
        E.wait("scalar", s, s.n)
    return finalize()


def _stage_inputs(inputs):
    f32 = np.float32
    x = np.asarray(inputs["L0_attn_x"], f32)
    c = np.asarray(inputs["L0_c"], f32)

    shared = {
        "wqkvT": np.ascontiguousarray(np.asarray(inputs["L0_attn_qkv_w"], f32).T),
        "wprojT": np.ascontiguousarray(np.asarray(inputs["L0_attn_proj_w"], f32).T),
        "wfc1T": np.ascontiguousarray(np.asarray(inputs["L0_attn_fc1_w"], f32).T),
        "wfc2T": np.ascontiguousarray(np.asarray(inputs["L0_attn_fc2_w"], f32).T),
        "w1_in": np.ascontiguousarray(np.asarray(inputs["L0_ffn_w1"], f32)),
        "w2_in": np.ascontiguousarray(np.asarray(inputs["L0_ffn_w2"], f32)),
        "wmod": np.ascontiguousarray(np.asarray(inputs["L0_wmod"], f32)),
        "wrouteT": np.ascontiguousarray(
            np.asarray(inputs["L0_ffn_w_route"], f32).T),
        "cent_in": np.ascontiguousarray(
            np.asarray(inputs["L0_ffn_centroids"], f32)),
        "b2_in": np.ascontiguousarray(np.asarray(inputs["L0_ffn_b2"], f32)),
        "qkvb_row": np.asarray(inputs["L0_attn_qkv_b"], f32)[None, :].copy(),
        "projb_row": np.asarray(inputs["L0_attn_proj_b"], f32)[None, :].copy(),
        "fc1b_fm_in": np.ascontiguousarray(
            np.asarray(inputs["L0_attn_fc1_b"], f32).reshape(32, 128).T),
        "fc2b_row": np.asarray(inputs["L0_attn_fc2_b"], f32)[None, :].copy(),
        "b1_row": np.asarray(inputs["L0_ffn_b1"], f32).reshape(1, -1).copy(),
        "lnrow": np.concatenate([
            np.asarray(inputs["L0_attn_ln1_w"], f32),
            np.asarray(inputs["L0_attn_ln1_b"], f32),
            np.asarray(inputs["L0_attn_ln2_w"], f32),
            np.asarray(inputs["L0_attn_ln2_b"], f32)])[None, :].copy(),
        "ident_bf_in": np.eye(128, dtype=ml_dtypes.bfloat16),
        "ident32_in": np.eye(128, dtype=f32),
    }

    in_maps = []
    for core in range(8):
        b = core // 2
        half = core % 2
        s0 = half * 2048
        ext = np.zeros((NT_EXT * TOK, D), f32)
        ext[128:128 + 2048] = x[b, s0:s0 + 2048]
        if half == 0:
            ext[128 + 2048:] = x[b, 2048:2048 + 128]
        else:
            ext[0:128] = x[b, 1920:2048]
        mask = np.zeros((1, NT_OWN * WIN), f32)
        if half == 0:
            mask[0, 0:128] = -1e30
        else:
            mask[0, 15 * WIN + 256:16 * WIN] = -1e30
        m = dict(shared)
        m["x_ext"] = ext
        m["maskrow"] = mask
        m["c_row8"] = np.ascontiguousarray(c[b].reshape(8, 128))
        in_maps.append(m)
    return in_maps


_prog_cache = {}


def _get_program(taps=()):
    key = tuple(taps)
    if key not in _prog_cache:
        _prog_cache[key] = build_program(taps)
    return _prog_cache[key]


def kernel(**inputs):
    in_maps = _stage_inputs(inputs)
    nc = _get_program()
    res = run_bass_kernel_spmd(nc, in_maps, core_ids=list(range(8)))
    out = np.zeros((4, 4096, 1024), np.float32)
    for core in range(8):
        b = core // 2
        half = core % 2
        y = np.asarray(res.results[core]["y"]).reshape(2048, 1024)
        out[b, half * 2048:(half + 1) * 2048] = y
    return out


# version 09
# speedup vs baseline: 1.0022x; 1.0022x over previous
import numpy as np
import ml_dtypes

import concourse.bass as bass
import concourse.mybir as mybir
from concourse.bass_utils import run_bass_kernel_spmd

F32 = mybir.dt.float32
BF16 = mybir.dt.bfloat16
AF = mybir.ActivationFunctionType
ALU = mybir.AluOpType
AX = mybir.AxisListType
ts = bass.ts

D = 1024
TOK = 128
NT_OWN = 16
NT_EXT = 18
HEADS = 16
WIN = 384
DFF = 4096
NEXP = 8
DEXP = 512
EPS = 1e-5
INV_D = 1.0 / D
QSCALE = 0.125

ENGS = ("sync", "tensor", "vector", "scalar", "gpsimd")
DBG_SKIP = set()


class Sem:
    def __init__(self, h):
        self.h = h
        self.n = 0
        self.base = 0

    def bump(self, k=1):
        self.n += k
        return (self, k)


class Emitter:
    def __init__(self):
        self.streams = {e: [] for e in ENGS}

    def on(self, eng, fn):
        self.streams[eng].append(fn)

    def wait(self, eng, sem, val):
        if val > 0:
            self.on(eng, lambda e, s=sem.h, v=val: e.wait_ge(s, v))

    def op(self, eng, fn, inc=None):
        if inc is None:
            self.on(eng, fn)
        else:
            sem, k = inc
            self.on(eng, lambda e, f=fn, s=sem.h, kk=k: f(e).then_inc(s, kk))


class Arena:

    def __init__(self, nc, base, top):
        b0 = (base + 31) & ~31
        nbytes = ((top - b0) // 32) * 32
        big = nc.alloc_sbuf_tensor_at("arena", [128, nbytes], mybir.dt.uint8,
                                      offset=b0)
        self.views = {F32: big.bitcast(F32).ap(), BF16: big.bitcast(BF16).ap()}
        self.base = 0
        self.top = nbytes
        self.off = 0

    def alloc(self, name, shape, dtype):
        esz = 2 if dtype == BF16 else 4
        elems = int(np.prod(shape[1:]))
        off = (self.off + 31) & ~31
        assert off + elems * esz <= self.top, (
            f"SBUF arena overflow: {name} {elems * esz}B at {off}")
        c0 = off // esz
        self.off = off + elems * esz
        return self.views[dtype][0:shape[0], c0:c0 + elems]


def build_program(taps=(), nphases=6, a2_tiles=NT_OWN, a2_nopf=99):
    nc = bass.Bass("TRN2", target_bir_lowering=False, detect_race_conditions=False)

    def din(name, shape, dtype=F32):
        return nc.dram_tensor(name, list(shape), dtype, kind="ExternalInput").ap()

    x_ext = din("x_ext", [NT_EXT * TOK, D])
    wqkvT = din("wqkvT", [D, 3 * D])
    wprojT = din("wprojT", [D, D])
    wfc1T = din("wfc1T", [D, DFF])
    wfc2T = din("wfc2T", [DFF, D])
    w1_in = din("w1_in", [NEXP, D, DEXP])
    w2_in = din("w2_in", [NEXP, DEXP, D])
    wmod = din("wmod", [D, 6 * D])
    wrouteT = din("wrouteT", [D, NEXP])
    cent_in = din("cent_in", [NEXP, D])
    b2_in = din("b2_in", [NEXP, D])
    qkvb_row = din("qkvb_row", [1, 3 * D])
    projb_row = din("projb_row", [1, D])
    fc1b_fm_in = din("fc1b_fm_in", [128, 32])
    fc2b_row = din("fc2b_row", [1, D])
    b1_row = din("b1_row", [1, NEXP * DEXP])
    lnrow = din("lnrow", [1, 4 * D])
    c_row8 = din("c_row8", [8, 128])
    maskrow = din("maskrow", [1, NT_OWN * WIN])
    ident_bf_in = din("ident_bf_in", [128, 128], BF16)
    ident32_in = din("ident32_in", [128, 128])

    def scr(name, shape, dtype=F32):
        kind = "ExternalOutput" if name in taps else "Internal"
        return nc.dram_tensor(name, list(shape), dtype, kind=kind).ap()

    s_h = scr("s_h", [NT_OWN, TOK, D])
    s_x1 = scr("s_x1", [NT_OWN, TOK, D])
    s_sxa = scr("s_sxa", [NT_OWN, TOK, D])
    s_s2 = scr("s_s2", [NT_OWN, TOK, D])
    s_modrow = scr("s_modrow", [1, 6 * D])
    s_h2T = scr("s_h2T", [NT_OWN, TOK, D], BF16)
    s_dbg = scr("s_dbg", [128, 4 * D])
    s_dbg2 = scr("s_dbg2", [NT_EXT, 128, 8])
    s_dbgao = scr("s_dbgao", [NT_OWN, 128, D], BF16)
    q_scr = scr("q_scr", [8, 128, NT_EXT * TOK], BF16)
    k_scr = scr("k_scr", [8, 128, NT_EXT * TOK], BF16)
    v_scr = scr("v_scr", [NT_EXT, 128, D], BF16)
    y_out = nc.dram_tensor("y", [NT_OWN, TOK, D], F32, kind="ExternalOutput").ap()

    def sb(name, shape, dtype):
        return nc.alloc_sbuf_tensor(name, list(shape), dtype).ap()

    ones_bf = sb("ones_bf", [1, 512], BF16)
    ones32 = sb("ones32", [1, 128], F32)
    ident_bf = sb("ident_bf", [128, 128], BF16)
    ident32 = sb("ident32", [128, 128], F32)
    b2_bf = sb("b2_bf", [NEXP, D], BF16)
    cnT_bf = sb("cnT_bf", [128, 64], BF16)
    wrT_bf = sb("wrT_bf", [128, 64], BF16)
    repl = sb("repl", [128, 4 * D], F32)
    stats_ln2 = sb("stats_ln2", [128, 2 * NT_OWN], F32)
    stats_s2 = sb("stats_s2", [128, 2 * NT_OWN], F32)

    def rs(i):
        return repl[:, i * D:(i + 1) * D]

    A = Arena(nc, nc.sbuf_base, nc.sbuf_top)
    banks = [nc.alloc_psum_tensor(f"bank{i}", [128, 512], F32) for i in range(8)]

    def bf(i):
        return banks[i].ap()

    def bb(i):
        return banks[i].bitcast(BF16).ap()

    E = Emitter()
    sem_stack = []
    sem_pool = []
    store_pool = []
    pool_state = {"i": 0, "j": 0}

    def _alloc_sem(name):
        cm = nc.semaphore(name)
        h = cm.__enter__()
        sem_stack.append(cm)
        return Sem(h)

    def mksem(name):
        i = pool_state["i"]
        if i >= len(sem_pool):
            sem_pool.append(_alloc_sem(f"pool{i}"))
        pool_state["i"] = i + 1
        return sem_pool[i]

    def mkstore(name):
        j = pool_state["j"]
        if j >= len(store_pool):
            store_pool.append(_alloc_sem(f"spool{j}"))
        pool_state["j"] = j + 1
        return store_pool[j]

    hw_pool = []

    def mkdma(name):
        k = pool_state.get("k", 0)
        if k >= len(hw_pool):
            hw_pool.append(_alloc_sem(f"hpool{k}"))
        pool_state["k"] = k + 1
        return hw_pool[k]

    def pool_reset():
        pool_state["i"] = 0
        pool_state["j"] = 0
        pool_state["k"] = 0
        for s in sem_pool + store_pool + hw_pool:
            s.base = s.n

    def waitr(eng, sem, rel):
        if rel > 0:
            E.wait(eng, sem, sem.base + rel)

    dv = {}

    def dma(eng, fn, sem, label=None):
        E.op(eng, fn, inc=sem.bump(16))
        if label is not None:
            dv[label] = (sem, sem.n)

    def wdma(eng, label):
        sv = dv.get(label)
        if sv is not None:
            E.wait(eng, sv[0], sv[1])

    BAR = _alloc_sem("BAR")

    def finalize():
        with nc.Block() as block:
            for eng in ENGS:
                getattr(block, eng)(
                    (lambda fns: (lambda e: [f(e) for f in fns]))(
                        E.streams[eng]))
        return nc

    def barrier(clear=True):
        for eng in ENGS:
            E.op(eng, lambda e: e.drain(), inc=BAR.bump())
        tgt = BAR.n
        for eng in ENGS:
            E.wait(eng, BAR, tgt)
        del clear

    def emit_lnstats(X, sm, sems, rstd_out, nmr_out, sq_trash,
                     reduce_inc=None):
        SC = sems["SC"]
        if reduce_inc is None:
            reduce_inc = SC.bump()
        rsem = reduce_inc[0]
        E.op("vector", lambda e: e.reduce_sum(sm["s1"], X, axis=AX.X),
             inc=reduce_inc)
        E.op("scalar", lambda e: e.activation(sq_trash, X, AF.Square,
                                              accum_out=sm["varb"]),
             inc=sems["SB"].bump())
        E.wait("vector", rsem, rsem.n)
        E.op("vector", lambda e: e.tensor_scalar(sm["mu"], sm["s1"], INV_D,
                                                 None, ALU.mult),
             inc=SC.bump())
        E.wait("vector", sems["SB"], sems["SB"].n)
        E.wait("vector", SC, SC.n)
        E.op("vector", lambda e: e.tensor_mul(sm["mu2"], sm["mu"], sm["mu"]),
             inc=SC.bump())
        E.wait("vector", SC, SC.n)
        E.op("vector", lambda e: e.scalar_tensor_tensor(
            sm["var"], sm["varb"], INV_D, sm["mu2"], ALU.mult, ALU.subtract),
            inc=SC.bump())
        E.wait("vector", SC, SC.n)
        E.op("vector", lambda e: e.tensor_scalar(sm["var"], sm["var"], EPS,
                                                 None, ALU.add),
             inc=SC.bump())
        E.wait("scalar", SC, SC.n)
        E.op("scalar", lambda e: e.activation(sm["lnv"], sm["var"], AF.Ln),
             inc=sems["SD"].bump())
        E.wait("scalar", sems["SD"], sems["SD"].n)
        E.op("scalar", lambda e, r=rstd_out: e.activation(r, sm["lnv"], AF.Exp,
                                                          scale=-0.5),
             inc=sems["SD"].bump())
        E.wait("vector", sems["SD"], sems["SD"].n)
        E.op("vector", lambda e, r=rstd_out, o=nmr_out: e.tensor_scalar(
            o, sm["mu"], r, -1.0, ALU.mult, ALU.mult), inc=sems["SE"].bump())

    RMM = _alloc_sem("RMM")
    REV = _alloc_sem("REV")

    def replicate_row(src_row, dst, b0, b1, add_one=False):
        E.wait("tensor", REV, REV.n)
        E.op("tensor", lambda e: e.matmul(bf(b0)[:, :], ones32[0:1, 0:128],
                                          src_row[0:1, 0:512],
                                          start=True, stop=True))
        E.op("tensor", lambda e: e.matmul(bf(b1)[:, :], ones32[0:1, 0:128],
                                          src_row[0:1, 512:1024],
                                          start=True, stop=True),
             inc=RMM.bump())
        E.wait("scalar", RMM, RMM.n)
        bias = 1.0 if add_one else 0.0
        E.op("scalar", lambda e: e.activation(dst[:, 0:512], bf(b0)[:, :],
                                              AF.Identity, bias=bias))
        E.op("scalar", lambda e: e.activation(dst[:, 512:1024], bf(b1)[:, :],
                                              AF.Identity, bias=bias),
             inc=REV.bump())

    LD0 = mkdma("LD0")
    LDmod = [mkdma("LDmod0"), mkdma("LDmod1")]
    LDrow = mkdma("LDrow")
    CAST0 = mksem("CAST0")
    MODMM = mksem("MODMM")
    MODEV = mksem("MODEV")
    P0A = mksem("P0A")
    P0B = mksem("P0B")
    P0C = mksem("P0C")
    ST0 = mkstore("ST0")

    wstage = [A.alloc(f"wst0_{i}", [128, 3072], F32) for i in range(2)]
    wmod_bf = A.alloc("wmod_bf", [128, 8 * 6144], BF16)
    modrow = A.alloc("modrow", [1, 6144], F32)
    c8 = A.alloc("c8", [8, 128], F32)
    cfm_bf = A.alloc("cfm_bf", [128, 8], BF16)
    cent = A.alloc("cent", [8, D], F32)
    centsq = A.alloc("centsq", [8, D], F32)
    css = A.alloc("css", [8, 2], F32)
    cnorm_bf = A.alloc("cnorm_bf", [8, D], BF16)
    wr_st = A.alloc("wr_st", [128, 64], F32)
    b2_st = A.alloc("b2_st", [8, D], F32)
    row_a = A.alloc("row_a", [1, D], F32)
    trow = [A.alloc(f"trow{i}", [1, D], F32) for i in range(2)]

    for dst, src in [(ident_bf[:, :], ident_bf_in[:, :]),
                     (ident32[:, :], ident32_in[:, :]),
                     (c8[:, :], c_row8[:, :]),
                     (cent[:, :], cent_in[:, :]),
                     (b2_st[:, :], b2_in[:, :])]:
        E.op("sync", lambda e, d=dst, s=src: e.dma_start(d, s), inc=LD0.bump(16))
    wr_src = bass.AP(wrouteT.tensor, 0, [[8, 128], [1024, 8], [1, 8]])
    E.op("sync", lambda e, s=wr_src: e.dma_start(
        wr_st.rearrange("p (a b) -> p a b", a=8), s), inc=LD0.bump(16))
    ld_pre = LD0.n

    E.op("vector", lambda e: e.memset(ones_bf[:, :], 1.0))
    E.op("vector", lambda e: e.memset(ones32[:, :], 1.0), inc=P0A.bump())
    E.wait("vector", LD0, ld_pre)
    E.op("vector", lambda e: e.tensor_copy(wrT_bf[:, :], wr_st[:, :]))
    E.op("vector", lambda e: e.tensor_copy(b2_bf[:, :], b2_st[:, :]))

    E.wait("scalar", LD0, ld_pre)
    E.op("scalar", lambda e: e.activation(c8[:, :], c8[:, :], AF.Silu),
         inc=P0B.bump())
    E.wait("tensor", P0B, P0B.n)
    E.wait("tensor", LD0, ld_pre)
    E.op("tensor", lambda e: e.transpose(bf(6)[0:128, 0:8], c8[0:8, 0:128],
                                         ident32[0:8, 0:8]), inc=MODMM.bump())
    E.wait("scalar", MODMM, MODMM.n)
    E.op("scalar", lambda e: e.activation(cfm_bf[:, :], bf(6)[0:128, 0:8],
                                          AF.Copy), inc=MODEV.bump())

    for k in range(8):
        for h in range(2):
            i = k * 2 + h
            if i >= 2:
                waitr("sync", CAST0, i - 1)
            E.op("sync", lambda e, kk=k, hh=h, ii=i: e.dma_start(
                wstage[ii % 2][:, :],
                wmod[ts(kk, 128), hh * 3072:(hh + 1) * 3072]),
                inc=LDmod[i % 2].bump(16))
            E.wait("vector", LDmod[i % 2], LDmod[i % 2].n)
            E.op("vector", lambda e, kk=k, hh=h, ii=i: e.tensor_copy(
                wmod_bf[:, kk * 6144 + hh * 3072: kk * 6144 + (hh + 1) * 3072],
                wstage[ii % 2][:, :]), inc=CAST0.bump())
    cast_all = CAST0.n

    E.wait("tensor", CAST0, cast_all)
    E.wait("tensor", MODEV, MODEV.n)
    for half in range(2):
        if half == 1:
            E.wait("tensor", MODEV, MODEV.n)
        for k in range(8):
            for nch in range(6):
                col = half * 3072 + nch * 512
                E.op("tensor", lambda e, kk=k, nn=nch, cc=col: e.matmul(
                    bf(nn)[0:1, :], cfm_bf[:, kk:kk + 1],
                    wmod_bf[:, kk * 6144 + cc: kk * 6144 + cc + 512],
                    start=(kk == 0), stop=(kk == 7)))
        E.op("tensor", lambda e: e.drain(), inc=MODMM.bump())
        E.wait("scalar", MODMM, MODMM.n)
        for nch in range(6):
            col = half * 3072 + nch * 512
            E.op("scalar", lambda e, nn=nch, cc=col: e.activation(
                modrow[0:1, cc:cc + 512], bf(nn)[0:1, :], AF.Identity),
                inc=MODEV.bump())

    E.wait("vector", MODEV, MODEV.n)
    E.op("vector", lambda e: e.tensor_scalar_add(modrow[0:1, 1024:2048],
                                                 modrow[0:1, 1024:2048], 1.0))
    E.op("vector", lambda e: e.tensor_scalar_add(modrow[0:1, 4096:5120],
                                                 modrow[0:1, 4096:5120], 1.0))
    for gc in (2048, 5120):
        g = modrow[0:1, gc:gc + 1024]
        E.op("vector", lambda e, gg=g: e.tensor_scalar(trow[0][0:1, :], gg,
                                                       -1.0, None, ALU.mult),
             inc=P0C.bump())
        E.wait("vector", P0C, P0C.n)
        E.op("vector", lambda e, gg=g: e.tensor_max(trow[0][0:1, :],
                                                    trow[0][0:1, :], gg),
             inc=P0C.bump())
        E.wait("vector", P0C, P0C.n)
        E.op("vector", lambda e: e.tensor_scalar_max(trow[0][0:1, :],
                                                     trow[0][0:1, :], 0.25),
             inc=P0C.bump())
        E.wait("vector", P0C, P0C.n)
        E.op("vector", lambda e, gg=g: e.tensor_scalar(trow[1][0:1, :], gg, 0.0,
                                                       None, ALU.is_ge),
             inc=P0C.bump())
        E.wait("vector", P0C, P0C.n)
        E.op("vector", lambda e: e.tensor_scalar(trow[1][0:1, :],
                                                 trow[1][0:1, :], 2.0, -1.0,
                                                 ALU.mult, ALU.add),
             inc=P0C.bump())
        E.wait("vector", P0C, P0C.n)
        E.op("vector", lambda e, gg=g: e.tensor_mul(gg, trow[1][0:1, :],
                                                    trow[0][0:1, :]),
             inc=P0C.bump())
        E.wait("vector", P0C, P0C.n)
    E.op("vector", lambda e: e.tensor_copy(trow[0][0:1, 0:8], modrow[0:1, 0:8]),
         inc=P0C.bump())
    E.wait("scalar", P0C, P0C.n)
    E.op("scalar", lambda e: e.dma_start(s_modrow[0:1, :], modrow[0:1, :]),
         inc=ST0.bump(16))

    E.wait("tensor", P0C, P0C.n)
    replicate_row(modrow[0:1, 0:1024], rs(0), 0, 1)
    replicate_row(modrow[0:1, 1024:2048], rs(1), 2, 3)
    replicate_row(modrow[0:1, 2048:3072], rs(2), 4, 5)
    E.op("sync", lambda e: e.dma_start(row_a[0:1, :], lnrow[0:1, 0:1024]),
         inc=LDrow.bump(16))
    E.wait("tensor", LDrow, LDrow.n)
    replicate_row(row_a[0:1, :], rs(3), 0, 1)

    E.wait("scalar", LD0, ld_pre)
    E.op("scalar", lambda e: e.activation(centsq[0:8, :], cent[0:8, :],
                                          AF.Square, accum_out=css[0:8, 0:1]),
         inc=P0B.bump())
    E.wait("scalar", P0B, P0B.n)
    E.op("scalar", lambda e: e.activation(css[0:8, 1:2], css[0:8, 0:1], AF.Ln),
         inc=P0B.bump())
    E.wait("scalar", P0B, P0B.n)
    E.op("scalar", lambda e: e.activation(css[0:8, 0:1], css[0:8, 1:2], AF.Exp,
                                          scale=0.5), inc=P0B.bump())
    E.wait("vector", P0B, P0B.n)
    E.op("vector", lambda e: e.tensor_scalar_max(css[0:8, 0:1], css[0:8, 0:1],
                                                 1e-12), inc=P0C.bump())
    E.wait("vector", P0C, P0C.n)
    E.op("vector", lambda e: e.reciprocal(css[0:8, 1:2], css[0:8, 0:1]),
         inc=P0C.bump())
    E.wait("vector", P0C, P0C.n)
    E.op("vector", lambda e: e.tensor_scalar(cnorm_bf[0:8, :], cent[0:8, :],
                                             css[0:8, 1:2], None, ALU.mult),
         inc=P0C.bump())
    E.wait("tensor", P0C, P0C.n)
    E.wait("tensor", REV, REV.n)
    for c in range(8):
        E.op("tensor", lambda e, cc=c: e.transpose(
            bb(6)[0:128, cc * 8:(cc + 1) * 8], cnorm_bf[0:8, ts(cc, 128)],
            ident_bf[0:8, 0:8]), inc=MODMM.bump() if c == 7 else None)
    E.wait("vector", MODMM, MODMM.n)
    E.op("vector", lambda e: e.tensor_copy(cnT_bf[:, :], bb(6)[0:128, 0:64]))

    if "s_dbg" in taps:
        E.wait("scalar", REV, REV.n)
        E.op("scalar", lambda e: e.dma_start(s_dbg[:, :], repl[:, :]),
             inc=ST0.bump(16))
    E.wait("scalar", ST0, ST0.n)
    barrier()
    if nphases == 1:
        return finalize()

    pool_reset()
    A.off = A.base
    wproj_bf = A.alloc("wproj_bf", [128, 8 * D], BF16)
    projb_bf = A.alloc("projb_bf", [1, D], BF16)
    maskrow_bf = A.alloc("maskrow_bf", [1, NT_OWN * WIN], BF16)
    ckptA = A.off
    wqkv_bf = A.alloc("wqkv_bf", [128, 8 * 3072], BF16)
    wstA = [A.alloc(f"wstA{i}", [128, 1536], F32) for i in range(2)]
    qkvb_bf = A.alloc("qkvb_bf", [1, 3 * D], BF16)
    rowstA = A.alloc("rowstA", [1, 3072], F32)
    lnb1_repl = A.alloc("lnb1_repl", [128, D], F32)
    x_ring = [A.alloc(f"xr{i}", [128, D], F32) for i in range(2)]
    h_ring = [A.alloc(f"hr{i}", [128, D], F32) for i in range(2)]
    u_buf = A.alloc("u_buf", [128, D], F32)
    u1_buf = A.alloc("u1_buf", [128, D], F32)
    h1_buf = A.alloc("h1_buf", [128, D], F32)
    a1_buf = A.alloc("a1_buf", [128, D], F32)
    abf_ring = [A.alloc(f"abf{i}", [128, D], BF16) for i in range(2)]
    sq_tr = A.alloc("sq_tr", [128, D], F32)
    aT_gr = [A.alloc(f"aT_g{i}", [128, 8 * 512], BF16) for i in range(2)]
    qk_st = [A.alloc(f"qkst{i}", [128, 512], BF16) for i in range(2)]
    v_st = [A.alloc(f"vst{i}", [128, D], BF16) for i in range(2)]
    smA = {k: A.alloc(f"smA_{k}", [128, 1], F32) for k in
           ("s1", "mu", "mu2", "varb", "var", "lnv", "rstd", "nmr",
            "s1b", "mub", "mu2b", "varbb", "varB", "lnvb", "rstdb", "nmrb")}
    sm1 = {k: smA[k] for k in ("s1", "mu", "mu2", "varb", "var", "lnv")}
    sm2 = {"s1": smA["s1b"], "mu": smA["mub"], "mu2": smA["mu2b"],
           "varb": smA["varbb"], "var": smA["varB"], "lnv": smA["lnvb"]}

    LDWs = [mkdma("LDWa"), mkdma("LDWb")]
    LDROW = mkdma("LDROW")
    WCAST = _alloc_sem("WCAST")
    LDXs = [mkdma("LDXa"), mkdma("LDXb")]
    SXR = mksem("SXR")
    SUX = mksem("SUX")
    SUH = mksem("SUH")
    SH = mksem("SH")
    SABF = mksem("SABF")
    STR = mksem("STR")
    SATEV = mksem("SATEV")
    SQMM = mksem("SQMM")
    SQEV = mksem("SQEV")
    SVMM = mksem("SVMM")
    SVEV = mksem("SVEV")
    STHs = [mkstore("STHa"), mkstore("STHb")]
    STQs = [mkstore("STQa"), mkstore("STQb")]
    STVs = [mkstore("STVa"), mkstore("STVb")]
    SLN1 = {k: mksem(f"A1a{k}") for k in ("SB", "SC", "SD", "SE")}
    SLN2 = {k: mksem(f"A1b{k}") for k in ("SB", "SC", "SD", "SE")}
    CH1 = mksem("CH1")

    dma("sync", lambda e: e.dma_start(rowstA[0:1, 0:1024],
                                      lnrow[0:1, 1024:2048]), LDROW)
    E.wait("tensor", LDROW, LDROW.n)
    replicate_row(rowstA[0:1, 0:1024], lnb1_repl, 0, 1)
    repl_a_done = REV.n

    E.wait("sync", RMM, RMM.n)
    E.op("sync", lambda e: e.dma_start(rowstA[0:1, :], qkvb_row[0:1, :]),
         inc=LDROW.bump(16))
    E.wait("vector", LDROW, LDROW.n)
    E.op("vector", lambda e: e.tensor_copy(qkvb_bf[0:1, :], rowstA[0:1, :]),
         inc=WCAST.bump())
    wc0 = WCAST.n
    for p in range(3):
        for k in range(8):
            i = p * 8 + k
            if i >= 2:
                E.wait("sync", WCAST, wc0 + i - 1)
            sW = LDWs[i % 2]
            E.op("sync", lambda e, kk=k, pp=p, ii=i: e.dma_start(
                wstA[ii % 2][:, 0:1024],
                wqkvT[ts(kk, 128), pp * 1024:(pp + 1) * 1024]),
                inc=sW.bump(16))
            E.wait("vector", sW, sW.n)
            E.op("vector", lambda e, kk=k, pp=p, ii=i: e.tensor_copy(
                wqkv_bf[:, kk * 3072 + pp * 1024: kk * 3072 + (pp + 1) * 1024],
                wstA[ii % 2][:, 0:1024]), inc=WCAST.bump())
        if p == 0:
            wcast_q = WCAST.n
        elif p == 1:
            wcast_k = WCAST.n
    wcast_v = WCAST.n
    E.wait("sync", WCAST, WCAST.n)
    E.op("sync", lambda e: e.dma_start(rowstA[0:1, 0:3072],
                                       maskrow[0:1, 0:3072]), inc=LDROW.bump(16))
    E.wait("vector", LDROW, LDROW.n)
    E.op("vector", lambda e: e.tensor_copy(maskrow_bf[0:1, 0:3072],
                                           rowstA[0:1, 0:3072]),
         inc=WCAST.bump())
    E.wait("sync", WCAST, WCAST.n)
    E.op("sync", lambda e: e.dma_start(rowstA[0:1, 0:3072],
                                       maskrow[0:1, 3072:6144]),
         inc=LDROW.bump(16))
    E.wait("vector", LDROW, LDROW.n)
    E.op("vector", lambda e: e.tensor_copy(maskrow_bf[0:1, 3072:6144],
                                           rowstA[0:1, 0:3072]),
         inc=WCAST.bump())
    wcast_qkv = WCAST.n

    def own_count(t):
        return max(0, min(t, 16))

    dma("sync", lambda e: e.dma_start(x_ring[0][:, :], x_ext[ts(0, 128), :]),
        LDXs[0], label=("a1x", 0))

    groups = [(0, 4), (4, 8), (8, 12), (12, 16), (16, 18)]

    def prep_tile_a1(t, lt, slot):
        xb = x_ring[t % 2]
        hb = h_ring[t % 2]
        ab = abf_ring[t % 2]
        if t + 1 < NT_EXT:
            waitr("sync", SXR, t)
            waitr("sync", SUX, t)
            dma("sync", lambda e, tt=t + 1: e.dma_start(
                x_ring[tt % 2][:, :], x_ext[ts(tt, 128), :]),
                LDXs[(t + 1) % 2], label=("a1x", t + 1))
        wdma("vector", ("a1x", t))
        wdma("scalar", ("a1x", t))
        emit_lnstats(xb[:, :], sm1, SLN1, smA["rstd"], smA["nmr"], sq_tr,
                     reduce_inc=SXR.bump())
        E.wait("scalar", SLN1["SE"], SLN1["SE"].n)
        waitr("scalar", SH, t)
        E.op("scalar", lambda e, x=xb: e.activation(
            u_buf[:, :], x[:, :], AF.Identity, scale=smA["rstd"],
            bias=smA["nmr"]), inc=SUX.bump())
        E.wait("vector", SUX, SUX.n)
        if t == 0:
            E.wait("vector", REV, repl_a_done)
        if t >= 2:
            wdma("vector", ("a1h", t - 2))
            waitr("vector", SUH, t - 1)
        E.op("vector", lambda e: e.scalar_tensor_tensor(
            h1_buf[:, :], u_buf[:, :], 1.0, rs(1), ALU.mult, ALU.mult),
            inc=CH1.bump())
        E.wait("vector", CH1, CH1.n)
        E.op("vector", lambda e, h=hb: e.scalar_tensor_tensor(
            h[:, :], h1_buf[:, :], 1.0, rs(0), ALU.mult, ALU.add),
            inc=SH.bump())
        if 1 <= t <= 16:
            E.wait("scalar", SH, SH.n)
            dma("scalar", lambda e, tt=t, h=hb: e.dma_start(
                s_h[tt - 1, :, :], h[:, :]), STHs[t % 2], label=("a1h", t))
        E.wait("scalar", SH, SH.n)
        E.wait("vector", SH, SH.n)
        emit_lnstats(hb[:, :], sm2, SLN2, smA["rstdb"], smA["nmrb"], sq_tr)
        E.wait("scalar", SLN2["SE"], SLN2["SE"].n)
        waitr("scalar", SABF, t)
        E.op("scalar", lambda e, h=hb: e.activation(
            u1_buf[:, :], h[:, :], AF.Identity, scale=smA["rstdb"],
            bias=smA["nmrb"]), inc=SUH.bump())
        E.wait("vector", SUH, SUH.n)
        waitr("vector", STR, t - 1)
        E.op("vector", lambda e: e.scalar_tensor_tensor(
            a1_buf[:, :], u1_buf[:, :], 1.0, rs(3), ALU.mult, ALU.mult),
            inc=CH1.bump())
        E.wait("vector", CH1, CH1.n)
        E.op("vector", lambda e, a=ab: e.scalar_tensor_tensor(
            a[:, :], a1_buf[:, :], 1.0, lnb1_repl, ALU.mult, ALU.add),
            inc=SABF.bump())
        E.wait("tensor", SABF, SABF.n)
        waitr("tensor", SATEV, t - 1)
        for c in range(8):
            E.op("tensor", lambda e, cc=c, a=ab, bk=t % 2: e.transpose(
                bb(bk)[:, ts(cc, 128)], a[:, ts(cc, 128)], ident_bf[:, :]),
                inc=STR.bump() if c == 7 else None)
        E.wait("vector", STR, STR.n)
        dstA = aT_gr[slot].rearrange("p (c w) -> p c w", c=8)[
            :, :, lt * 128:(lt + 1) * 128]
        E.op("vector", lambda e, d=dstA, bk=t % 2: e.tensor_copy(
            d, bb(bk).rearrange("p (c w) -> p c w", c=8)),
            inc=SATEV.bump())

    for t in range(4):
        prep_tile_a1(t, t, 0)
    qk_use = 0
    for gi, (t0, t1) in enumerate(groups):
        ng = (t1 - t0) * 128
        nxt = groups[gi + 1] if gi + 1 < len(groups) else None
        aTb = aT_gr[gi % 2]
        E.wait("tensor", SATEV, SATEV.n)
        for m in range(16):
            if gi == 0 and m == 0:
                E.wait("tensor", WCAST, wcast_q)
            if gi == 0 and m == 8:
                E.wait("tensor", WCAST, wcast_k)
            is_q = m < 8
            mm8 = m % 8
            colw = mm8 * 128 if is_q else 1024 + mm8 * 128
            bk = 2 + (qk_use % 2)
            waitr("tensor", SQEV, qk_use - 1)
            for k in range(8):
                E.op("tensor", lambda e, kk=k, cw=colw, b=bk, nn=ng, aT=aTb:
                     e.matmul(
                         bf(b)[:, 0:nn],
                         wqkv_bf[:, kk * 3072 + cw: kk * 3072 + cw + 128],
                         aT[:, kk * 512: kk * 512 + nn],
                         start=(kk == 0), stop=False))
            E.op("tensor", lambda e, cw=colw, b=bk, nn=ng: e.matmul(
                bf(b)[:, 0:nn], qkvb_bf[0:1, cw:cw + 128], ones_bf[0:1, 0:nn],
                start=False, stop=True), inc=SQMM.bump())
            E.wait("scalar", SQMM, SQMM.n)
            wdma("scalar", ("a1qk", qk_use - 2))
            sc = QSCALE if is_q else 1.0
            E.op("scalar", lambda e, b=bk, s=sc, st=qk_st[qk_use % 2], nn=ng:
                 e.activation(st[:, 0:nn], bf(b)[:, 0:nn], AF.Copy, scale=s),
                 inc=SQEV.bump())
            E.wait("scalar", SQEV, SQEV.n)
            dscr = q_scr if is_q else k_scr
            dma("scalar", lambda e, d=dscr, mmm=mm8, st=qk_st[qk_use % 2],
                c0=t0 * 128, nn=ng: e.dma_start(d[mmm, :, c0:c0 + nn],
                                                st[:, 0:nn]),
                STQs[qk_use % 2], label=("a1qk", qk_use))
            qk_use += 1
            if nxt is not None and m % 4 == 3:
                j = m // 4
                if nxt[0] + j < nxt[1]:
                    prep_tile_a1(nxt[0] + j, j, (gi + 1) % 2)
        if gi == 0:
            E.wait("tensor", WCAST, wcast_v)
        for t in range(t0, t1):
            lt = t - t0
            bks = (4, 5) if (t % 2 == 0) else (6, 7)
            E.wait("tensor", SVEV, max(0, SVMM.n - 1))
            for n in range(2):
                for k in range(8):
                    E.op("tensor", lambda e, kk=k, nn=n, b=bks[n], l=lt,
                         aT=aTb: e.matmul(
                             bf(b)[:, :],
                             aT[:, kk * 512 + l * 128: kk * 512 + l * 128 + 128],
                             wqkv_bf[:, kk * 3072 + 2048 + nn * 512:
                                     kk * 3072 + 2048 + nn * 512 + 512],
                             start=(kk == 0), stop=False))
                E.op("tensor", lambda e, nn=n, b=bks[n]: e.matmul(
                    bf(b)[:, :], ones_bf[0:1, 0:128],
                    qkvb_bf[0:1, 2048 + nn * 512: 2048 + nn * 512 + 512],
                    start=False, stop=True),
                    inc=SVMM.bump() if n == 1 else None)
            E.wait("scalar", SVMM, SVMM.n)
            wdma("scalar", ("a1v", t - 2))
            E.op("scalar", lambda e, b=bks[0], st=v_st[t % 2]: e.activation(
                st[:, 0:512], bf(b)[:, :], AF.Copy))
            E.op("scalar", lambda e, b=bks[1], st=v_st[t % 2]: e.activation(
                st[:, 512:1024], bf(b)[:, :], AF.Copy), inc=SVEV.bump())
            E.wait("scalar", SVEV, SVEV.n)
            dma("scalar", lambda e, tt=t, st=v_st[t % 2]: e.dma_start(
                v_scr[tt, :, :], st[:, :]), STVs[t % 2], label=("a1v", t))

    for k in range(8):
        E.wait("sync", WCAST, wcast_v if k < 2 else wcast_qkv + k - 1)
        sW = LDWs[k % 2]
        E.op("sync", lambda e, kk=k: e.dma_start(
            wstA[kk % 2][:, 0:1024], wprojT[ts(kk, 128), :]), inc=sW.bump(16))
        E.wait("vector", sW, sW.n)
        E.op("vector", lambda e, kk=k: e.tensor_copy(
            wproj_bf[:, kk * 1024:(kk + 1) * 1024], wstA[kk % 2][:, 0:1024]),
            inc=WCAST.bump())
    E.wait("sync", WCAST, WCAST.n)
    E.op("sync", lambda e: e.dma_start(rowstA[0:1, 0:1024], projb_row[0:1, :]),
         inc=LDROW.bump(16))
    E.wait("vector", LDROW, LDROW.n)
    E.op("vector", lambda e: e.tensor_copy(projb_bf[0:1, :], rowstA[0:1, 0:1024]),
         inc=WCAST.bump())
    wcast_proj = WCAST.n

    for s in STHs + STQs + STVs:
        E.wait("scalar", s, s.n)
    barrier()
    if nphases == 2:
        return finalize()

    pool_reset()
    A.off = ckptA
    k_ring = [A.alloc(f"kring{i}", [128, 8 * 128], BF16) for i in range(4)]
    v_ring = [A.alloc(f"vring{i}", [128, D], BF16) for i in range(4)]
    q_ring = [A.alloc(f"qring{i}", [128, 8 * 512], BF16) for i in range(4)]
    h2_ring = [A.alloc(f"h2r{i}", [128, D], F32) for i in range(2)]
    x2_ring = [A.alloc(f"x2r{i}", [128, D], F32) for i in range(2)]
    p_sb = [A.alloc(f"psb{i}", [128, WIN], BF16) for i in range(3)]
    pT_sb = [A.alloc(f"pTsb{i}", [128, WIN], BF16) for i in range(3)]
    ao_sb = A.alloc("ao_sb", [128, D], BF16)
    aoT_sb = A.alloc("aoT_sb", [128, D], BF16)
    x1_sb = [A.alloc(f"x1sb{i}", [128, D], F32) for i in range(2)]
    sxa_sb = [A.alloc(f"sxasb{i}", [128, D], F32) for i in range(2)]
    sxa_t1 = A.alloc("sxa_t1", [128, D], F32)
    ssum = A.alloc("ssum", [128, HEADS], F32)
    rrb = A.alloc("rrb", [128, HEADS], F32)
    sq_tr2 = A.alloc("sq_tr2", [128, D], F32)
    s1A2 = A.alloc("s1A2", [128, NT_OWN], F32)
    vbA2 = A.alloc("vbA2", [128, NT_OWN], F32)
    muA2 = A.alloc("muA2", [128, NT_OWN], F32)
    t1A2 = A.alloc("t1A2", [128, NT_OWN], F32)

    LKVs = [mkdma(f"LKV{i}") for i in range(4)]
    LVVs = [mkdma(f"LVV{i}") for i in range(4)]
    LQs = [mkdma(f"LQ{i}") for i in range(2)]
    LHs = [mkdma(f"LH{i}") for i in range(2)]
    LX2s = [mkdma(f"LX2{i}") for i in range(2)]
    SSC = mksem("SSC")
    SEXP = mksem("SEXP")
    SRR = mksem("SRR")
    SPT = mksem("SPT")
    SPTEp = [mksem("SPTEa"), mksem("SPTEb")]
    SVM2 = mksem("SVM2")
    SOEV = mksem("SOEV")
    SAOT = mksem("SAOT")
    SAOE = mksem("SAOE")
    SPRJ = mksem("SPRJ")
    SX1 = mksem("SX1")
    SSXA = mksem("SSXA")
    STX1s = [mkstore("STX1a"), mkstore("STX1b")]
    STSXAs = [mkstore("STSXAa"), mkstore("STSXAb")]
    SLNA2 = {k: mksem(f"A2c{k}") for k in ("SB", "SC", "SD", "SE")}
    CH2 = mksem("CH2")

    def kload(slot_ext):
        src = bass.AP(k_scr.tensor, slot_ext * 128,
                      [[NT_EXT * TOK, 128], [128 * NT_EXT * TOK, 8], [1, 128]])
        dst = k_ring[slot_ext % 4].rearrange("p (c w) -> p c w", c=8)
        return dst, src

    for s in range(3):
        dst, src = kload(s)
        dma("sync", lambda e, d=dst, ss=src: e.dma_start(d, ss),
            LKVs[s % 4], label=("kv", s))
        dma("sync", lambda e, d=v_ring[s % 4], ss=s: e.dma_start(
            d[:, :], v_scr[ss, :, :]), LVVs[s % 4], label=("vv", s))
    qsrc0 = bass.AP(q_scr.tensor, 128,
                    [[NT_EXT * TOK, 128], [128 * NT_EXT * TOK, 8], [1, 512]])
    dma("sync", lambda e, s=qsrc0: e.dma_start(
        q_ring[0].rearrange("p (c w) -> p c w", c=8), s), LQs[0],
        label=("qg", 0))
    dma("sync", lambda e: e.dma_start(h2_ring[0][:, :], s_h[0, :, :]),
        LHs[0], label=("h", 0))
    dma("sync", lambda e: e.dma_start(x2_ring[0][:, :], x_ext[ts(1, 128), :]),
        LX2s[0], label=("x2", 0))

    for t in range(min(NT_OWN, a2_tiles)):
        g = t // 4
        if t + 3 < NT_EXT and t < a2_nopf:
            waitr("sync", SVM2, HEADS * t)
            dst, src = kload(t + 3)
            dma("sync", lambda e, d=dst, ss=src: e.dma_start(d, ss),
                LKVs[(t + 3) % 4], label=("kv", t + 3))
            dma("sync", lambda e, d=v_ring[(t + 3) % 4], ss=t + 3:
                e.dma_start(d[:, :], v_scr[ss, :, :]), LVVs[(t + 3) % 4],
                label=("vv", t + 3))
        if t % 4 == 0 and g + 1 < 4 and t < a2_nopf:
            waitr("sync", SSC, HEADS * 4 * g)
            c0 = (g + 1) * 512 + 128
            nq = min(512, NT_EXT * TOK - c0)
            qsrc = bass.AP(q_scr.tensor, c0,
                           [[NT_EXT * TOK, 128], [128 * NT_EXT * TOK, 8],
                            [1, nq]])
            dma("sync", lambda e, d=q_ring[(g + 1) % 4], ss=qsrc, nn=nq:
                e.dma_start(d.rearrange("p (c w) -> p c w", c=8)[:, :, 0:nn],
                            ss), LQs[(g + 1) % 2], label=("qg", g + 1))
        if t + 1 < min(NT_OWN, a2_nopf):
            waitr("sync", SX1, t)
            dma("sync", lambda e, tt=t + 1: e.dma_start(
                h2_ring[tt % 2][:, :], s_h[tt, :, :]), LHs[(t + 1) % 2],
                label=("h", t + 1))
            waitr("sync", SSXA, t)
            dma("sync", lambda e, tt=t + 1: e.dma_start(
                x2_ring[tt % 2][:, :], x_ext[ts(tt + 1, 128), :]),
                LX2s[(t + 1) % 2], label=("x2", t + 1))

        qg = q_ring[g % 4]
        qcol0 = (t % 4) * 128
        masked = t in (0, NT_OWN - 1)

        def emit_tr_copy(idx):
            tb = 2 if idx % 2 == 0 else 5
            waitr("tensor", SEXP, idx + 1)
            waitr("tensor", SPTEp[idx % 2], idx // 2)
            for j in range(3):
                E.op("tensor", lambda e, jj=j, pp=p_sb[idx % 3], b=tb:
                     e.transpose(
                         bb(b)[:, jj * 128:(jj + 1) * 128],
                         pp[:, jj * 128:(jj + 1) * 128], ident_bf[:, :]),
                     inc=SPT.bump() if j == 2 else None)
            if idx % 2 == 0:
                E.wait("scalar", SPT, SPT.n)
                E.op("scalar", lambda e, pt=pT_sb[idx % 3], b=tb:
                     e.activation(pt[:, :], bb(b)[:, 0:WIN], AF.Copy),
                     inc=SPTEp[0].bump())
            else:
                E.wait("vector", SPT, SPT.n)
                E.op("vector", lambda e, pt=pT_sb[idx % 3], b=tb:
                     e.tensor_copy(pt[:, :], bb(b)[:, 0:WIN]),
                     inc=SPTEp[1].bump())

        def emit_v(idx):
            hh = idx % HEADS
            tt = idx // HEADS
            hf = hh // 8
            so = hh % 8
            waitr("tensor", SPTEp[idx % 2], idx // 2 + 1)
            if so == 0:
                waitr("tensor", SOEV, 8 * ((tt * 2 + hf) - 1))
            for j in range(3):
                E.op("tensor", lambda e, jj=j, sl=(tt + j) % 4, h2=hh, hf2=hf,
                     pt=pT_sb[idx % 3]: e.matmul(
                         bf(3 + hf2)[:, (h2 % 8) * 64:((h2 % 8) + 1) * 64],
                         pt[:, jj * 128:(jj + 1) * 128],
                         v_ring[sl][:, h2 * 64:(h2 + 1) * 64],
                         start=(jj == 0), stop=(jj == 2)),
                     inc=SVM2.bump() if j == 2 else None)
            if so == 7:
                E.wait("vector", SVM2, SVM2.n)
                E.wait("vector", SRR, SRR.n)
                if tt > 0 and hf == 0:
                    waitr("vector", SAOT, tt)
                    wdma("vector", ("dbgao", tt - 1))
                for h2i in range(hf * 8, hf * 8 + 8):
                    E.op("vector", lambda e, hh2=h2i, hf2=hf: e.tensor_scalar(
                        ao_sb[:, hh2 * 64:(hh2 + 1) * 64],
                        bf(3 + hf2)[:, (hh2 % 8) * 64:((hh2 % 8) + 1) * 64],
                        rrb[:, hh2:hh2 + 1], None, ALU.mult),
                        inc=SOEV.bump())

        for h in range(HEADS):
            idx = t * HEADS + h
            plane = h // 2
            prow = (h % 2) * 64
            scb = idx % 2
            if h == 0:
                for w in range(3):
                    wdma("tensor", ("kv", t + w))
                    wdma("tensor", ("vv", t + w))
                wdma("tensor", ("qg", g))
            waitr("tensor", SEXP, idx - 1)
            for j in range(3):
                E.op("tensor", lambda e, jj=j, sl=(t + j) % 4, b=scb, pl=plane,
                     pr=prow, q0=qcol0, qgb=qg, mk=masked: e.matmul(
                         bf(b)[:, jj * 128:(jj + 1) * 128],
                         qgb[pr:pr + 64, pl * 512 + q0: pl * 512 + q0 + 128],
                         k_ring[sl][pr:pr + 64, pl * 128:(pl + 1) * 128],
                         start=(jj == 0), stop=(jj == 2 and not mk)),
                     inc=SSC.bump() if (j == 2 and not masked) else None)
            if masked:
                E.op("tensor", lambda e, b=scb, tt=t: e.matmul(
                    bf(b)[:, 0:WIN], ones_bf[0:1, 0:128],
                    maskrow_bf[0:1, tt * WIN:(tt + 1) * WIN],
                    start=False, stop=True), inc=SSC.bump())
            E.wait("scalar", SSC, SSC.n)
            E.op("scalar", lambda e, b=scb, hh=h, pp=p_sb[idx % 3]:
                 e.activation(pp[:, :], bf(b)[:, 0:WIN], AF.Exp,
                              accum_out=ssum[:, hh:hh + 1]), inc=SEXP.bump())
            E.wait("vector", SEXP, SEXP.n)
            E.op("vector", lambda e, hh=h: e.reciprocal(rrb[:, hh:hh + 1],
                                                        ssum[:, hh:hh + 1]),
                 inc=SRR.bump())
            if h > 0:
                emit_tr_copy(idx - 1)
            if h > 1:
                emit_v(idx - 2)
        emit_tr_copy(t * HEADS + HEADS - 1)
        emit_v(t * HEADS + HEADS - 2)
        emit_v(t * HEADS + HEADS - 1)
        if "s_dbgao" in taps:
            E.wait("scalar", SOEV, SOEV.n)
            dma("scalar", lambda e, tt=t: e.dma_start(
                s_dbgao[tt, :, :], ao_sb[:, :]), ST0, label=("dbgao", t))
        E.wait("tensor", SOEV, SOEV.n)
        waitr("tensor", SAOE, t)
        for c in range(8):
            E.op("tensor", lambda e, cc=c: e.transpose(
                bb(5)[:, ts(cc, 128)], ao_sb[:, ts(cc, 128)], ident_bf[:, :]),
                inc=SAOT.bump() if c == 7 else None)
        E.wait("vector", SAOT, SAOT.n)
        E.op("vector", lambda e: e.tensor_copy(aoT_sb[:, :], bb(5)[:, 0:D]),
             inc=SAOE.bump())
        E.wait("tensor", SAOE, SAOE.n)
        waitr("tensor", SX1, t)
        E.wait("tensor", WCAST, wcast_proj)
        for n in range(2):
            for k in range(8):
                E.op("tensor", lambda e, kk=k, nn=n: e.matmul(
                    bf(6 + nn)[:, :], aoT_sb[:, ts(kk, 128)],
                    wproj_bf[:, kk * 1024 + nn * 512: kk * 1024 + nn * 512 + 512],
                    start=(kk == 0), stop=False))
            E.op("tensor", lambda e, nn=n: e.matmul(
                bf(6 + nn)[:, :], ones_bf[0:1, 0:128],
                projb_bf[0:1, nn * 512:(nn + 1) * 512], start=False, stop=True),
                inc=SPRJ.bump() if n == 1 else None)
        E.wait("vector", SPRJ, SPRJ.n)
        wdma("vector", ("h", t))
        wdma("vector", ("stx1", t - 2))
        waitr("vector", SLNA2["SB"], t - 1)
        E.op("vector", lambda e, tt=t: e.tensor_add(
            x1_sb[tt % 2][:, 0:512], bf(6)[:, :], h2_ring[tt % 2][:, 0:512]))
        E.op("vector", lambda e, tt=t: e.tensor_add(
            x1_sb[tt % 2][:, 512:1024], bf(7)[:, :],
            h2_ring[tt % 2][:, 512:1024]), inc=SX1.bump())
        E.wait("scalar", SX1, SX1.n)
        dma("scalar", lambda e, tt=t: e.dma_start(
            s_x1[tt, :, :], x1_sb[tt % 2][:, :]), STX1s[t % 2],
            label=("stx1", t))
        wdma("vector", ("x2", t))
        wdma("vector", ("stsxa", t - 2))
        E.wait("vector", SX1, SX1.n)
        E.op("vector", lambda e, tt=t: e.scalar_tensor_tensor(
            sxa_t1[:, :], x1_sb[tt % 2][:, :], 1.0, rs(2), ALU.mult, ALU.mult),
            inc=CH2.bump())
        E.wait("vector", CH2, CH2.n)
        E.op("vector", lambda e, tt=t: e.tensor_add(
            sxa_sb[tt % 2][:, :], sxa_t1[:, :], x2_ring[tt % 2][:, :]),
            inc=SSXA.bump())
        E.wait("scalar", SSXA, SSXA.n)
        dma("scalar", lambda e, tt=t: e.dma_start(
            s_sxa[tt, :, :], sxa_sb[tt % 2][:, :]), STSXAs[t % 2],
            label=("stsxa", t))
        E.wait("scalar", SX1, SX1.n)
        E.wait("vector", SX1, SX1.n)
        E.op("vector", lambda e, tt=t: e.reduce_sum(
            s1A2[:, tt:tt + 1], x1_sb[tt % 2][:, :], axis=AX.X),
            inc=SLNA2["SC"].bump())
        waitr("scalar", SLNA2["SB"], t)
        E.op("scalar", lambda e, tt=t: e.activation(
            sq_tr2[:, :], x1_sb[tt % 2][:, :], AF.Square,
            accum_out=vbA2[:, tt:tt + 1]), inc=SLNA2["SB"].bump())

    _ps = stats_ln2.ap[0][0]
    stats_ev = bass.AP(stats_ln2.tensor, stats_ln2.offset, [[_ps, 128], [2, 16]])
    stats_od = bass.AP(stats_ln2.tensor, stats_ln2.offset + 1,
                       [[_ps, 128], [2, 16]])
    E.wait("vector", SLNA2["SB"], SLNA2["SB"].n)
    E.wait("vector", SLNA2["SC"], SLNA2["SC"].n)
    E.op("vector", lambda e: e.tensor_scalar(muA2[:, :], s1A2[:, :], INV_D,
                                             None, ALU.mult), inc=CH2.bump())
    E.wait("vector", CH2, CH2.n)
    E.op("vector", lambda e: e.tensor_mul(t1A2[:, :], muA2[:, :], muA2[:, :]),
         inc=CH2.bump())
    E.wait("vector", CH2, CH2.n)
    E.op("vector", lambda e: e.scalar_tensor_tensor(
        t1A2[:, :], vbA2[:, :], INV_D, t1A2[:, :], ALU.mult, ALU.subtract),
        inc=CH2.bump())
    E.wait("vector", CH2, CH2.n)
    E.op("vector", lambda e: e.tensor_scalar_add(t1A2[:, :], t1A2[:, :], EPS),
         inc=CH2.bump())
    E.wait("scalar", CH2, CH2.n)
    E.op("scalar", lambda e: e.activation(s1A2[:, :], t1A2[:, :], AF.Ln),
         inc=CH2.bump())
    E.wait("scalar", CH2, CH2.n)
    E.op("scalar", lambda e, ev=stats_ev: e.activation(ev, s1A2[:, :], AF.Exp,
                                                       scale=-0.5),
         inc=CH2.bump())
    E.wait("vector", CH2, CH2.n)
    E.op("vector", lambda e, ev=stats_ev: e.tensor_mul(t1A2[:, :], muA2[:, :],
                                                       ev), inc=CH2.bump())
    E.wait("vector", CH2, CH2.n)
    E.op("vector", lambda e, ov=stats_od: e.tensor_scalar(
        ov, t1A2[:, :], -1.0, None, ALU.mult), inc=CH2.bump())

    for s in STX1s + STSXAs:
        E.wait("scalar", s, s.n)
    barrier()
    if nphases == 3:
        return finalize()

    pool_reset()
    A.off = A.base
    wfc1_bf = A.alloc("wfc1_bf", [128, 8 * DFF], BF16)
    wfc2_bf = A.alloc("wfc2_bf", [128, 32 * D], BF16)
    wstM = [A.alloc(f"wstM{i}", [128, 512], F32) for i in range(2)]
    wstM2 = [A.alloc(f"wstM2{i}", [128, 512], F32) for i in range(2)]
    fc1b_fm = A.alloc("fc1b_fm", [128, 32], F32)
    fc2b_bf = A.alloc("fc2b_bf", [1, D], BF16)
    lnw2_repl = A.alloc("lnw2_repl", [128, D], F32)
    lnb2_repl = A.alloc("lnb2_repl", [128, D], F32)
    x1_ring = [A.alloc(f"x1rM{i}", [128, D], F32) for i in range(3)]
    u2_buf = A.alloc("u2_buf", [128, D], F32)
    zbf_buf = A.alloc("zbf_buf", [128, D], BF16)
    zT_gr = [A.alloc(f"zT_g{i}", [128, 8 * 256], BF16) for i in range(2)]
    tT_r = [A.alloc(f"tTr{i}", [128, 256], BF16) for i in range(2)]
    s2_sb = [A.alloc(f"s2sb{i}", [128, D], F32) for i in range(2)]
    s1M = A.alloc("s1M", [128, NT_OWN], F32)
    vbM = A.alloc("vbM", [128, NT_OWN], F32)
    muM = A.alloc("muM", [128, NT_OWN], F32)
    t1M = A.alloc("t1M", [128, NT_OWN], F32)

    LWMs = [mkdma("LWMa"), mkdma("LWMb")]
    LWM2s = [_alloc_sem("LWM2a"), _alloc_sem("LWM2b")]
    for _s in LWM2s:
        _s.base = 0
    LROWM = mkdma("LROWM")
    LFCB = mkdma("LFCB")
    WCM = mksem("WCM")
    WCM2 = mksem("WCM2")
    WCB = mksem("WCB")
    LX1Ms = [mkdma(f"LX1M{i}") for i in range(3)]
    LXMs = [mkdma("LXMa"), mkdma("LXMb")]
    SUZ = mksem("SUZ")
    SZ = mksem("SZ")
    SZT = mksem("SZT")
    SZTE = mksem("SZTE")
    SF1 = mksem("SF1")
    SGE = mksem("SGE")
    SF2C = mksem("SF2C")
    SF2T = mksem("SF2T")
    SS2 = mksem("SS2")
    STS2s = [mkstore("STS2a"), mkstore("STS2b")]
    SLNM = {k: mksem(f"Mc{k}") for k in ("SB", "SC", "SD", "SE")}
    CHM = mksem("CHM")

    E.op("sync", lambda e: e.dma_start(u2_buf[0:1, :], lnrow[0:1, 2048:3072]),
         inc=LROWM.bump(16))
    E.wait("tensor", LROWM, LROWM.n)
    replicate_row(u2_buf[0:1, :], lnw2_repl, 0, 1)
    E.wait("sync", RMM, RMM.n)
    E.op("sync", lambda e: e.dma_start(u2_buf[0:1, :], lnrow[0:1, 3072:4096]),
         inc=LROWM.bump(16))
    E.wait("tensor", LROWM, LROWM.n)
    replicate_row(u2_buf[0:1, :], lnb2_repl, 2, 3)
    repl_m_done = REV.n

    for t in range(2):
        dma("sync", lambda e, tt=t: e.dma_start(
            x1_ring[tt % 3][:, :], s_x1[tt, :, :]), LX1Ms[t % 3],
            label=("x1m", t))
        dma("sync", lambda e, tt=t: e.dma_start(
            s2_sb[tt % 2][:, :], s_sxa[tt, :, :]), LXMs[t % 2],
            label=("xm", t))

    E.op("sync", lambda e: e.dma_start(fc1b_fm[:, :], fc1b_fm_in[:, :]),
         inc=LFCB.bump(16))
    E.wait("sync", RMM, RMM.n)
    E.op("sync", lambda e: e.dma_start(u2_buf[0:1, :], fc2b_row[0:1, :]),
         inc=LROWM.bump(16))
    for i in range(64):
        k1, q1 = i % 8, i // 8
        k2, q2 = i // 2, i % 2
        if i >= 2:
            waitr("sync", WCM, i - 1)
            waitr("gpsimd", WCM2, i - 1)
        E.op("sync", lambda e, kk=k1, qq=q1, ii=i: e.dma_start(
            wstM[ii % 2][:, :], wfc1T[ts(kk, 128), qq * 512:(qq + 1) * 512]),
            inc=LWMs[i % 2].bump(16))
        E.op("gpsimd", lambda e, kk=k2, qq=q2, ii=i: e.dma_start(
            wstM2[ii % 2][:, :], wfc2T[ts(kk, 128), qq * 512:(qq + 1) * 512]),
            inc=LWM2s[i % 2].bump(16))
    E.wait("scalar", LFCB, LFCB.n)
    lwm_n = {0: LWMs[0].base, 1: LWMs[1].base}
    lwm2_n = {0: LWM2s[0].base, 1: LWM2s[1].base}

    def emit_casts(i0, i1):
        for i in range(i0, i1):
            k1, q1 = i % 8, i // 8
            k2, q2 = i // 2, i % 2
            lwm_n[i % 2] += 16
            E.wait("vector", LWMs[i % 2], lwm_n[i % 2])
            E.op("vector", lambda e, kk=k1, qq=q1, ii=i: e.tensor_copy(
                wfc1_bf[:, kk * DFF + qq * 512: kk * DFF + (qq + 1) * 512],
                wstM[ii % 2][:, :]), inc=WCM.bump())
            lwm2_n[i % 2] += 16
            E.wait("vector", LWM2s[i % 2], lwm2_n[i % 2])
            E.op("vector", lambda e, kk=k2, qq=q2, ii=i: e.tensor_copy(
                wfc2_bf[:, kk * 1024 + qq * 512: kk * 1024 + (qq + 1) * 512],
                wstM2[ii % 2][:, :]), inc=WCM2.bump())

    emit_casts(0, 2)
    E.wait("vector", LROWM, LROWM.n)
    E.op("vector", lambda e: e.tensor_copy(fc2b_bf[0:1, :], u2_buf[0:1, :]),
         inc=WCB.bump())

    obA = (4, 5)
    obB = (6, 7)

    def prep_tile(t, slot, lt):
        if t + 2 < NT_OWN:
            waitr("sync", SUZ, t)
            dma("sync", lambda e, tt=t + 2: e.dma_start(
                x1_ring[tt % 3][:, :], s_x1[tt, :, :]), LX1Ms[(t + 2) % 3],
                label=("x1m", t + 2))
        wdma("scalar", ("x1m", t))
        if t == 0:
            E.wait("scalar", REV, repl_m_done)
            waitr("scalar", WCB, 1)
        waitr("scalar", SZ, t)
        waitr("scalar", SS2, 2 * (t // 2) - 2)
        E.op("scalar", lambda e, tt=t: e.activation(
            u2_buf[:, :], x1_ring[tt % 3][:, :], AF.Identity,
            scale=stats_ln2[:, 2 * tt:2 * tt + 1],
            bias=stats_ln2[:, 2 * tt + 1:2 * tt + 2]), inc=SUZ.bump())
        E.wait("vector", SUZ, SUZ.n)
        if t == 0:
            E.wait("vector", REV, repl_m_done)
        waitr("vector", SZT, t)
        E.op("vector", lambda e: e.scalar_tensor_tensor(
            u2_buf[:, :], u2_buf[:, :], 1.0, lnw2_repl, ALU.mult, ALU.mult),
            inc=CHM.bump())
        E.wait("vector", CHM, CHM.n)
        E.op("vector", lambda e: e.scalar_tensor_tensor(
            zbf_buf[:, :], u2_buf[:, :], 1.0, lnb2_repl, ALU.mult, ALU.add),
            inc=SZ.bump())
        E.wait("tensor", SZ, SZ.n)
        waitr("tensor", SZTE, t - 1)
        for c in range(8):
            E.op("tensor", lambda e, cc=c, bk=t % 2: e.transpose(
                bb(bk)[:, ts(cc, 128)], zbf_buf[:, ts(cc, 128)],
                ident_bf[:, :]), inc=SZT.bump() if c == 7 else None)
        E.wait("vector", SZT, SZT.n)
        waitr("vector", SF1, 32 * (t // 2 - 1))
        dstz = zT_gr[slot].rearrange("p (c w) -> p c w", c=8)[
            :, :, lt * 128:(lt + 1) * 128]
        E.op("vector", lambda e, d=dstz, bk=t % 2: e.tensor_copy(
            d, bb(bk).rearrange("p (c w) -> p c w", c=8)), inc=SZTE.bump())

    def emit_fc2(g, m):
        mu = g * 32 + m
        if g == 0:
            waitr("tensor", WCM2, 2 * (m + 1))
        waitr("tensor", SGE, mu + 1)
        for lt in range(2):
            ob = obA if lt == 0 else obB
            for n in range(2):
                E.op("tensor", lambda e, nn=n, l=lt, mm=m, b=ob[n],
                     st=tT_r[mu % 2]: e.matmul(
                         bf(b)[:, :], st[:, l * 128:(l + 1) * 128],
                         wfc2_bf[:, mm * 1024 + nn * 512: mm * 1024 + nn * 512 + 512],
                         start=(mm == 0), stop=False),
                     inc=SF2C.bump() if (lt == 1 and n == 1) else None)

    prep_tile(0, 0, 0)
    prep_tile(1, 0, 1)
    emit_casts(2, 64)
    for g in range(8):
        tiles = (2 * g, 2 * g + 1)
        E.wait("tensor", SZTE, SZTE.n)
        waitr("tensor", SS2, 2 * g)
        for m in range(32):
            mu = g * 32 + m
            fb = 2 + (mu % 2)
            if g == 0:
                waitr("tensor", WCM, 8 * (m // 4) + 8)
            waitr("tensor", SGE, mu - 1)
            for k in range(8):
                E.op("tensor", lambda e, kk=k, mm=m, b=fb, zt=zT_gr[g % 2]:
                     e.matmul(
                         bf(b)[:, 0:256],
                         wfc1_bf[:, kk * DFF + mm * 128: kk * DFF + mm * 128 + 128],
                         zt[:, kk * 256:(kk + 1) * 256],
                         start=(kk == 0), stop=(kk == 7)),
                     inc=SF1.bump() if k == 7 else None)
            E.wait("scalar", SF1, SF1.n)
            waitr("scalar", SF2C, mu - 1)
            E.op("scalar", lambda e, mm=m, b=fb, st=tT_r[mu % 2]: e.activation(
                st[:, :], bf(b)[:, 0:256], AF.Gelu, bias=fc1b_fm[:, mm:mm + 1]),
                inc=SGE.bump())
            if m >= 1:
                emit_fc2(g, m - 1)
        emit_fc2(g, 31)
        for lt, t in enumerate(tiles):
            ob = obA if lt == 0 else obB
            if g == 0 and lt == 0:
                waitr("tensor", WCB, 1)
            for n in range(2):
                E.op("tensor", lambda e, nn=n, b=ob[n]: e.matmul(
                    bf(b)[:, :], ones_bf[0:1, 0:128],
                    fc2b_bf[0:1, nn * 512:(nn + 1) * 512],
                    start=False, stop=True),
                    inc=SF2T.bump() if n == 1 else None)
        if g + 1 < 8:
            prep_tile(2 * g + 2, (g + 1) % 2, 0)
            prep_tile(2 * g + 3, (g + 1) % 2, 1)
        for lt, t in enumerate(tiles):
            ob = obA if lt == 0 else obB
            E.wait("vector", SF2T, SF2T.n)
            wdma("vector", ("xm", t))
            waitr("vector", SLNM["SB"], t)
            E.op("vector", lambda e, b=ob[0]: e.scalar_tensor_tensor(
                u2_buf[:, 0:512], bf(b)[:, :], 1.0, rs(2)[:, 0:512],
                ALU.mult, ALU.mult))
            E.op("vector", lambda e, b=ob[1]: e.scalar_tensor_tensor(
                u2_buf[:, 512:1024], bf(b)[:, :], 1.0, rs(2)[:, 512:1024],
                ALU.mult, ALU.mult), inc=CHM.bump())
            E.wait("vector", CHM, CHM.n)
            E.op("vector", lambda e, tt=t: e.tensor_add(
                s2_sb[tt % 2][:, :], u2_buf[:, :], s2_sb[tt % 2][:, :]),
                inc=SS2.bump())
            E.wait("scalar", SS2, SS2.n)
            dma("scalar", lambda e, tt=t: e.dma_start(
                s_s2[tt, :, :], s2_sb[tt % 2][:, :]), STS2s[t % 2],
                label=("sts2", t))
            E.wait("scalar", SS2, SS2.n)
            E.wait("vector", SS2, SS2.n)
            E.op("vector", lambda e, tt=t: e.reduce_sum(
                s1M[:, tt:tt + 1], s2_sb[tt % 2][:, :], axis=AX.X),
                inc=SLNM["SC"].bump())
            waitr("scalar", SLNM["SB"], t)
            E.op("scalar", lambda e, tt=t: e.activation(
                u2_buf[:, :], s2_sb[tt % 2][:, :], AF.Square,
                accum_out=vbM[:, tt:tt + 1]), inc=SLNM["SB"].bump())
            if t + 2 < NT_OWN:
                wdma("sync", ("sts2", t))
                waitr("sync", SLNM["SB"], t + 1)
                waitr("sync", SLNM["SC"], t + 1)
                dma("sync", lambda e, tt=t + 2: e.dma_start(
                    s2_sb[tt % 2][:, :], s_sxa[tt, :, :]), LXMs[t % 2],
                    label=("xm", t + 2))

    _ps2 = stats_s2.ap[0][0]
    sts2_ev = bass.AP(stats_s2.tensor, stats_s2.offset, [[_ps2, 128], [2, 16]])
    sts2_od = bass.AP(stats_s2.tensor, stats_s2.offset + 1,
                      [[_ps2, 128], [2, 16]])
    E.wait("vector", SLNM["SB"], SLNM["SB"].n)
    E.wait("vector", SLNM["SC"], SLNM["SC"].n)
    E.op("vector", lambda e: e.tensor_scalar(muM[:, :], s1M[:, :], INV_D,
                                             None, ALU.mult), inc=CHM.bump())
    E.wait("vector", CHM, CHM.n)
    E.op("vector", lambda e: e.tensor_mul(t1M[:, :], muM[:, :], muM[:, :]),
         inc=CHM.bump())
    E.wait("vector", CHM, CHM.n)
    E.op("vector", lambda e: e.scalar_tensor_tensor(
        t1M[:, :], vbM[:, :], INV_D, t1M[:, :], ALU.mult, ALU.subtract),
        inc=CHM.bump())
    E.wait("vector", CHM, CHM.n)
    E.op("vector", lambda e: e.tensor_scalar_add(t1M[:, :], t1M[:, :], EPS),
         inc=CHM.bump())
    E.wait("scalar", CHM, CHM.n)
    E.op("scalar", lambda e: e.activation(s1M[:, :], t1M[:, :], AF.Ln),
         inc=CHM.bump())
    E.wait("scalar", CHM, CHM.n)
    E.op("scalar", lambda e, ev=sts2_ev: e.activation(ev, s1M[:, :], AF.Exp,
                                                      scale=-0.5),
         inc=CHM.bump())
    E.wait("vector", CHM, CHM.n)
    E.op("vector", lambda e, ev=sts2_ev: e.tensor_mul(t1M[:, :], muM[:, :],
                                                      ev), inc=CHM.bump())
    E.wait("vector", CHM, CHM.n)
    E.op("vector", lambda e, ov=sts2_od: e.tensor_scalar(
        ov, t1M[:, :], -1.0, None, ALU.mult), inc=CHM.bump())

    for s in STS2s:
        E.wait("scalar", s, s.n)
    barrier()
    if nphases == 4:
        return finalize()

    pool_reset()
    A.off = A.base
    gates = A.alloc("gates", [128, NEXP * NT_OWN], F32)
    gatesT_bf = A.alloc("gatesT_bf", [8, NT_OWN * TOK], BF16)
    r2_all = A.alloc("r2_all", [128, NT_OWN], F32)
    ck5 = A.off
    s2E_ring = [A.alloc(f"s2E{i}", [128, D], F32) for i in range(2)]
    u3_buf = A.alloc("u3_buf", [128, D], F32)
    h1E_buf = A.alloc("h1E_buf", [128, D], F32)
    h2bf_buf = A.alloc("h2bf_buf", [128, D], BF16)
    h2st = [A.alloc(f"h2st{i}", [128, D], BF16) for i in range(2)]
    rowstE = A.alloc("rowstE", [1, D], F32)
    cos_all = A.alloc("cos_all", [128, NEXP * NT_OWN], F32)
    route_all = A.alloc("route_all", [128, NEXP * NT_OWN], F32)
    u_all = A.alloc("u_all", [128, NEXP * NT_OWN], F32)
    gtub_all = A.alloc("gtub_all", [128, NEXP * NT_OWN], F32)
    gmask_all = A.alloc("gmask_all", [128, NEXP * NT_OWN], F32)
    gE_all = A.alloc("gE_all", [128, NEXP * NT_OWN], F32)
    g16 = {k: A.alloc(f"g16_{k}", [128, NT_OWN], F32) for k in
           ("m1", "m2", "rsum", "rinv", "r2s")}

    def bc16(a):
        return bass.AP(a.tensor, a.offset, [[a.ap[0][0], 128], [1, 16], [0, 8]])

    def c16(a):
        return bass.AP(a.tensor, a.offset, [[a.ap[0][0], 128], [1, 16], [1, 1]])

    def g3(a):
        return a.rearrange("p (t e) -> p t e", t=16)
    assert A.off <= ck5 + 33 * 1024, f"5a scratch too big: {A.off - ck5}"
    A.off = ck5 + 33 * 1024
    w1_bf = A.alloc("w1_bf", [128, NEXP * 8 * DEXP], BF16)
    w2_bf = A.alloc("w2_bf", [128, NEXP * 4 * D], BF16)
    wstE = [A.alloc(f"wstE{i}", [128, 512], F32) for i in range(2)]
    wstE2 = [A.alloc(f"wstE2{i}", [128, 512], F32) for i in range(2)]
    b1_bf = A.alloc("b1_bf", [1, NEXP * DEXP], BF16)

    LWEs = [mkdma("LWEa"), mkdma("LWEb")]
    LWE2s = [_alloc_sem("LWE2a"), _alloc_sem("LWE2b")]
    for _s in LWE2s:
        _s.base = 0
    LROWE = mkdma("LROWE")
    WCE = mksem("WCE")
    WCE2 = mksem("WCE2")
    LS2s = [mkdma("LS2a"), mkdma("LS2b")]
    SU3 = mksem("SU3")
    SH2 = mksem("SH2")
    SH2T = mksem("SH2T")
    SH2E = mksem("SH2E")
    STH2s = [mkstore("STH2a"), mkstore("STH2b")]
    SLG = mksem("SLG")
    SLGE = mksem("SLGE")
    SGA = mksem("SGA")
    SGDONE = mksem("SGDONE")
    SGT = mksem("SGT")
    SR2 = mksem("SR2")
    CHE = mksem("CHE")

    E.op("sync", lambda e: e.dma_start(rowstE[0:1, :], s_modrow[0:1, 3072:4096]),
         inc=LROWE.bump(16))
    E.wait("tensor", LROWE, LROWE.n)
    replicate_row(rowstE[0:1, :], rs(0), 0, 1)
    E.wait("sync", RMM, RMM.n)
    E.op("sync", lambda e: e.dma_start(rowstE[0:1, :], s_modrow[0:1, 4096:5120]),
         inc=LROWE.bump(16))
    E.wait("tensor", LROWE, LROWE.n)
    replicate_row(rowstE[0:1, :], rs(1), 2, 3)
    E.wait("sync", RMM, RMM.n)
    E.op("sync", lambda e: e.dma_start(rowstE[0:1, :], s_modrow[0:1, 5120:6144]),
         inc=LROWE.bump(16))
    E.wait("tensor", LROWE, LROWE.n)
    replicate_row(rowstE[0:1, :], rs(2), 4, 5)
    repl_e_done = REV.n

    lw1n = {0: LWEs[0].base, 1: LWEs[1].base}
    lw2n = {0: 0, 1: 0}

    def emit_w1_loads(i0, i1):
        for i in range(i0, i1):
            e_, k = divmod(i, 8)
            w1e = w1_in[e_].rearrange("(kc p) n -> kc p n", p=128)
            if i >= 2:
                waitr("sync", WCE, i - 1)
            E.op("sync", lambda e, s=w1e, kk=k, ii=i: e.dma_start(
                wstE[ii % 2][:, :], s[kk]), inc=LWEs[i % 2].bump(16))

    def emit_w1_casts(i0, i1):
        for i in range(i0, i1):
            e_, k = divmod(i, 8)
            lw1n[i % 2] += 16
            E.wait("vector", LWEs[i % 2], lw1n[i % 2])
            col = (e_ * 8 + k) * 512
            E.op("vector", lambda e, cc=col, ii=i: e.tensor_copy(
                w1_bf[:, cc:cc + 512], wstE[ii % 2][:, :]), inc=WCE.bump())

    def emit_w2_loads(i0, i1):
        for i in range(i0, i1):
            e_, k = divmod(i, 8)
            w2e = w2_in[e_].rearrange("(kc p) n -> kc p n", p=128)
            src = w2e[k // 2][:, (k % 2) * 512:(k % 2) * 512 + 512]
            if i >= 2:
                waitr("gpsimd", WCE2, i - 1)
            E.op("gpsimd", lambda e, s=src, ii=i: e.dma_start(
                wstE2[ii % 2][:, :], s), inc=LWE2s[i % 2].bump(16))

    def emit_w2_casts(i0, i1):
        for i in range(i0, i1):
            e_, k = divmod(i, 8)
            lw2n[i % 2] += 16
            E.wait("scalar", LWE2s[i % 2], lw2n[i % 2])
            col = e_ * 4096 + (k // 2) * 1024 + (k % 2) * 512
            E.op("scalar", lambda e, cc=col, ii=i: e.activation(
                w2_bf[:, cc:cc + 512], wstE2[ii % 2][:, :], AF.Copy),
                inc=WCE2.bump())

    emit_w2_loads(0, 64)

    for t in range(2):
        dma("sync", lambda e, tt=t: e.dma_start(
            s2E_ring[tt % 2][:, :], s_s2[tt, :, :]), LS2s[t % 2],
            label=("s2e", t))
    emit_w1_loads(0, 2)
    emit_w1_casts(0, 2)
    emit_w2_casts(0, 2)

    for t in range(NT_OWN):
        if t + 2 < NT_OWN:
            waitr("sync", SU3, t + 1)
            dma("sync", lambda e, tt=t + 2: e.dma_start(
                s2E_ring[tt % 2][:, :], s_s2[tt, :, :]), LS2s[t % 2],
                label=("s2e", t + 2))
        emit_w1_loads(2 + 4 * t, min(2 + 4 * (t + 1), 64))
        wdma("scalar", ("s2e", t))
        waitr("scalar", SR2, t)
        E.op("scalar", lambda e, tt=t: e.activation(
            u3_buf[:, :], s2E_ring[tt % 2][:, :], AF.Identity,
            scale=stats_s2[:, 2 * tt:2 * tt + 1],
            bias=stats_s2[:, 2 * tt + 1:2 * tt + 2]), inc=SU3.bump())
        E.wait("vector", SU3, SU3.n)
        if t == 0:
            E.wait("vector", REV, repl_e_done)
        waitr("vector", SR2, t)
        waitr("vector", SH2T, t)
        E.op("vector", lambda e: e.scalar_tensor_tensor(
            h1E_buf[:, :], u3_buf[:, :], 1.0, rs(1), ALU.mult, ALU.mult),
            inc=CHE.bump())
        E.wait("vector", CHE, CHE.n)
        E.op("vector", lambda e: e.scalar_tensor_tensor(
            h2bf_buf[:, :], h1E_buf[:, :], 1.0, rs(0), ALU.mult, ALU.add),
            inc=SH2.bump())
        E.wait("scalar", SH2, SH2.n)
        E.op("scalar", lambda e, tt=t: e.activation(
            u3_buf[:, :], h2bf_buf[:, :], AF.Square,
            accum_out=r2_all[:, tt:tt + 1]), inc=SR2.bump())
        E.wait("tensor", SH2, SH2.n)
        waitr("tensor", SH2E, t - 1)
        for c in range(8):
            E.op("tensor", lambda e, cc=c, bk=t % 2: e.transpose(
                bb(bk)[:, ts(cc, 128)], h2bf_buf[:, ts(cc, 128)],
                ident_bf[:, :]), inc=SH2T.bump() if c == 7 else None)
        E.wait("vector", SH2T, SH2T.n)
        wdma("vector", ("sth2", t - 2))
        waitr("vector", SLG, t - 1)
        E.op("vector", lambda e, bk=t % 2, st=h2st[t % 2]: e.tensor_copy(
            st[:, :], bb(bk)[:, 0:D]), inc=SH2E.bump())
        E.wait("scalar", SH2E, SH2E.n)
        dma("scalar", lambda e, tt=t, st=h2st[t % 2]: e.dma_start(
            s_h2T[tt, :, :], st[:, :]), STH2s[t % 2], label=("sth2", t))
        E.wait("tensor", SH2E, SH2E.n)
        waitr("tensor", SLGE, t)
        for k in range(8):
            E.op("tensor", lambda e, kk=k, st=h2st[t % 2]: e.matmul(
                bf(2)[:, 0:8], st[:, ts(kk, 128)], cnT_bf[:, kk * 8:(kk + 1) * 8],
                start=(kk == 0), stop=(kk == 7)))
        for k in range(8):
            E.op("tensor", lambda e, kk=k, st=h2st[t % 2]: e.matmul(
                bf(2)[:, 256:264], st[:, ts(kk, 128)],
                wrT_bf[:, kk * 8:(kk + 1) * 8], start=(kk == 0), stop=(kk == 7)),
                inc=SLG.bump() if k == 7 else None)
        E.wait("vector", SLG, SLG.n)
        E.op("vector", lambda e, tt=t: e.tensor_copy(
            cos_all[:, tt * 8:(tt + 1) * 8], bf(2)[:, 0:8]))
        E.op("vector", lambda e, tt=t: e.tensor_copy(
            route_all[:, tt * 8:(tt + 1) * 8], bf(2)[:, 256:264]),
            inc=SLGE.bump())
        emit_w1_casts(2 + 4 * t, min(2 + 4 * (t + 1), 64))
        emit_w2_casts(2 + 4 * t, min(2 + 4 * (t + 1), 64))

    E.wait("sync", RMM, RMM.n)
    for q in range(4):
        if q > 0:
            E.wait("sync", WCE, WCE.n)
        E.op("sync", lambda e, qq=q: e.dma_start(
            rowstE[0:1, :], b1_row[0:1, qq * 1024:(qq + 1) * 1024]),
            inc=LROWE.bump(16))
        E.wait("vector", LROWE, LROWE.n)
        E.op("vector", lambda e, qq=q: e.tensor_copy(
            b1_bf[0:1, qq * 1024:(qq + 1) * 1024], rowstE[0:1, :]),
            inc=WCE.bump())
    wce_total = WCE.n
    wce2_total = WCE2.n

    def rtg(eng="vector"):
        E.wait(eng, SGA, SGA.n)

    E.wait("scalar", SR2, SR2.n)
    E.op("scalar", lambda e: e.activation(g16["m1"], r2_all[:, 0:NT_OWN],
                                          AF.Ln), inc=SGA.bump())
    rtg("scalar")
    E.op("scalar", lambda e: e.activation(g16["r2s"], g16["m1"], AF.Exp,
                                          scale=0.5), inc=SGA.bump())
    rtg()
    E.op("vector", lambda e: e.tensor_scalar_max(g16["r2s"], g16["r2s"],
                                                 1e-12), inc=SGA.bump())
    rtg()
    E.op("vector", lambda e: e.reciprocal(g16["rinv"], g16["r2s"]),
         inc=SGA.bump())
    rtg()
    E.wait("vector", SLGE, SLGE.n)
    E.op("vector", lambda e: e.tensor_tensor(
        g3(cos_all), g3(cos_all), bc16(g16["rinv"]), ALU.mult),
        inc=SGA.bump())
    rtg()
    E.op("vector", lambda e: e.tensor_add(cos_all[:, :], cos_all[:, :],
                                          route_all[:, :]), inc=SGA.bump())
    rtg()
    E.op("vector", lambda e: e.reduce_max(c16(g16["m1"]), g3(cos_all),
                                          axis=AX.X), inc=SGA.bump())
    rtg()
    E.op("vector", lambda e: e.tensor_tensor(
        g3(u_all), g3(cos_all), bc16(g16["m1"]), ALU.subtract),
        inc=SGA.bump())
    E.op("vector", lambda e: e.tensor_tensor(
        g3(gmask_all), g3(cos_all), bc16(g16["m1"]), ALU.is_ge),
        inc=SGA.bump())
    rtg("scalar")
    E.op("scalar", lambda e: e.activation(gE_all[:, :], u_all[:, :], AF.Exp),
         inc=SGA.bump())
    rtg()
    E.op("vector", lambda e: e.scalar_tensor_tensor(
        gtub_all[:, :], gmask_all[:, :], -1e30, cos_all[:, :],
        ALU.mult, ALU.add), inc=SGA.bump())
    rtg()
    E.op("vector", lambda e: e.reduce_max(c16(g16["m2"]), g3(gtub_all),
                                          axis=AX.X), inc=SGA.bump())
    rtg()
    E.op("vector", lambda e: e.tensor_tensor(
        g3(gmask_all), g3(cos_all), bc16(g16["m2"]), ALU.is_ge),
        inc=SGA.bump())
    rtg()
    E.op("vector", lambda e: e.tensor_mul(gE_all[:, :], gE_all[:, :],
                                          gmask_all[:, :]), inc=SGA.bump())
    rtg()
    E.op("vector", lambda e: e.reduce_sum(c16(g16["rsum"]), g3(gE_all),
                                          axis=AX.X), inc=SGA.bump())
    rtg()
    E.op("vector", lambda e: e.reciprocal(g16["rinv"], g16["rsum"]),
         inc=SGA.bump())
    rtg()
    E.op("vector", lambda e: e.tensor_tensor(
        g3(gates[:, 0:NEXP * NT_OWN]), g3(gE_all), bc16(g16["rinv"]),
        ALU.mult), inc=SGDONE.bump())
    E.wait("tensor", SGDONE, SGDONE.n)
    for b in range(4):
        waitr("tensor", SGT, 2 * b)
        for ti in range(4):
            E.op("tensor", lambda e, tt=b * 4 + ti, cc=ti: e.transpose(
                bf(3)[0:8, cc * 128:(cc + 1) * 128],
                gates[:, tt * 8:(tt + 1) * 8], ident32[:, :]),
                inc=SGT.bump() if ti == 3 else None)
        E.wait("scalar", SGT, SGT.n)
        E.op("scalar", lambda e, bb_=b: e.activation(
            gatesT_bf[0:8, bb_ * 512:(bb_ + 1) * 512], bf(3)[0:8, 0:512],
            AF.Copy), inc=SGT.bump())

    for s in STH2s:
        E.wait("scalar", s, s.n)
    barrier(clear=False)
    if nphases == 5:
        return finalize()

    A.off = ck5
    h2T_ring = [A.alloc(f"h2Tr{i}", [128, D], BF16) for i in range(2)]
    s2b_ring = [A.alloc(f"s2b{i}", [128, D], F32) for i in range(2)]
    tE_f = A.alloc("tE_f", [128, DEXP], F32)
    tEg_bf = [A.alloc(f"tEg{i}", [128, DEXP], BF16) for i in range(2)]
    tT_bf = [A.alloc(f"tTbf{i}", [128, DEXP], BF16) for i in range(3)]
    yo_t1 = A.alloc("yo_t1", [128, D], F32)
    yo_sb = [A.alloc(f"yo{i}", [128, D], F32) for i in range(2)]
    assert A.off <= ck5 + 33 * 1024, f"5b scratch too big: {A.off - ck5}"

    LH2Bs = [mkdma("LH2Ba"), mkdma("LH2Bb")]
    LS2Bs = [mkdma("LS2Ba"), mkdma("LS2Bb")]
    STE = mksem("STE")
    SGEL = mksem("SGEL")
    SSCL = mksem("SSCL")
    STT = mksem("STT")
    STTE = mksem("STTE")
    SOUT = mksem("SOUT")
    SYEV = mksem("SYEV")
    STYs = [mkstore("STYa"), mkstore("STYb")]

    for t in range(2):
        dma("sync", lambda e, tt=t: e.dma_start(
            h2T_ring[tt % 2][:, :], s_h2T[tt, :, :]), LH2Bs[t % 2],
            label=("h2b", t))
        dma("sync", lambda e, tt=t: e.dma_start(
            s2b_ring[tt % 2][:, :], s_s2[tt, :, :]), LS2Bs[t % 2],
            label=("s2b", t))

    E.wait("tensor", WCE, wce_total)
    E.wait("tensor", WCE2, wce2_total)
    for t in range(NT_OWN):
        if t + 2 < NT_OWN:
            waitr("sync", STE, NEXP * (t + 1))
            waitr("sync", SYEV, t + 1)
            dma("sync", lambda e, tt=t + 2: e.dma_start(
                h2T_ring[tt % 2][:, :], s_h2T[tt, :, :]), LH2Bs[t % 2],
                label=("h2b", t + 2))
            dma("sync", lambda e, tt=t + 2: e.dma_start(
                s2b_ring[tt % 2][:, :], s_s2[tt, :, :]), LS2Bs[t % 2],
                label=("s2b", t + 2))
        obks = (4, 5) if (t % 2 == 0) else (6, 7)

        def tail_trc(idx):
            waitr("tensor", SSCL, idx + 1)
            waitr("tensor", STTE, idx - 1)
            for c in range(4):
                E.op("tensor", lambda e, cc=c, ii=idx: e.transpose(
                    bb(2 + ii % 2)[:, ts(cc, 128)],
                    tEg_bf[ii % 2][:, ts(cc, 128)], ident_bf[:, :]),
                    inc=STT.bump() if c == 3 else None)
            E.wait("scalar", STT, STT.n)
            E.op("scalar", lambda e, ii=idx: e.activation(
                tT_bf[ii % 3][:, :], bb(2 + ii % 2)[:, 0:512], AF.Copy),
                inc=STTE.bump())

        def tail_fc2(idx):
            ee = idx % NEXP
            waitr("tensor", STTE, idx + 1)
            if ee == 0:
                waitr("tensor", SYEV, t - 1)
            for n in range(2):
                for k in range(4):
                    E.op("tensor", lambda e, kk=k, nn=n, ee2=ee, ii=idx,
                         b=obks[n]: e.matmul(
                             bf(b)[:, :], tT_bf[ii % 3][:, ts(kk, 128)],
                             w2_bf[:, (ee2 * 4 + kk) * 1024 + nn * 512:
                                   (ee2 * 4 + kk) * 1024 + nn * 512 + 512],
                             start=(ee2 == 0 and kk == 0), stop=False))

        for e_ in range(NEXP):
            idx = t * NEXP + e_
            bk = idx % 2
            if e_ == 0:
                wdma("tensor", ("h2b", t))
            waitr("tensor", SGEL, idx - 1)
            for k in range(8):
                E.op("tensor", lambda e, kk=k, ee=e_, b=bk, ht=h2T_ring[t % 2]:
                     e.matmul(bf(b)[:, :], ht[:, ts(kk, 128)],
                              w1_bf[:, (ee * 8 + kk) * 512:(ee * 8 + kk) * 512 + 512],
                              start=(kk == 0), stop=False))
            E.op("tensor", lambda e, ee=e_, b=bk: e.matmul(
                bf(b)[:, :], ones_bf[0:1, 0:128],
                b1_bf[0:1, ee * 512:(ee + 1) * 512], start=False, stop=True),
                inc=STE.bump())
            E.wait("scalar", STE, STE.n)
            waitr("scalar", SSCL, idx)
            E.op("scalar", lambda e, b=bk: e.activation(
                tE_f[:, :], bf(b)[:, :], AF.Gelu), inc=SGEL.bump())
            E.wait("vector", SGEL, SGEL.n)
            waitr("vector", STT, idx - 1)
            E.op("vector", lambda e, tt=t, ee=e_, ii=idx: e.tensor_scalar(
                tEg_bf[ii % 2][:, :], tE_f[:, :],
                gates[:, tt * 8 + ee: tt * 8 + ee + 1], None, ALU.mult),
                inc=SSCL.bump())
            if e_ > 0:
                tail_trc(idx - 1)
            if e_ > 1:
                tail_fc2(idx - 2)
        tail_trc(t * NEXP + NEXP - 1)
        tail_fc2(t * NEXP + NEXP - 2)
        tail_fc2(t * NEXP + NEXP - 1)
        for n in range(2):
            E.op("tensor", lambda e, nn=n, tt=t, b=obks[n]: e.matmul(
                bf(b)[:, :], gatesT_bf[0:8, tt * 128:(tt + 1) * 128],
                b2_bf[0:8, nn * 512:(nn + 1) * 512], start=False, stop=True),
                inc=SOUT.bump() if n == 1 else None)
        E.wait("vector", SOUT, SOUT.n)
        wdma("vector", ("s2b", t))
        wdma("vector", ("sty", t - 2))
        E.op("vector", lambda e, b=obks[0]: e.scalar_tensor_tensor(
            yo_t1[:, 0:512], bf(b)[:, :], 1.0, rs(2)[:, 0:512],
            ALU.mult, ALU.mult), inc=CHE.bump())
        E.op("vector", lambda e, b=obks[1]: e.scalar_tensor_tensor(
            yo_t1[:, 512:1024], bf(b)[:, :], 1.0, rs(2)[:, 512:1024],
            ALU.mult, ALU.mult), inc=CHE.bump())
        E.wait("vector", CHE, CHE.n)
        E.op("vector", lambda e, tt=t: e.tensor_add(
            yo_sb[tt % 2][:, :], yo_t1[:, :], s2b_ring[tt % 2][:, :]),
            inc=SYEV.bump())
        E.wait("scalar", SYEV, SYEV.n)
        dma("scalar", lambda e, tt=t: e.dma_start(
            y_out[tt, :, :], yo_sb[tt % 2][:, :]), STYs[t % 2],
            label=("sty", t))

    for s in STYs:
        E.wait("scalar", s, s.n)
    return finalize()


def _stage_inputs(inputs):
    f32 = np.float32
    x = np.asarray(inputs["L0_attn_x"], f32)
    c = np.asarray(inputs["L0_c"], f32)

    shared = {
        "wqkvT": np.ascontiguousarray(np.asarray(inputs["L0_attn_qkv_w"], f32).T),
        "wprojT": np.ascontiguousarray(np.asarray(inputs["L0_attn_proj_w"], f32).T),
        "wfc1T": np.ascontiguousarray(np.asarray(inputs["L0_attn_fc1_w"], f32).T),
        "wfc2T": np.ascontiguousarray(np.asarray(inputs["L0_attn_fc2_w"], f32).T),
        "w1_in": np.ascontiguousarray(np.asarray(inputs["L0_ffn_w1"], f32)),
        "w2_in": np.ascontiguousarray(np.asarray(inputs["L0_ffn_w2"], f32)),
        "wmod": np.ascontiguousarray(np.asarray(inputs["L0_wmod"], f32)),
        "wrouteT": np.ascontiguousarray(
            np.asarray(inputs["L0_ffn_w_route"], f32).T),
        "cent_in": np.ascontiguousarray(
            np.asarray(inputs["L0_ffn_centroids"], f32)),
        "b2_in": np.ascontiguousarray(np.asarray(inputs["L0_ffn_b2"], f32)),
        "qkvb_row": np.asarray(inputs["L0_attn_qkv_b"], f32)[None, :].copy(),
        "projb_row": np.asarray(inputs["L0_attn_proj_b"], f32)[None, :].copy(),
        "fc1b_fm_in": np.ascontiguousarray(
            np.asarray(inputs["L0_attn_fc1_b"], f32).reshape(32, 128).T),
        "fc2b_row": np.asarray(inputs["L0_attn_fc2_b"], f32)[None, :].copy(),
        "b1_row": np.asarray(inputs["L0_ffn_b1"], f32).reshape(1, -1).copy(),
        "lnrow": np.concatenate([
            np.asarray(inputs["L0_attn_ln1_w"], f32),
            np.asarray(inputs["L0_attn_ln1_b"], f32),
            np.asarray(inputs["L0_attn_ln2_w"], f32),
            np.asarray(inputs["L0_attn_ln2_b"], f32)])[None, :].copy(),
        "ident_bf_in": np.eye(128, dtype=ml_dtypes.bfloat16),
        "ident32_in": np.eye(128, dtype=f32),
    }

    in_maps = []
    for core in range(8):
        b = core // 2
        half = core % 2
        s0 = half * 2048
        ext = np.zeros((NT_EXT * TOK, D), f32)
        ext[128:128 + 2048] = x[b, s0:s0 + 2048]
        if half == 0:
            ext[128 + 2048:] = x[b, 2048:2048 + 128]
        else:
            ext[0:128] = x[b, 1920:2048]
        mask = np.zeros((1, NT_OWN * WIN), f32)
        if half == 0:
            mask[0, 0:128] = -1e30
        else:
            mask[0, 15 * WIN + 256:16 * WIN] = -1e30
        m = dict(shared)
        m["x_ext"] = ext
        m["maskrow"] = mask
        m["c_row8"] = np.ascontiguousarray(c[b].reshape(8, 128))
        in_maps.append(m)
    return in_maps


_prog_cache = {}


def _get_program(taps=()):
    key = tuple(taps)
    if key not in _prog_cache:
        _prog_cache[key] = build_program(taps)
    return _prog_cache[key]


def kernel(**inputs):
    in_maps = _stage_inputs(inputs)
    nc = _get_program()
    res = run_bass_kernel_spmd(nc, in_maps, core_ids=list(range(8)))
    out = np.zeros((4, 4096, 1024), np.float32)
    for core in range(8):
        b = core // 2
        half = core % 2
        y = np.asarray(res.results[core]["y"]).reshape(2048, 1024)
        out[b, half * 2048:(half + 1) * 2048] = y
    return out


# version 10 (final)
# speedup vs baseline: 1.0248x; 1.0226x over previous
import numpy as np
import ml_dtypes

import concourse.bass as bass
import concourse.mybir as mybir
from concourse.bass_utils import run_bass_kernel_spmd

F32 = mybir.dt.float32
BF16 = mybir.dt.bfloat16
AF = mybir.ActivationFunctionType
ALU = mybir.AluOpType
AX = mybir.AxisListType
ts = bass.ts

D = 1024
TOK = 128
NT_OWN = 16
NT_EXT = 18
HEADS = 16
WIN = 384
DFF = 4096
NEXP = 8
DEXP = 512
EPS = 1e-5
INV_D = 1.0 / D
QSCALE = 0.125

ENGS = ("sync", "tensor", "vector", "scalar", "gpsimd")
DBG_SKIP = set()


class Sem:
    def __init__(self, h):
        self.h = h
        self.n = 0
        self.base = 0

    def bump(self, k=1):
        self.n += k
        return (self, k)


class Emitter:
    def __init__(self):
        self.streams = {e: [] for e in ENGS}

    def on(self, eng, fn):
        self.streams[eng].append(fn)

    def wait(self, eng, sem, val):
        if val > 0:
            self.on(eng, lambda e, s=sem.h, v=val: e.wait_ge(s, v))

    def op(self, eng, fn, inc=None):
        if inc is None:
            self.on(eng, fn)
        else:
            sem, k = inc
            self.on(eng, lambda e, f=fn, s=sem.h, kk=k: f(e).then_inc(s, kk))


class Arena:

    def __init__(self, nc, base, top):
        b0 = (base + 31) & ~31
        nbytes = ((top - b0) // 32) * 32
        big = nc.alloc_sbuf_tensor_at("arena", [128, nbytes], mybir.dt.uint8,
                                      offset=b0)
        self.views = {F32: big.bitcast(F32).ap(), BF16: big.bitcast(BF16).ap()}
        self.base = 0
        self.top = nbytes
        self.off = 0

    def alloc(self, name, shape, dtype):
        esz = 2 if dtype == BF16 else 4
        elems = int(np.prod(shape[1:]))
        off = (self.off + 31) & ~31
        assert off + elems * esz <= self.top, (
            f"SBUF arena overflow: {name} {elems * esz}B at {off}")
        c0 = off // esz
        self.off = off + elems * esz
        return self.views[dtype][0:shape[0], c0:c0 + elems]


def build_program(taps=(), nphases=6, a2_tiles=NT_OWN, a2_nopf=99):
    nc = bass.Bass("TRN2", target_bir_lowering=False, detect_race_conditions=False)

    def din(name, shape, dtype=F32):
        return nc.dram_tensor(name, list(shape), dtype, kind="ExternalInput").ap()

    x_ext = din("x_ext", [NT_EXT * TOK, D])
    wqkvT = din("wqkvT", [D, 3 * D])
    wprojT = din("wprojT", [D, D])
    wfc1T = din("wfc1T", [D, DFF])
    wfc2T = din("wfc2T", [DFF, D])
    w1_in = din("w1_in", [NEXP, D, DEXP])
    w2_in = din("w2_in", [NEXP, DEXP, D])
    wmod = din("wmod", [D, 6 * D])
    wrouteT = din("wrouteT", [D, NEXP])
    cent_in = din("cent_in", [NEXP, D])
    b2_in = din("b2_in", [NEXP, D])
    qkvb_row = din("qkvb_row", [1, 3 * D])
    projb_row = din("projb_row", [1, D])
    fc1b_fm_in = din("fc1b_fm_in", [128, 32])
    fc2b_row = din("fc2b_row", [1, D])
    b1_row = din("b1_row", [1, NEXP * DEXP])
    lnrow = din("lnrow", [1, 4 * D])
    c_row8 = din("c_row8", [8, 128])
    maskrow = din("maskrow", [1, NT_OWN * WIN])
    ident_bf_in = din("ident_bf_in", [128, 128], BF16)
    ident32_in = din("ident32_in", [128, 128])

    def scr(name, shape, dtype=F32):
        kind = "ExternalOutput" if name in taps else "Internal"
        return nc.dram_tensor(name, list(shape), dtype, kind=kind).ap()

    s_h = scr("s_h", [NT_OWN, TOK, D])
    s_x1 = scr("s_x1", [NT_OWN, TOK, D])
    s_sxa = scr("s_sxa", [NT_OWN, TOK, D])
    s_s2 = scr("s_s2", [NT_OWN, TOK, D])
    s_modrow = scr("s_modrow", [1, 6 * D])
    s_h2T = scr("s_h2T", [NT_OWN, TOK, D], BF16)
    s_dbg = scr("s_dbg", [128, 4 * D])
    s_dbg2 = scr("s_dbg2", [NT_EXT, 128, 8])
    s_dbgao = scr("s_dbgao", [NT_OWN, 128, D], BF16)
    q_scr = scr("q_scr", [8, 128, NT_EXT * TOK], BF16)
    k_scr = scr("k_scr", [8, 128, NT_EXT * TOK], BF16)
    v_scr = scr("v_scr", [NT_EXT, 128, D], BF16)
    y_out = nc.dram_tensor("y", [NT_OWN, TOK, D], F32, kind="ExternalOutput").ap()

    def sb(name, shape, dtype):
        return nc.alloc_sbuf_tensor(name, list(shape), dtype).ap()

    ones_bf = sb("ones_bf", [1, 512], BF16)
    ones32 = sb("ones32", [1, 128], F32)
    ident_bf = sb("ident_bf", [128, 128], BF16)
    ident32 = sb("ident32", [128, 128], F32)
    b2_bf = sb("b2_bf", [NEXP, D], BF16)
    cnT_bf = sb("cnT_bf", [128, 64], BF16)
    wrT_bf = sb("wrT_bf", [128, 64], BF16)
    repl = sb("repl", [128, 4 * D], F32)
    stats_ln2 = sb("stats_ln2", [128, 2 * NT_OWN], F32)
    stats_s2 = sb("stats_s2", [128, 2 * NT_OWN], F32)

    def rs(i):
        return repl[:, i * D:(i + 1) * D]

    A = Arena(nc, nc.sbuf_base, nc.sbuf_top)
    banks = [nc.alloc_psum_tensor(f"bank{i}", [128, 512], F32) for i in range(8)]

    def bf(i):
        return banks[i].ap()

    def bb(i):
        return banks[i].bitcast(BF16).ap()

    E = Emitter()
    sem_stack = []
    sem_pool = []
    store_pool = []
    pool_state = {"i": 0, "j": 0}

    def _alloc_sem(name):
        cm = nc.semaphore(name)
        h = cm.__enter__()
        sem_stack.append(cm)
        return Sem(h)

    def mksem(name):
        i = pool_state["i"]
        if i >= len(sem_pool):
            sem_pool.append(_alloc_sem(f"pool{i}"))
        pool_state["i"] = i + 1
        return sem_pool[i]

    def mkstore(name):
        j = pool_state["j"]
        if j >= len(store_pool):
            store_pool.append(_alloc_sem(f"spool{j}"))
        pool_state["j"] = j + 1
        return store_pool[j]

    hw_pool = []

    def mkdma(name):
        k = pool_state.get("k", 0)
        if k >= len(hw_pool):
            hw_pool.append(_alloc_sem(f"hpool{k}"))
        pool_state["k"] = k + 1
        return hw_pool[k]

    def pool_reset():
        pool_state["i"] = 0
        pool_state["j"] = 0
        pool_state["k"] = 0
        for s in sem_pool + store_pool + hw_pool:
            s.base = s.n

    def waitr(eng, sem, rel):
        if rel > 0:
            E.wait(eng, sem, sem.base + rel)

    dv = {}

    def dma(eng, fn, sem, label=None):
        E.op(eng, fn, inc=sem.bump(16))
        if label is not None:
            dv[label] = (sem, sem.n)

    def wdma(eng, label):
        sv = dv.get(label)
        if sv is not None:
            E.wait(eng, sv[0], sv[1])

    BAR = _alloc_sem("BAR")

    def finalize():
        with nc.Block() as block:
            for eng in ENGS:
                getattr(block, eng)(
                    (lambda fns: (lambda e: [f(e) for f in fns]))(
                        E.streams[eng]))
        return nc

    def barrier(clear=True):
        for eng in ENGS:
            E.op(eng, lambda e: e.drain(), inc=BAR.bump())
        tgt = BAR.n
        for eng in ENGS:
            E.wait(eng, BAR, tgt)
        del clear

    def emit_lnstats(X, sm, sems, rstd_out, nmr_out, sq_trash,
                     reduce_inc=None):
        SC = sems["SC"]
        if reduce_inc is None:
            reduce_inc = SC.bump()
        rsem = reduce_inc[0]
        E.op("vector", lambda e: e.reduce_sum(sm["s1"], X, axis=AX.X),
             inc=reduce_inc)
        E.op("scalar", lambda e: e.activation(sq_trash, X, AF.Square,
                                              accum_out=sm["varb"]),
             inc=sems["SB"].bump())
        E.wait("vector", rsem, rsem.n)
        E.op("vector", lambda e: e.tensor_scalar(sm["mu"], sm["s1"], INV_D,
                                                 None, ALU.mult),
             inc=SC.bump())
        E.wait("vector", sems["SB"], sems["SB"].n)
        E.wait("vector", SC, SC.n)
        E.op("vector", lambda e: e.tensor_mul(sm["mu2"], sm["mu"], sm["mu"]),
             inc=SC.bump())
        E.wait("vector", SC, SC.n)
        E.op("vector", lambda e: e.scalar_tensor_tensor(
            sm["var"], sm["varb"], INV_D, sm["mu2"], ALU.mult, ALU.subtract),
            inc=SC.bump())
        E.wait("vector", SC, SC.n)
        E.op("vector", lambda e: e.tensor_scalar(sm["var"], sm["var"], EPS,
                                                 None, ALU.add),
             inc=SC.bump())
        E.wait("scalar", SC, SC.n)
        E.op("scalar", lambda e: e.activation(sm["lnv"], sm["var"], AF.Ln),
             inc=sems["SD"].bump())
        E.wait("scalar", sems["SD"], sems["SD"].n)
        E.op("scalar", lambda e, r=rstd_out: e.activation(r, sm["lnv"], AF.Exp,
                                                          scale=-0.5),
             inc=sems["SD"].bump())
        E.wait("vector", sems["SD"], sems["SD"].n)
        E.op("vector", lambda e, r=rstd_out, o=nmr_out: e.tensor_scalar(
            o, sm["mu"], r, -1.0, ALU.mult, ALU.mult), inc=sems["SE"].bump())

    RMM = _alloc_sem("RMM")
    REV = _alloc_sem("REV")

    def replicate_row(src_row, dst, b0, b1, add_one=False):
        E.wait("tensor", REV, REV.n)
        E.op("tensor", lambda e: e.matmul(bf(b0)[:, :], ones32[0:1, 0:128],
                                          src_row[0:1, 0:512],
                                          start=True, stop=True))
        E.op("tensor", lambda e: e.matmul(bf(b1)[:, :], ones32[0:1, 0:128],
                                          src_row[0:1, 512:1024],
                                          start=True, stop=True),
             inc=RMM.bump())
        E.wait("scalar", RMM, RMM.n)
        bias = 1.0 if add_one else 0.0
        E.op("scalar", lambda e: e.activation(dst[:, 0:512], bf(b0)[:, :],
                                              AF.Identity, bias=bias))
        E.op("scalar", lambda e: e.activation(dst[:, 512:1024], bf(b1)[:, :],
                                              AF.Identity, bias=bias),
             inc=REV.bump())

    LD0 = mkdma("LD0")
    LDmod = [mkdma("LDmod0"), mkdma("LDmod1")]
    LDrow = mkdma("LDrow")
    CAST0 = mksem("CAST0")
    MODMM = mksem("MODMM")
    MODEV = mksem("MODEV")
    P0A = mksem("P0A")
    P0B = mksem("P0B")
    P0C = mksem("P0C")
    ST0 = mkstore("ST0")

    wstage = [A.alloc(f"wst0_{i}", [128, 3072], F32) for i in range(2)]
    wmod_bf = A.alloc("wmod_bf", [128, 8 * 6144], BF16)
    modrow = A.alloc("modrow", [1, 6144], F32)
    c8 = A.alloc("c8", [8, 128], F32)
    cfm_bf = A.alloc("cfm_bf", [128, 8], BF16)
    cent = A.alloc("cent", [8, D], F32)
    centsq = A.alloc("centsq", [8, D], F32)
    css = A.alloc("css", [8, 2], F32)
    cnorm_bf = A.alloc("cnorm_bf", [8, D], BF16)
    wr_st = A.alloc("wr_st", [128, 64], F32)
    b2_st = A.alloc("b2_st", [8, D], F32)
    row_a = A.alloc("row_a", [1, D], F32)
    trow = [A.alloc(f"trow{i}", [1, D], F32) for i in range(2)]

    for dst, src in [(ident_bf[:, :], ident_bf_in[:, :]),
                     (ident32[:, :], ident32_in[:, :]),
                     (c8[:, :], c_row8[:, :]),
                     (cent[:, :], cent_in[:, :]),
                     (b2_st[:, :], b2_in[:, :])]:
        E.op("sync", lambda e, d=dst, s=src: e.dma_start(d, s), inc=LD0.bump(16))
    wr_src = bass.AP(wrouteT.tensor, 0, [[8, 128], [1024, 8], [1, 8]])
    E.op("sync", lambda e, s=wr_src: e.dma_start(
        wr_st.rearrange("p (a b) -> p a b", a=8), s), inc=LD0.bump(16))
    ld_pre = LD0.n

    E.op("vector", lambda e: e.memset(ones_bf[:, :], 1.0))
    E.op("vector", lambda e: e.memset(ones32[:, :], 1.0), inc=P0A.bump())
    E.wait("vector", LD0, ld_pre)
    E.op("vector", lambda e: e.tensor_copy(wrT_bf[:, :], wr_st[:, :]))
    E.op("vector", lambda e: e.tensor_copy(b2_bf[:, :], b2_st[:, :]))

    E.wait("scalar", LD0, ld_pre)
    E.op("scalar", lambda e: e.activation(c8[:, :], c8[:, :], AF.Silu),
         inc=P0B.bump())
    E.wait("tensor", P0B, P0B.n)
    E.wait("tensor", LD0, ld_pre)
    E.op("tensor", lambda e: e.transpose(bf(6)[0:128, 0:8], c8[0:8, 0:128],
                                         ident32[0:8, 0:8]), inc=MODMM.bump())
    E.wait("scalar", MODMM, MODMM.n)
    E.op("scalar", lambda e: e.activation(cfm_bf[:, :], bf(6)[0:128, 0:8],
                                          AF.Copy), inc=MODEV.bump())

    for k in range(8):
        for h in range(2):
            i = k * 2 + h
            if i >= 2:
                waitr("sync", CAST0, i - 1)
            E.op("sync", lambda e, kk=k, hh=h, ii=i: e.dma_start(
                wstage[ii % 2][:, :],
                wmod[ts(kk, 128), hh * 3072:(hh + 1) * 3072]),
                inc=LDmod[i % 2].bump(16))
            E.wait("vector", LDmod[i % 2], LDmod[i % 2].n)
            E.op("vector", lambda e, kk=k, hh=h, ii=i: e.tensor_copy(
                wmod_bf[:, kk * 6144 + hh * 3072: kk * 6144 + (hh + 1) * 3072],
                wstage[ii % 2][:, :]), inc=CAST0.bump())
    cast_all = CAST0.n

    E.wait("tensor", CAST0, cast_all)
    E.wait("tensor", MODEV, MODEV.n)
    for half in range(2):
        if half == 1:
            E.wait("tensor", MODEV, MODEV.n)
        for k in range(8):
            for nch in range(6):
                col = half * 3072 + nch * 512
                E.op("tensor", lambda e, kk=k, nn=nch, cc=col: e.matmul(
                    bf(nn)[0:1, :], cfm_bf[:, kk:kk + 1],
                    wmod_bf[:, kk * 6144 + cc: kk * 6144 + cc + 512],
                    start=(kk == 0), stop=(kk == 7)))
        E.op("tensor", lambda e: e.drain(), inc=MODMM.bump())
        E.wait("scalar", MODMM, MODMM.n)
        for nch in range(6):
            col = half * 3072 + nch * 512
            E.op("scalar", lambda e, nn=nch, cc=col: e.activation(
                modrow[0:1, cc:cc + 512], bf(nn)[0:1, :], AF.Identity),
                inc=MODEV.bump())

    E.wait("vector", MODEV, MODEV.n)
    E.op("vector", lambda e: e.tensor_scalar_add(modrow[0:1, 1024:2048],
                                                 modrow[0:1, 1024:2048], 1.0))
    E.op("vector", lambda e: e.tensor_scalar_add(modrow[0:1, 4096:5120],
                                                 modrow[0:1, 4096:5120], 1.0))
    for gc in (2048, 5120):
        g = modrow[0:1, gc:gc + 1024]
        E.op("vector", lambda e, gg=g: e.tensor_scalar(trow[0][0:1, :], gg,
                                                       -1.0, None, ALU.mult),
             inc=P0C.bump())
        E.wait("vector", P0C, P0C.n)
        E.op("vector", lambda e, gg=g: e.tensor_max(trow[0][0:1, :],
                                                    trow[0][0:1, :], gg),
             inc=P0C.bump())
        E.wait("vector", P0C, P0C.n)
        E.op("vector", lambda e: e.tensor_scalar_max(trow[0][0:1, :],
                                                     trow[0][0:1, :], 0.25),
             inc=P0C.bump())
        E.wait("vector", P0C, P0C.n)
        E.op("vector", lambda e, gg=g: e.tensor_scalar(trow[1][0:1, :], gg, 0.0,
                                                       None, ALU.is_ge),
             inc=P0C.bump())
        E.wait("vector", P0C, P0C.n)
        E.op("vector", lambda e: e.tensor_scalar(trow[1][0:1, :],
                                                 trow[1][0:1, :], 2.0, -1.0,
                                                 ALU.mult, ALU.add),
             inc=P0C.bump())
        E.wait("vector", P0C, P0C.n)
        E.op("vector", lambda e, gg=g: e.tensor_mul(gg, trow[1][0:1, :],
                                                    trow[0][0:1, :]),
             inc=P0C.bump())
        E.wait("vector", P0C, P0C.n)
    E.op("vector", lambda e: e.tensor_copy(trow[0][0:1, 0:8], modrow[0:1, 0:8]),
         inc=P0C.bump())
    E.wait("scalar", P0C, P0C.n)
    E.op("scalar", lambda e: e.dma_start(s_modrow[0:1, :], modrow[0:1, :]),
         inc=ST0.bump(16))

    E.wait("tensor", P0C, P0C.n)
    replicate_row(modrow[0:1, 0:1024], rs(0), 0, 1)
    replicate_row(modrow[0:1, 1024:2048], rs(1), 2, 3)
    replicate_row(modrow[0:1, 2048:3072], rs(2), 4, 5)
    E.op("sync", lambda e: e.dma_start(row_a[0:1, :], lnrow[0:1, 0:1024]),
         inc=LDrow.bump(16))
    E.wait("tensor", LDrow, LDrow.n)
    replicate_row(row_a[0:1, :], rs(3), 0, 1)

    E.wait("scalar", LD0, ld_pre)
    E.op("scalar", lambda e: e.activation(centsq[0:8, :], cent[0:8, :],
                                          AF.Square, accum_out=css[0:8, 0:1]),
         inc=P0B.bump())
    E.wait("scalar", P0B, P0B.n)
    E.op("scalar", lambda e: e.activation(css[0:8, 1:2], css[0:8, 0:1], AF.Ln),
         inc=P0B.bump())
    E.wait("scalar", P0B, P0B.n)
    E.op("scalar", lambda e: e.activation(css[0:8, 0:1], css[0:8, 1:2], AF.Exp,
                                          scale=0.5), inc=P0B.bump())
    E.wait("vector", P0B, P0B.n)
    E.op("vector", lambda e: e.tensor_scalar_max(css[0:8, 0:1], css[0:8, 0:1],
                                                 1e-12), inc=P0C.bump())
    E.wait("vector", P0C, P0C.n)
    E.op("vector", lambda e: e.reciprocal(css[0:8, 1:2], css[0:8, 0:1]),
         inc=P0C.bump())
    E.wait("vector", P0C, P0C.n)
    E.op("vector", lambda e: e.tensor_scalar(cnorm_bf[0:8, :], cent[0:8, :],
                                             css[0:8, 1:2], None, ALU.mult),
         inc=P0C.bump())
    E.wait("tensor", P0C, P0C.n)
    E.wait("tensor", REV, REV.n)
    for c in range(8):
        E.op("tensor", lambda e, cc=c: e.transpose(
            bb(6)[0:128, cc * 8:(cc + 1) * 8], cnorm_bf[0:8, ts(cc, 128)],
            ident_bf[0:8, 0:8]), inc=MODMM.bump() if c == 7 else None)
    E.wait("vector", MODMM, MODMM.n)
    E.op("vector", lambda e: e.tensor_copy(cnT_bf[:, :], bb(6)[0:128, 0:64]))

    if "s_dbg" in taps:
        E.wait("scalar", REV, REV.n)
        E.op("scalar", lambda e: e.dma_start(s_dbg[:, :], repl[:, :]),
             inc=ST0.bump(16))
    E.wait("scalar", ST0, ST0.n)
    barrier()
    if nphases == 1:
        return finalize()

    pool_reset()
    A.off = A.base
    wproj_bf = A.alloc("wproj_bf", [128, 8 * D], BF16)
    projb_bf = A.alloc("projb_bf", [1, D], BF16)
    maskrow_bf = A.alloc("maskrow_bf", [1, NT_OWN * WIN], BF16)
    ckptA = A.off
    wqkv_bf = A.alloc("wqkv_bf", [128, 8 * 3072], BF16)
    wstA = [A.alloc(f"wstA{i}", [128, 1536], F32) for i in range(2)]
    qkvb_bf = A.alloc("qkvb_bf", [1, 3 * D], BF16)
    rowstA = A.alloc("rowstA", [1, 3072], F32)
    lnb1_repl = A.alloc("lnb1_repl", [128, D], F32)
    x_ring = [A.alloc(f"xr{i}", [128, D], F32) for i in range(2)]
    h_ring = [A.alloc(f"hr{i}", [128, D], F32) for i in range(2)]
    u_buf = A.alloc("u_buf", [128, D], F32)
    u1_buf = A.alloc("u1_buf", [128, D], F32)
    h1_buf = A.alloc("h1_buf", [128, D], F32)
    a1_buf = A.alloc("a1_buf", [128, D], F32)
    abf_ring = [A.alloc(f"abf{i}", [128, D], BF16) for i in range(2)]
    sq_tr = A.alloc("sq_tr", [128, D], F32)
    aT_gr = [A.alloc(f"aT_g{i}", [128, 8 * 512], BF16) for i in range(2)]
    qk_st = [A.alloc(f"qkst{i}", [128, 512], BF16) for i in range(2)]
    v_st = [A.alloc(f"vst{i}", [128, D], BF16) for i in range(2)]
    smA = {k: A.alloc(f"smA_{k}", [128, 1], F32) for k in
           ("s1", "mu", "mu2", "varb", "var", "lnv", "rstd", "nmr",
            "s1b", "mub", "mu2b", "varbb", "varB", "lnvb", "rstdb", "nmrb")}
    sm1 = {k: smA[k] for k in ("s1", "mu", "mu2", "varb", "var", "lnv")}
    sm2 = {"s1": smA["s1b"], "mu": smA["mub"], "mu2": smA["mu2b"],
           "varb": smA["varbb"], "var": smA["varB"], "lnv": smA["lnvb"]}

    LDWs = [mkdma("LDWa"), mkdma("LDWb")]
    LDROW = mkdma("LDROW")
    WCAST = _alloc_sem("WCAST")
    LDXs = [mkdma("LDXa"), mkdma("LDXb")]
    SXR = mksem("SXR")
    SUX = mksem("SUX")
    SUH = mksem("SUH")
    SH = mksem("SH")
    SABF = mksem("SABF")
    STR = mksem("STR")
    SATEV = mksem("SATEV")
    SQMM = mksem("SQMM")
    SQEV = mksem("SQEV")
    SVMM = mksem("SVMM")
    SVEV = mksem("SVEV")
    STHs = [mkstore("STHa"), mkstore("STHb")]
    STQs = [mkstore("STQa"), mkstore("STQb")]
    STVs = [mkstore("STVa"), mkstore("STVb")]
    SLN1 = {k: mksem(f"A1a{k}") for k in ("SB", "SC", "SD", "SE")}
    SLN2 = {k: mksem(f"A1b{k}") for k in ("SB", "SC", "SD", "SE")}
    CH1 = mksem("CH1")

    dma("sync", lambda e: e.dma_start(rowstA[0:1, 0:1024],
                                      lnrow[0:1, 1024:2048]), LDROW)
    E.wait("tensor", LDROW, LDROW.n)
    replicate_row(rowstA[0:1, 0:1024], lnb1_repl, 0, 1)
    repl_a_done = REV.n

    E.wait("sync", RMM, RMM.n)
    E.op("sync", lambda e: e.dma_start(rowstA[0:1, :], qkvb_row[0:1, :]),
         inc=LDROW.bump(16))
    E.wait("vector", LDROW, LDROW.n)
    E.op("vector", lambda e: e.tensor_copy(qkvb_bf[0:1, :], rowstA[0:1, :]),
         inc=WCAST.bump())
    wc0 = WCAST.n
    for p in range(3):
        for k in range(8):
            i = p * 8 + k
            if i >= 2:
                E.wait("sync", WCAST, wc0 + i - 1)
            sW = LDWs[i % 2]
            E.op("sync", lambda e, kk=k, pp=p, ii=i: e.dma_start(
                wstA[ii % 2][:, 0:1024],
                wqkvT[ts(kk, 128), pp * 1024:(pp + 1) * 1024]),
                inc=sW.bump(16))
            E.wait("vector", sW, sW.n)
            E.op("vector", lambda e, kk=k, pp=p, ii=i: e.tensor_copy(
                wqkv_bf[:, kk * 3072 + pp * 1024: kk * 3072 + (pp + 1) * 1024],
                wstA[ii % 2][:, 0:1024]), inc=WCAST.bump())
        if p == 0:
            wcast_q = WCAST.n
        elif p == 1:
            wcast_k = WCAST.n
    wcast_v = WCAST.n
    E.wait("sync", WCAST, WCAST.n)
    E.op("sync", lambda e: e.dma_start(rowstA[0:1, 0:3072],
                                       maskrow[0:1, 0:3072]), inc=LDROW.bump(16))
    E.wait("vector", LDROW, LDROW.n)
    E.op("vector", lambda e: e.tensor_copy(maskrow_bf[0:1, 0:3072],
                                           rowstA[0:1, 0:3072]),
         inc=WCAST.bump())
    E.wait("sync", WCAST, WCAST.n)
    E.op("sync", lambda e: e.dma_start(rowstA[0:1, 0:3072],
                                       maskrow[0:1, 3072:6144]),
         inc=LDROW.bump(16))
    E.wait("vector", LDROW, LDROW.n)
    E.op("vector", lambda e: e.tensor_copy(maskrow_bf[0:1, 3072:6144],
                                           rowstA[0:1, 0:3072]),
         inc=WCAST.bump())
    wcast_qkv = WCAST.n

    def own_count(t):
        return max(0, min(t, 16))

    dma("sync", lambda e: e.dma_start(x_ring[0][:, :], x_ext[ts(0, 128), :]),
        LDXs[0], label=("a1x", 0))

    groups = [(0, 4), (4, 8), (8, 12), (12, 16), (16, 18)]

    def prep_tile_a1(t, lt, slot):
        xb = x_ring[t % 2]
        hb = h_ring[t % 2]
        ab = abf_ring[t % 2]
        if t + 1 < NT_EXT:
            waitr("sync", SXR, t)
            waitr("sync", SUX, t)
            dma("sync", lambda e, tt=t + 1: e.dma_start(
                x_ring[tt % 2][:, :], x_ext[ts(tt, 128), :]),
                LDXs[(t + 1) % 2], label=("a1x", t + 1))
        wdma("vector", ("a1x", t))
        wdma("scalar", ("a1x", t))
        emit_lnstats(xb[:, :], sm1, SLN1, smA["rstd"], smA["nmr"], sq_tr,
                     reduce_inc=SXR.bump())
        E.wait("scalar", SLN1["SE"], SLN1["SE"].n)
        waitr("scalar", SH, t)
        E.op("scalar", lambda e, x=xb: e.activation(
            u_buf[:, :], x[:, :], AF.Identity, scale=smA["rstd"],
            bias=smA["nmr"]), inc=SUX.bump())
        E.wait("vector", SUX, SUX.n)
        if t == 0:
            E.wait("vector", REV, repl_a_done)
        if t >= 2:
            wdma("vector", ("a1h", t - 2))
            waitr("vector", SUH, t - 1)
        E.op("vector", lambda e: e.scalar_tensor_tensor(
            h1_buf[:, :], u_buf[:, :], 1.0, rs(1), ALU.mult, ALU.mult),
            inc=CH1.bump())
        E.wait("vector", CH1, CH1.n)
        E.op("vector", lambda e, h=hb: e.scalar_tensor_tensor(
            h[:, :], h1_buf[:, :], 1.0, rs(0), ALU.mult, ALU.add),
            inc=SH.bump())
        if 1 <= t <= 16:
            E.wait("scalar", SH, SH.n)
            dma("scalar", lambda e, tt=t, h=hb: e.dma_start(
                s_h[tt - 1, :, :], h[:, :]), STHs[t % 2], label=("a1h", t))
        E.wait("scalar", SH, SH.n)
        E.wait("vector", SH, SH.n)
        emit_lnstats(hb[:, :], sm2, SLN2, smA["rstdb"], smA["nmrb"], sq_tr)
        E.wait("scalar", SLN2["SE"], SLN2["SE"].n)
        waitr("scalar", SABF, t)
        E.op("scalar", lambda e, h=hb: e.activation(
            u1_buf[:, :], h[:, :], AF.Identity, scale=smA["rstdb"],
            bias=smA["nmrb"]), inc=SUH.bump())
        E.wait("vector", SUH, SUH.n)
        waitr("vector", STR, t - 1)
        E.op("vector", lambda e: e.scalar_tensor_tensor(
            a1_buf[:, :], u1_buf[:, :], 1.0, rs(3), ALU.mult, ALU.mult),
            inc=CH1.bump())
        E.wait("vector", CH1, CH1.n)
        E.op("vector", lambda e, a=ab: e.scalar_tensor_tensor(
            a[:, :], a1_buf[:, :], 1.0, lnb1_repl, ALU.mult, ALU.add),
            inc=SABF.bump())
        E.wait("tensor", SABF, SABF.n)
        waitr("tensor", SATEV, t - 1)
        for c in range(8):
            E.op("tensor", lambda e, cc=c, a=ab, bk=t % 2: e.transpose(
                bb(bk)[:, ts(cc, 128)], a[:, ts(cc, 128)], ident_bf[:, :]),
                inc=STR.bump() if c == 7 else None)
        E.wait("vector", STR, STR.n)
        dstA = aT_gr[slot].rearrange("p (c w) -> p c w", c=8)[
            :, :, lt * 128:(lt + 1) * 128]
        E.op("vector", lambda e, d=dstA, bk=t % 2: e.tensor_copy(
            d, bb(bk).rearrange("p (c w) -> p c w", c=8)),
            inc=SATEV.bump())

    for t in range(4):
        prep_tile_a1(t, t, 0)
    qk_use = 0
    for gi, (t0, t1) in enumerate(groups):
        ng = (t1 - t0) * 128
        nxt = groups[gi + 1] if gi + 1 < len(groups) else None
        aTb = aT_gr[gi % 2]
        E.wait("tensor", SATEV, SATEV.n)
        for m in range(16):
            if gi == 0 and m == 0:
                E.wait("tensor", WCAST, wcast_q)
            if gi == 0 and m == 8:
                E.wait("tensor", WCAST, wcast_k)
            is_q = m < 8
            mm8 = m % 8
            colw = mm8 * 128 if is_q else 1024 + mm8 * 128
            bk = 2 + (qk_use % 2)
            waitr("tensor", SQEV, qk_use - 1)
            for k in range(8):
                E.op("tensor", lambda e, kk=k, cw=colw, b=bk, nn=ng, aT=aTb:
                     e.matmul(
                         bf(b)[:, 0:nn],
                         wqkv_bf[:, kk * 3072 + cw: kk * 3072 + cw + 128],
                         aT[:, kk * 512: kk * 512 + nn],
                         start=(kk == 0), stop=False))
            E.op("tensor", lambda e, cw=colw, b=bk, nn=ng: e.matmul(
                bf(b)[:, 0:nn], qkvb_bf[0:1, cw:cw + 128], ones_bf[0:1, 0:nn],
                start=False, stop=True), inc=SQMM.bump())
            E.wait("scalar", SQMM, SQMM.n)
            wdma("scalar", ("a1qk", qk_use - 2))
            sc = QSCALE if is_q else 1.0
            E.op("scalar", lambda e, b=bk, s=sc, st=qk_st[qk_use % 2], nn=ng:
                 e.activation(st[:, 0:nn], bf(b)[:, 0:nn], AF.Copy, scale=s),
                 inc=SQEV.bump())
            E.wait("scalar", SQEV, SQEV.n)
            dscr = q_scr if is_q else k_scr
            dma("scalar", lambda e, d=dscr, mmm=mm8, st=qk_st[qk_use % 2],
                c0=t0 * 128, nn=ng: e.dma_start(d[mmm, :, c0:c0 + nn],
                                                st[:, 0:nn]),
                STQs[qk_use % 2], label=("a1qk", qk_use))
            qk_use += 1
            if nxt is not None and m % 4 == 3:
                j = m // 4
                if nxt[0] + j < nxt[1]:
                    prep_tile_a1(nxt[0] + j, j, (gi + 1) % 2)
        if gi == 0:
            E.wait("tensor", WCAST, wcast_v)
        for t in range(t0, t1):
            lt = t - t0
            bks = (4, 5) if (t % 2 == 0) else (6, 7)
            E.wait("tensor", SVEV, max(0, SVMM.n - 1))
            for n in range(2):
                for k in range(8):
                    E.op("tensor", lambda e, kk=k, nn=n, b=bks[n], l=lt,
                         aT=aTb: e.matmul(
                             bf(b)[:, :],
                             aT[:, kk * 512 + l * 128: kk * 512 + l * 128 + 128],
                             wqkv_bf[:, kk * 3072 + 2048 + nn * 512:
                                     kk * 3072 + 2048 + nn * 512 + 512],
                             start=(kk == 0), stop=False))
                E.op("tensor", lambda e, nn=n, b=bks[n]: e.matmul(
                    bf(b)[:, :], ones_bf[0:1, 0:128],
                    qkvb_bf[0:1, 2048 + nn * 512: 2048 + nn * 512 + 512],
                    start=False, stop=True),
                    inc=SVMM.bump() if n == 1 else None)
            E.wait("scalar", SVMM, SVMM.n)
            wdma("scalar", ("a1v", t - 2))
            E.op("scalar", lambda e, b=bks[0], st=v_st[t % 2]: e.activation(
                st[:, 0:512], bf(b)[:, :], AF.Copy))
            E.op("scalar", lambda e, b=bks[1], st=v_st[t % 2]: e.activation(
                st[:, 512:1024], bf(b)[:, :], AF.Copy), inc=SVEV.bump())
            E.wait("scalar", SVEV, SVEV.n)
            dma("scalar", lambda e, tt=t, st=v_st[t % 2]: e.dma_start(
                v_scr[tt, :, :], st[:, :]), STVs[t % 2], label=("a1v", t))

    for k in range(8):
        E.wait("sync", WCAST, wcast_v if k < 2 else wcast_qkv + k - 1)
        sW = LDWs[k % 2]
        E.op("sync", lambda e, kk=k: e.dma_start(
            wstA[kk % 2][:, 0:1024], wprojT[ts(kk, 128), :]), inc=sW.bump(16))
        E.wait("vector", sW, sW.n)
        E.op("vector", lambda e, kk=k: e.tensor_copy(
            wproj_bf[:, kk * 1024:(kk + 1) * 1024], wstA[kk % 2][:, 0:1024]),
            inc=WCAST.bump())
    E.wait("sync", WCAST, WCAST.n)
    E.op("sync", lambda e: e.dma_start(rowstA[0:1, 0:1024], projb_row[0:1, :]),
         inc=LDROW.bump(16))
    E.wait("vector", LDROW, LDROW.n)
    E.op("vector", lambda e: e.tensor_copy(projb_bf[0:1, :], rowstA[0:1, 0:1024]),
         inc=WCAST.bump())
    wcast_proj = WCAST.n

    for s in STHs + STQs + STVs:
        E.wait("scalar", s, s.n)
    barrier()
    if nphases == 2:
        return finalize()

    pool_reset()
    A.off = ckptA
    k_ring = [A.alloc(f"kring{i}", [128, 8 * 128], BF16) for i in range(4)]
    v_ring = [A.alloc(f"vring{i}", [128, D], BF16) for i in range(4)]
    q_ring = [A.alloc(f"qring{i}", [128, 8 * 512], BF16) for i in range(4)]
    h2_ring = [A.alloc(f"h2r{i}", [128, D], F32) for i in range(2)]
    x2_ring = [A.alloc(f"x2r{i}", [128, D], F32) for i in range(2)]
    p_sb = [A.alloc(f"psb{i}", [128, WIN], BF16) for i in range(4)]
    pT_sb = [A.alloc(f"pTsb{i}", [128, WIN], BF16) for i in range(4)]
    ao_sb = A.alloc("ao_sb", [128, D], BF16)
    aoT_sb = A.alloc("aoT_sb", [128, D], BF16)
    x1_sb = [A.alloc(f"x1sb{i}", [128, D], F32) for i in range(2)]
    sxa_sb = [A.alloc(f"sxasb{i}", [128, D], F32) for i in range(2)]
    sxa_t1 = A.alloc("sxa_t1", [128, D], F32)
    ssum = A.alloc("ssum", [128, HEADS], F32)
    rrb = A.alloc("rrb", [128, HEADS], F32)
    sq_tr2 = A.alloc("sq_tr2", [128, D], F32)
    s1A2 = A.alloc("s1A2", [128, NT_OWN], F32)
    vbA2 = A.alloc("vbA2", [128, NT_OWN], F32)
    muA2 = A.alloc("muA2", [128, NT_OWN], F32)
    t1A2 = A.alloc("t1A2", [128, NT_OWN], F32)

    LKVs = [mkdma(f"LKV{i}") for i in range(4)]
    LVVs = [mkdma(f"LVV{i}") for i in range(4)]
    LQs = [mkdma(f"LQ{i}") for i in range(2)]
    LHs = [mkdma(f"LH{i}") for i in range(2)]
    LX2s = [mkdma(f"LX2{i}") for i in range(2)]
    SSC = mksem("SSC")
    SEXP = mksem("SEXP")
    SRR = mksem("SRR")
    SPT = mksem("SPT")
    SPTEp = [mksem("SPTEa"), mksem("SPTEb")]
    SVM2 = mksem("SVM2")
    SOEV = mksem("SOEV")
    SAOT = mksem("SAOT")
    SAOE = mksem("SAOE")
    SPRJ = mksem("SPRJ")
    SX1 = mksem("SX1")
    SSXA = mksem("SSXA")
    STX1s = [mkstore("STX1a"), mkstore("STX1b")]
    STSXAs = [mkstore("STSXAa"), mkstore("STSXAb")]
    SLNA2 = {k: mksem(f"A2c{k}") for k in ("SB", "SC", "SD", "SE")}
    CH2 = mksem("CH2")

    def kload(slot_ext):
        src = bass.AP(k_scr.tensor, slot_ext * 128,
                      [[NT_EXT * TOK, 128], [128 * NT_EXT * TOK, 8], [1, 128]])
        dst = k_ring[slot_ext % 4].rearrange("p (c w) -> p c w", c=8)
        return dst, src

    for s in range(3):
        dst, src = kload(s)
        dma("sync", lambda e, d=dst, ss=src: e.dma_start(d, ss),
            LKVs[s % 4], label=("kv", s))
        dma("sync", lambda e, d=v_ring[s % 4], ss=s: e.dma_start(
            d[:, :], v_scr[ss, :, :]), LVVs[s % 4], label=("vv", s))
    qsrc0 = bass.AP(q_scr.tensor, 128,
                    [[NT_EXT * TOK, 128], [128 * NT_EXT * TOK, 8], [1, 512]])
    dma("sync", lambda e, s=qsrc0: e.dma_start(
        q_ring[0].rearrange("p (c w) -> p c w", c=8), s), LQs[0],
        label=("qg", 0))
    dma("sync", lambda e: e.dma_start(h2_ring[0][:, :], s_h[0, :, :]),
        LHs[0], label=("h", 0))
    dma("sync", lambda e: e.dma_start(x2_ring[0][:, :], x_ext[ts(1, 128), :]),
        LX2s[0], label=("x2", 0))

    def tail_proj(t):
        waitr("tensor", SAOE, t + 1)
        waitr("tensor", SX1, t)
        E.wait("tensor", WCAST, wcast_proj)
        for n in range(2):
            for k in range(8):
                E.op("tensor", lambda e, kk=k, nn=n: e.matmul(
                    bf(6 + nn)[:, :], aoT_sb[:, ts(kk, 128)],
                    wproj_bf[:, kk * 1024 + nn * 512: kk * 1024 + nn * 512 + 512],
                    start=(kk == 0), stop=False))
            E.op("tensor", lambda e, nn=n: e.matmul(
                bf(6 + nn)[:, :], ones_bf[0:1, 0:128],
                projb_bf[0:1, nn * 512:(nn + 1) * 512], start=False, stop=True),
                inc=SPRJ.bump() if n == 1 else None)
        E.wait("vector", SPRJ, SPRJ.n)
        wdma("vector", ("h", t))
        wdma("vector", ("stx1", t - 2))
        waitr("vector", SLNA2["SB"], t - 1)
        E.op("vector", lambda e, tt=t: e.tensor_add(
            x1_sb[tt % 2][:, 0:512], bf(6)[:, :], h2_ring[tt % 2][:, 0:512]))
        E.op("vector", lambda e, tt=t: e.tensor_add(
            x1_sb[tt % 2][:, 512:1024], bf(7)[:, :],
            h2_ring[tt % 2][:, 512:1024]), inc=SX1.bump())
        E.wait("scalar", SX1, SX1.n)
        dma("scalar", lambda e, tt=t: e.dma_start(
            s_x1[tt, :, :], x1_sb[tt % 2][:, :]), STX1s[t % 2],
            label=("stx1", t))
        wdma("vector", ("x2", t))
        wdma("vector", ("stsxa", t - 2))
        E.wait("vector", SX1, SX1.n)
        E.op("vector", lambda e, tt=t: e.scalar_tensor_tensor(
            sxa_t1[:, :], x1_sb[tt % 2][:, :], 1.0, rs(2), ALU.mult, ALU.mult),
            inc=CH2.bump())
        E.wait("vector", CH2, CH2.n)
        E.op("vector", lambda e, tt=t: e.tensor_add(
            sxa_sb[tt % 2][:, :], sxa_t1[:, :], x2_ring[tt % 2][:, :]),
            inc=SSXA.bump())
        E.wait("scalar", SSXA, SSXA.n)
        dma("scalar", lambda e, tt=t: e.dma_start(
            s_sxa[tt, :, :], sxa_sb[tt % 2][:, :]), STSXAs[t % 2],
            label=("stsxa", t))
        E.wait("scalar", SX1, SX1.n)
        E.wait("vector", SX1, SX1.n)
        E.op("vector", lambda e, tt=t: e.reduce_sum(
            s1A2[:, tt:tt + 1], x1_sb[tt % 2][:, :], axis=AX.X),
            inc=SLNA2["SC"].bump())
        waitr("scalar", SLNA2["SB"], t)
        E.op("scalar", lambda e, tt=t: e.activation(
            sq_tr2[:, :], x1_sb[tt % 2][:, :], AF.Square,
            accum_out=vbA2[:, tt:tt + 1]), inc=SLNA2["SB"].bump())

    for t in range(min(NT_OWN, a2_tiles)):
        g = t // 4
        if t + 3 < NT_EXT and t < a2_nopf:
            waitr("sync", SVM2, HEADS * t)
            dst, src = kload(t + 3)
            dma("sync", lambda e, d=dst, ss=src: e.dma_start(d, ss),
                LKVs[(t + 3) % 4], label=("kv", t + 3))
            dma("sync", lambda e, d=v_ring[(t + 3) % 4], ss=t + 3:
                e.dma_start(d[:, :], v_scr[ss, :, :]), LVVs[(t + 3) % 4],
                label=("vv", t + 3))
        if t % 4 == 0 and g + 1 < 4 and t < a2_nopf:
            waitr("sync", SSC, HEADS * 4 * g)
            c0 = (g + 1) * 512 + 128
            nq = min(512, NT_EXT * TOK - c0)
            qsrc = bass.AP(q_scr.tensor, c0,
                           [[NT_EXT * TOK, 128], [128 * NT_EXT * TOK, 8],
                            [1, nq]])
            dma("sync", lambda e, d=q_ring[(g + 1) % 4], ss=qsrc, nn=nq:
                e.dma_start(d.rearrange("p (c w) -> p c w", c=8)[:, :, 0:nn],
                            ss), LQs[(g + 1) % 2], label=("qg", g + 1))
        if t + 1 < min(NT_OWN, a2_nopf):
            waitr("sync", SX1, t)
            dma("sync", lambda e, tt=t + 1: e.dma_start(
                h2_ring[tt % 2][:, :], s_h[tt, :, :]), LHs[(t + 1) % 2],
                label=("h", t + 1))
            waitr("sync", SSXA, t)
            dma("sync", lambda e, tt=t + 1: e.dma_start(
                x2_ring[tt % 2][:, :], x_ext[ts(tt + 1, 128), :]),
                LX2s[(t + 1) % 2], label=("x2", t + 1))

        qg = q_ring[g % 4]
        qcol0 = (t % 4) * 128
        masked = t in (0, NT_OWN - 1)

        def emit_tr_copy(idx):
            tb = 2 if idx % 2 == 0 else 5
            waitr("tensor", SEXP, idx + 1)
            waitr("tensor", SPTEp[idx % 2], idx // 2)
            if tb == 5:
                waitr("tensor", SAOE, idx // HEADS)
            for j in range(3):
                E.op("tensor", lambda e, jj=j, pp=p_sb[idx % 4], b=tb:
                     e.transpose(
                         bb(b)[:, jj * 128:(jj + 1) * 128],
                         pp[:, jj * 128:(jj + 1) * 128], ident_bf[:, :]),
                     inc=SPT.bump() if j == 2 else None)
            if idx % 2 == 0:
                E.wait("scalar", SPT, SPT.n)
                E.op("scalar", lambda e, pt=pT_sb[idx % 4], b=tb:
                     e.activation(pt[:, :], bb(b)[:, 0:WIN], AF.Copy),
                     inc=SPTEp[0].bump())
            else:
                E.wait("vector", SPT, SPT.n)
                E.op("vector", lambda e, pt=pT_sb[idx % 4], b=tb:
                     e.tensor_copy(pt[:, :], bb(b)[:, 0:WIN]),
                     inc=SPTEp[1].bump())

        def emit_v(idx):
            hh = idx % HEADS
            tt = idx // HEADS
            hf = hh // 8
            so = hh % 8
            waitr("tensor", SPTEp[idx % 2], idx // 2 + 1)
            if so == 0:
                waitr("tensor", SOEV, 8 * ((tt * 2 + hf) - 1))
            for j in range(3):
                E.op("tensor", lambda e, jj=j, sl=(tt + j) % 4, h2=hh, hf2=hf,
                     pt=pT_sb[idx % 4]: e.matmul(
                         bf(3 + hf2)[:, (h2 % 8) * 64:((h2 % 8) + 1) * 64],
                         pt[:, jj * 128:(jj + 1) * 128],
                         v_ring[sl][:, h2 * 64:(h2 + 1) * 64],
                         start=(jj == 0), stop=(jj == 2)),
                     inc=SVM2.bump() if j == 2 else None)
            if so == 7:
                E.wait("vector", SVM2, SVM2.n)
                E.wait("vector", SRR, SRR.n)
                if tt > 0 and hf == 0:
                    waitr("vector", SAOT, tt)
                    wdma("vector", ("dbgao", tt - 1))
                for h2i in range(hf * 8, hf * 8 + 8):
                    E.op("vector", lambda e, hh2=h2i, hf2=hf: e.tensor_scalar(
                        ao_sb[:, hh2 * 64:(hh2 + 1) * 64],
                        bf(3 + hf2)[:, (hh2 % 8) * 64:((hh2 % 8) + 1) * 64],
                        rrb[:, hh2:hh2 + 1], None, ALU.mult),
                        inc=SOEV.bump())

        for h in range(HEADS):
            idx = t * HEADS + h
            plane = h // 2
            prow = (h % 2) * 64
            scb = idx % 2
            if h == 0:
                for w in range(3):
                    wdma("tensor", ("kv", t + w))
                    wdma("tensor", ("vv", t + w))
                wdma("tensor", ("qg", g))
            waitr("tensor", SEXP, idx - 1)
            for j in range(3):
                E.op("tensor", lambda e, jj=j, sl=(t + j) % 4, b=scb, pl=plane,
                     pr=prow, q0=qcol0, qgb=qg, mk=masked: e.matmul(
                         bf(b)[:, jj * 128:(jj + 1) * 128],
                         qgb[pr:pr + 64, pl * 512 + q0: pl * 512 + q0 + 128],
                         k_ring[sl][pr:pr + 64, pl * 128:(pl + 1) * 128],
                         start=(jj == 0), stop=(jj == 2 and not mk)),
                     inc=SSC.bump() if (j == 2 and not masked) else None)
            if masked:
                E.op("tensor", lambda e, b=scb, tt=t: e.matmul(
                    bf(b)[:, 0:WIN], ones_bf[0:1, 0:128],
                    maskrow_bf[0:1, tt * WIN:(tt + 1) * WIN],
                    start=False, stop=True), inc=SSC.bump())
            E.wait("scalar", SSC, SSC.n)
            E.op("scalar", lambda e, b=scb, hh=h, pp=p_sb[idx % 4]:
                 e.activation(pp[:, :], bf(b)[:, 0:WIN], AF.Exp,
                              accum_out=ssum[:, hh:hh + 1]), inc=SEXP.bump())
            E.wait("vector", SEXP, SEXP.n)
            E.op("vector", lambda e, hh=h: e.reciprocal(rrb[:, hh:hh + 1],
                                                        ssum[:, hh:hh + 1]),
                 inc=SRR.bump())
            if h > 1:
                emit_tr_copy(idx - 2)
            if h > 3:
                emit_v(idx - 4)
            if h == 6 and t > 0:
                tail_proj(t - 1)
        for fl in (HEADS - 2, HEADS - 1):
            emit_tr_copy(t * HEADS + fl)
        for fl in (HEADS - 4, HEADS - 3, HEADS - 2, HEADS - 1):
            emit_v(t * HEADS + fl)
        if "s_dbgao" in taps:
            E.wait("scalar", SOEV, SOEV.n)
            dma("scalar", lambda e, tt=t: e.dma_start(
                s_dbgao[tt, :, :], ao_sb[:, :]), ST0, label=("dbgao", t))
        E.wait("tensor", SOEV, SOEV.n)
        waitr("tensor", SAOE, t)
        for c in range(8):
            E.op("tensor", lambda e, cc=c: e.transpose(
                bb(5)[:, ts(cc, 128)], ao_sb[:, ts(cc, 128)], ident_bf[:, :]),
                inc=SAOT.bump() if c == 7 else None)
        E.wait("vector", SAOT, SAOT.n)
        waitr("vector", SPRJ, t)
        E.op("vector", lambda e: e.tensor_copy(aoT_sb[:, :], bb(5)[:, 0:D]),
             inc=SAOE.bump())

    tail_proj(NT_OWN - 1)

    _ps = stats_ln2.ap[0][0]
    stats_ev = bass.AP(stats_ln2.tensor, stats_ln2.offset, [[_ps, 128], [2, 16]])
    stats_od = bass.AP(stats_ln2.tensor, stats_ln2.offset + 1,
                       [[_ps, 128], [2, 16]])
    E.wait("vector", SLNA2["SB"], SLNA2["SB"].n)
    E.wait("vector", SLNA2["SC"], SLNA2["SC"].n)
    E.op("vector", lambda e: e.tensor_scalar(muA2[:, :], s1A2[:, :], INV_D,
                                             None, ALU.mult), inc=CH2.bump())
    E.wait("vector", CH2, CH2.n)
    E.op("vector", lambda e: e.tensor_mul(t1A2[:, :], muA2[:, :], muA2[:, :]),
         inc=CH2.bump())
    E.wait("vector", CH2, CH2.n)
    E.op("vector", lambda e: e.scalar_tensor_tensor(
        t1A2[:, :], vbA2[:, :], INV_D, t1A2[:, :], ALU.mult, ALU.subtract),
        inc=CH2.bump())
    E.wait("vector", CH2, CH2.n)
    E.op("vector", lambda e: e.tensor_scalar_add(t1A2[:, :], t1A2[:, :], EPS),
         inc=CH2.bump())
    E.wait("scalar", CH2, CH2.n)
    E.op("scalar", lambda e: e.activation(s1A2[:, :], t1A2[:, :], AF.Ln),
         inc=CH2.bump())
    E.wait("scalar", CH2, CH2.n)
    E.op("scalar", lambda e, ev=stats_ev: e.activation(ev, s1A2[:, :], AF.Exp,
                                                       scale=-0.5),
         inc=CH2.bump())
    E.wait("vector", CH2, CH2.n)
    E.op("vector", lambda e, ev=stats_ev: e.tensor_mul(t1A2[:, :], muA2[:, :],
                                                       ev), inc=CH2.bump())
    E.wait("vector", CH2, CH2.n)
    E.op("vector", lambda e, ov=stats_od: e.tensor_scalar(
        ov, t1A2[:, :], -1.0, None, ALU.mult), inc=CH2.bump())

    for s in STX1s + STSXAs:
        E.wait("scalar", s, s.n)
    barrier()
    if nphases == 3:
        return finalize()

    pool_reset()
    A.off = A.base
    wfc1_bf = A.alloc("wfc1_bf", [128, 8 * DFF], BF16)
    wfc2_bf = A.alloc("wfc2_bf", [128, 32 * D], BF16)
    wstM = [A.alloc(f"wstM{i}", [128, 512], F32) for i in range(2)]
    wstM2 = [A.alloc(f"wstM2{i}", [128, 512], F32) for i in range(2)]
    fc1b_fm = A.alloc("fc1b_fm", [128, 32], F32)
    fc2b_bf = A.alloc("fc2b_bf", [1, D], BF16)
    lnw2_repl = A.alloc("lnw2_repl", [128, D], F32)
    lnb2_repl = A.alloc("lnb2_repl", [128, D], F32)
    x1_ring = [A.alloc(f"x1rM{i}", [128, D], F32) for i in range(3)]
    u2_buf = A.alloc("u2_buf", [128, D], F32)
    zbf_buf = A.alloc("zbf_buf", [128, D], BF16)
    zT_gr = [A.alloc(f"zT_g{i}", [128, 8 * 256], BF16) for i in range(2)]
    tT_r = [A.alloc(f"tTr{i}", [128, 256], BF16) for i in range(2)]
    s2_sb = [A.alloc(f"s2sb{i}", [128, D], F32) for i in range(2)]
    s1M = A.alloc("s1M", [128, NT_OWN], F32)
    vbM = A.alloc("vbM", [128, NT_OWN], F32)
    muM = A.alloc("muM", [128, NT_OWN], F32)
    t1M = A.alloc("t1M", [128, NT_OWN], F32)

    LWMs = [mkdma("LWMa"), mkdma("LWMb")]
    LWM2s = [_alloc_sem("LWM2a"), _alloc_sem("LWM2b")]
    for _s in LWM2s:
        _s.base = 0
    LROWM = mkdma("LROWM")
    LFCB = mkdma("LFCB")
    WCM = mksem("WCM")
    WCM2 = mksem("WCM2")
    WCB = mksem("WCB")
    LX1Ms = [mkdma(f"LX1M{i}") for i in range(3)]
    LXMs = [mkdma("LXMa"), mkdma("LXMb")]
    SUZ = mksem("SUZ")
    SZ = mksem("SZ")
    SZT = mksem("SZT")
    SZTE = mksem("SZTE")
    SF1 = mksem("SF1")
    SGE = mksem("SGE")
    SF2C = mksem("SF2C")
    SF2T = mksem("SF2T")
    SS2 = mksem("SS2")
    STS2s = [mkstore("STS2a"), mkstore("STS2b")]
    SLNM = {k: mksem(f"Mc{k}") for k in ("SB", "SC", "SD", "SE")}
    CHM = mksem("CHM")

    E.op("sync", lambda e: e.dma_start(u2_buf[0:1, :], lnrow[0:1, 2048:3072]),
         inc=LROWM.bump(16))
    E.wait("tensor", LROWM, LROWM.n)
    replicate_row(u2_buf[0:1, :], lnw2_repl, 0, 1)
    E.wait("sync", RMM, RMM.n)
    E.op("sync", lambda e: e.dma_start(u2_buf[0:1, :], lnrow[0:1, 3072:4096]),
         inc=LROWM.bump(16))
    E.wait("tensor", LROWM, LROWM.n)
    replicate_row(u2_buf[0:1, :], lnb2_repl, 2, 3)
    repl_m_done = REV.n

    for t in range(2):
        dma("sync", lambda e, tt=t: e.dma_start(
            x1_ring[tt % 3][:, :], s_x1[tt, :, :]), LX1Ms[t % 3],
            label=("x1m", t))
        dma("sync", lambda e, tt=t: e.dma_start(
            s2_sb[tt % 2][:, :], s_sxa[tt, :, :]), LXMs[t % 2],
            label=("xm", t))

    E.op("sync", lambda e: e.dma_start(fc1b_fm[:, :], fc1b_fm_in[:, :]),
         inc=LFCB.bump(16))
    E.wait("sync", RMM, RMM.n)
    E.op("sync", lambda e: e.dma_start(u2_buf[0:1, :], fc2b_row[0:1, :]),
         inc=LROWM.bump(16))
    for i in range(64):
        k1, q1 = i % 8, i // 8
        k2, q2 = i // 2, i % 2
        if i >= 2:
            waitr("sync", WCM, i - 1)
            waitr("gpsimd", WCM2, i - 1)
        E.op("sync", lambda e, kk=k1, qq=q1, ii=i: e.dma_start(
            wstM[ii % 2][:, :], wfc1T[ts(kk, 128), qq * 512:(qq + 1) * 512]),
            inc=LWMs[i % 2].bump(16))
        E.op("gpsimd", lambda e, kk=k2, qq=q2, ii=i: e.dma_start(
            wstM2[ii % 2][:, :], wfc2T[ts(kk, 128), qq * 512:(qq + 1) * 512]),
            inc=LWM2s[i % 2].bump(16))
    E.wait("scalar", LFCB, LFCB.n)
    lwm_n = {0: LWMs[0].base, 1: LWMs[1].base}
    lwm2_n = {0: LWM2s[0].base, 1: LWM2s[1].base}

    def emit_casts(i0, i1):
        for i in range(i0, i1):
            k1, q1 = i % 8, i // 8
            k2, q2 = i // 2, i % 2
            lwm_n[i % 2] += 16
            E.wait("vector", LWMs[i % 2], lwm_n[i % 2])
            E.op("vector", lambda e, kk=k1, qq=q1, ii=i: e.tensor_copy(
                wfc1_bf[:, kk * DFF + qq * 512: kk * DFF + (qq + 1) * 512],
                wstM[ii % 2][:, :]), inc=WCM.bump())
            lwm2_n[i % 2] += 16
            E.wait("vector", LWM2s[i % 2], lwm2_n[i % 2])
            E.op("vector", lambda e, kk=k2, qq=q2, ii=i: e.tensor_copy(
                wfc2_bf[:, kk * 1024 + qq * 512: kk * 1024 + (qq + 1) * 512],
                wstM2[ii % 2][:, :]), inc=WCM2.bump())

    emit_casts(0, 2)
    E.wait("vector", LROWM, LROWM.n)
    E.op("vector", lambda e: e.tensor_copy(fc2b_bf[0:1, :], u2_buf[0:1, :]),
         inc=WCB.bump())

    obA = (4, 5)
    obB = (6, 7)

    def prep_tile(t, slot, lt):
        if t + 2 < NT_OWN:
            waitr("sync", SUZ, t)
            dma("sync", lambda e, tt=t + 2: e.dma_start(
                x1_ring[tt % 3][:, :], s_x1[tt, :, :]), LX1Ms[(t + 2) % 3],
                label=("x1m", t + 2))
        wdma("scalar", ("x1m", t))
        if t == 0:
            E.wait("scalar", REV, repl_m_done)
            waitr("scalar", WCB, 1)
        waitr("scalar", SZ, t)
        waitr("scalar", SS2, 2 * (t // 2) - 2)
        E.op("scalar", lambda e, tt=t: e.activation(
            u2_buf[:, :], x1_ring[tt % 3][:, :], AF.Identity,
            scale=stats_ln2[:, 2 * tt:2 * tt + 1],
            bias=stats_ln2[:, 2 * tt + 1:2 * tt + 2]), inc=SUZ.bump())
        E.wait("vector", SUZ, SUZ.n)
        if t == 0:
            E.wait("vector", REV, repl_m_done)
        waitr("vector", SZT, t)
        E.op("vector", lambda e: e.scalar_tensor_tensor(
            u2_buf[:, :], u2_buf[:, :], 1.0, lnw2_repl, ALU.mult, ALU.mult),
            inc=CHM.bump())
        E.wait("vector", CHM, CHM.n)
        E.op("vector", lambda e: e.scalar_tensor_tensor(
            zbf_buf[:, :], u2_buf[:, :], 1.0, lnb2_repl, ALU.mult, ALU.add),
            inc=SZ.bump())
        E.wait("tensor", SZ, SZ.n)
        waitr("tensor", SZTE, t - 1)
        for c in range(8):
            E.op("tensor", lambda e, cc=c, bk=t % 2: e.transpose(
                bb(bk)[:, ts(cc, 128)], zbf_buf[:, ts(cc, 128)],
                ident_bf[:, :]), inc=SZT.bump() if c == 7 else None)
        E.wait("vector", SZT, SZT.n)
        waitr("vector", SF1, 32 * (t // 2 - 1))
        dstz = zT_gr[slot].rearrange("p (c w) -> p c w", c=8)[
            :, :, lt * 128:(lt + 1) * 128]
        E.op("vector", lambda e, d=dstz, bk=t % 2: e.tensor_copy(
            d, bb(bk).rearrange("p (c w) -> p c w", c=8)), inc=SZTE.bump())

    def emit_fc2(g, m):
        mu = g * 32 + m
        if g == 0:
            waitr("tensor", WCM2, 2 * (m + 1))
        waitr("tensor", SGE, mu + 1)
        for lt in range(2):
            ob = obA if lt == 0 else obB
            for n in range(2):
                E.op("tensor", lambda e, nn=n, l=lt, mm=m, b=ob[n],
                     st=tT_r[mu % 2]: e.matmul(
                         bf(b)[:, :], st[:, l * 128:(l + 1) * 128],
                         wfc2_bf[:, mm * 1024 + nn * 512: mm * 1024 + nn * 512 + 512],
                         start=(mm == 0), stop=False),
                     inc=SF2C.bump() if (lt == 1 and n == 1) else None)

    prep_tile(0, 0, 0)
    prep_tile(1, 0, 1)
    emit_casts(2, 64)
    for g in range(8):
        tiles = (2 * g, 2 * g + 1)
        E.wait("tensor", SZTE, SZTE.n)
        waitr("tensor", SS2, 2 * g)
        for m in range(32):
            mu = g * 32 + m
            fb = 2 + (mu % 2)
            if g == 0:
                waitr("tensor", WCM, 8 * (m // 4) + 8)
            waitr("tensor", SGE, mu - 1)
            for k in range(8):
                E.op("tensor", lambda e, kk=k, mm=m, b=fb, zt=zT_gr[g % 2]:
                     e.matmul(
                         bf(b)[:, 0:256],
                         wfc1_bf[:, kk * DFF + mm * 128: kk * DFF + mm * 128 + 128],
                         zt[:, kk * 256:(kk + 1) * 256],
                         start=(kk == 0), stop=(kk == 7)),
                     inc=SF1.bump() if k == 7 else None)
            E.wait("scalar", SF1, SF1.n)
            waitr("scalar", SF2C, mu - 1)
            E.op("scalar", lambda e, mm=m, b=fb, st=tT_r[mu % 2]: e.activation(
                st[:, :], bf(b)[:, 0:256], AF.Gelu, bias=fc1b_fm[:, mm:mm + 1]),
                inc=SGE.bump())
            if m >= 1:
                emit_fc2(g, m - 1)
        emit_fc2(g, 31)
        for lt, t in enumerate(tiles):
            ob = obA if lt == 0 else obB
            if g == 0 and lt == 0:
                waitr("tensor", WCB, 1)
            for n in range(2):
                E.op("tensor", lambda e, nn=n, b=ob[n]: e.matmul(
                    bf(b)[:, :], ones_bf[0:1, 0:128],
                    fc2b_bf[0:1, nn * 512:(nn + 1) * 512],
                    start=False, stop=True),
                    inc=SF2T.bump() if n == 1 else None)
        if g + 1 < 8:
            prep_tile(2 * g + 2, (g + 1) % 2, 0)
            prep_tile(2 * g + 3, (g + 1) % 2, 1)
        for lt, t in enumerate(tiles):
            ob = obA if lt == 0 else obB
            E.wait("vector", SF2T, SF2T.n)
            wdma("vector", ("xm", t))
            waitr("vector", SLNM["SB"], t)
            E.op("vector", lambda e, b=ob[0]: e.scalar_tensor_tensor(
                u2_buf[:, 0:512], bf(b)[:, :], 1.0, rs(2)[:, 0:512],
                ALU.mult, ALU.mult))
            E.op("vector", lambda e, b=ob[1]: e.scalar_tensor_tensor(
                u2_buf[:, 512:1024], bf(b)[:, :], 1.0, rs(2)[:, 512:1024],
                ALU.mult, ALU.mult), inc=CHM.bump())
            E.wait("vector", CHM, CHM.n)
            E.op("vector", lambda e, tt=t: e.tensor_add(
                s2_sb[tt % 2][:, :], u2_buf[:, :], s2_sb[tt % 2][:, :]),
                inc=SS2.bump())
            E.wait("scalar", SS2, SS2.n)
            dma("scalar", lambda e, tt=t: e.dma_start(
                s_s2[tt, :, :], s2_sb[tt % 2][:, :]), STS2s[t % 2],
                label=("sts2", t))
            E.wait("scalar", SS2, SS2.n)
            E.wait("vector", SS2, SS2.n)
            E.op("vector", lambda e, tt=t: e.reduce_sum(
                s1M[:, tt:tt + 1], s2_sb[tt % 2][:, :], axis=AX.X),
                inc=SLNM["SC"].bump())
            waitr("scalar", SLNM["SB"], t)
            E.op("scalar", lambda e, tt=t: e.activation(
                u2_buf[:, :], s2_sb[tt % 2][:, :], AF.Square,
                accum_out=vbM[:, tt:tt + 1]), inc=SLNM["SB"].bump())
            if t + 2 < NT_OWN:
                wdma("sync", ("sts2", t))
                waitr("sync", SLNM["SB"], t + 1)
                waitr("sync", SLNM["SC"], t + 1)
                dma("sync", lambda e, tt=t + 2: e.dma_start(
                    s2_sb[tt % 2][:, :], s_sxa[tt, :, :]), LXMs[t % 2],
                    label=("xm", t + 2))

    _ps2 = stats_s2.ap[0][0]
    sts2_ev = bass.AP(stats_s2.tensor, stats_s2.offset, [[_ps2, 128], [2, 16]])
    sts2_od = bass.AP(stats_s2.tensor, stats_s2.offset + 1,
                      [[_ps2, 128], [2, 16]])
    E.wait("vector", SLNM["SB"], SLNM["SB"].n)
    E.wait("vector", SLNM["SC"], SLNM["SC"].n)
    E.op("vector", lambda e: e.tensor_scalar(muM[:, :], s1M[:, :], INV_D,
                                             None, ALU.mult), inc=CHM.bump())
    E.wait("vector", CHM, CHM.n)
    E.op("vector", lambda e: e.tensor_mul(t1M[:, :], muM[:, :], muM[:, :]),
         inc=CHM.bump())
    E.wait("vector", CHM, CHM.n)
    E.op("vector", lambda e: e.scalar_tensor_tensor(
        t1M[:, :], vbM[:, :], INV_D, t1M[:, :], ALU.mult, ALU.subtract),
        inc=CHM.bump())
    E.wait("vector", CHM, CHM.n)
    E.op("vector", lambda e: e.tensor_scalar_add(t1M[:, :], t1M[:, :], EPS),
         inc=CHM.bump())
    E.wait("scalar", CHM, CHM.n)
    E.op("scalar", lambda e: e.activation(s1M[:, :], t1M[:, :], AF.Ln),
         inc=CHM.bump())
    E.wait("scalar", CHM, CHM.n)
    E.op("scalar", lambda e, ev=sts2_ev: e.activation(ev, s1M[:, :], AF.Exp,
                                                      scale=-0.5),
         inc=CHM.bump())
    E.wait("vector", CHM, CHM.n)
    E.op("vector", lambda e, ev=sts2_ev: e.tensor_mul(t1M[:, :], muM[:, :],
                                                      ev), inc=CHM.bump())
    E.wait("vector", CHM, CHM.n)
    E.op("vector", lambda e, ov=sts2_od: e.tensor_scalar(
        ov, t1M[:, :], -1.0, None, ALU.mult), inc=CHM.bump())

    for s in STS2s:
        E.wait("scalar", s, s.n)
    barrier()
    if nphases == 4:
        return finalize()

    pool_reset()
    A.off = A.base
    gates = A.alloc("gates", [128, NEXP * NT_OWN], F32)
    gatesT_bf = A.alloc("gatesT_bf", [8, NT_OWN * TOK], BF16)
    r2_all = A.alloc("r2_all", [128, NT_OWN], F32)
    ck5 = A.off
    s2E_ring = [A.alloc(f"s2E{i}", [128, D], F32) for i in range(2)]
    u3_buf = A.alloc("u3_buf", [128, D], F32)
    h1E_buf = A.alloc("h1E_buf", [128, D], F32)
    h2bf_buf = A.alloc("h2bf_buf", [128, D], BF16)
    h2st = [A.alloc(f"h2st{i}", [128, D], BF16) for i in range(2)]
    rowstE = A.alloc("rowstE", [1, D], F32)
    cos_all = A.alloc("cos_all", [128, NEXP * NT_OWN], F32)
    route_all = A.alloc("route_all", [128, NEXP * NT_OWN], F32)
    u_all = A.alloc("u_all", [128, NEXP * NT_OWN], F32)
    gtub_all = A.alloc("gtub_all", [128, NEXP * NT_OWN], F32)
    gmask_all = A.alloc("gmask_all", [128, NEXP * NT_OWN], F32)
    gE_all = A.alloc("gE_all", [128, NEXP * NT_OWN], F32)
    g16 = {k: A.alloc(f"g16_{k}", [128, NT_OWN], F32) for k in
           ("m1", "m2", "rsum", "rinv", "r2s")}

    def bc16(a):
        return bass.AP(a.tensor, a.offset, [[a.ap[0][0], 128], [1, 16], [0, 8]])

    def c16(a):
        return bass.AP(a.tensor, a.offset, [[a.ap[0][0], 128], [1, 16], [1, 1]])

    def g3(a):
        return a.rearrange("p (t e) -> p t e", t=16)
    assert A.off <= ck5 + 33 * 1024, f"5a scratch too big: {A.off - ck5}"
    A.off = ck5 + 33 * 1024
    w1_bf = A.alloc("w1_bf", [128, NEXP * 8 * DEXP], BF16)
    w2_bf = A.alloc("w2_bf", [128, NEXP * 4 * D], BF16)
    wstE = [A.alloc(f"wstE{i}", [128, 512], F32) for i in range(2)]
    wstE2 = [A.alloc(f"wstE2{i}", [128, 512], F32) for i in range(2)]
    b1_bf = A.alloc("b1_bf", [1, NEXP * DEXP], BF16)

    LWEs = [mkdma("LWEa"), mkdma("LWEb")]
    LWE2s = [_alloc_sem("LWE2a"), _alloc_sem("LWE2b")]
    for _s in LWE2s:
        _s.base = 0
    LROWE = mkdma("LROWE")
    WCE = mksem("WCE")
    WCE2 = mksem("WCE2")
    LS2s = [mkdma("LS2a"), mkdma("LS2b")]
    SU3 = mksem("SU3")
    SH2 = mksem("SH2")
    SH2T = mksem("SH2T")
    SH2E = mksem("SH2E")
    STH2s = [mkstore("STH2a"), mkstore("STH2b")]
    SLG = mksem("SLG")
    SLGE = mksem("SLGE")
    SGA = mksem("SGA")
    SGDONE = mksem("SGDONE")
    SGT = mksem("SGT")
    SR2 = mksem("SR2")
    CHE = mksem("CHE")

    E.op("sync", lambda e: e.dma_start(rowstE[0:1, :], s_modrow[0:1, 3072:4096]),
         inc=LROWE.bump(16))
    E.wait("tensor", LROWE, LROWE.n)
    replicate_row(rowstE[0:1, :], rs(0), 0, 1)
    E.wait("sync", RMM, RMM.n)
    E.op("sync", lambda e: e.dma_start(rowstE[0:1, :], s_modrow[0:1, 4096:5120]),
         inc=LROWE.bump(16))
    E.wait("tensor", LROWE, LROWE.n)
    replicate_row(rowstE[0:1, :], rs(1), 2, 3)
    E.wait("sync", RMM, RMM.n)
    E.op("sync", lambda e: e.dma_start(rowstE[0:1, :], s_modrow[0:1, 5120:6144]),
         inc=LROWE.bump(16))
    E.wait("tensor", LROWE, LROWE.n)
    replicate_row(rowstE[0:1, :], rs(2), 4, 5)
    repl_e_done = REV.n

    lw1n = {0: LWEs[0].base, 1: LWEs[1].base}
    lw2n = {0: 0, 1: 0}

    def emit_w1_loads(i0, i1):
        for i in range(i0, i1):
            e_, k = divmod(i, 8)
            w1e = w1_in[e_].rearrange("(kc p) n -> kc p n", p=128)
            if i >= 2:
                waitr("sync", WCE, i - 1)
            E.op("sync", lambda e, s=w1e, kk=k, ii=i: e.dma_start(
                wstE[ii % 2][:, :], s[kk]), inc=LWEs[i % 2].bump(16))

    def emit_w1_casts(i0, i1):
        for i in range(i0, i1):
            e_, k = divmod(i, 8)
            lw1n[i % 2] += 16
            E.wait("vector", LWEs[i % 2], lw1n[i % 2])
            col = (e_ * 8 + k) * 512
            E.op("vector", lambda e, cc=col, ii=i: e.tensor_copy(
                w1_bf[:, cc:cc + 512], wstE[ii % 2][:, :]), inc=WCE.bump())

    def emit_w2_loads(i0, i1):
        for i in range(i0, i1):
            e_, k = divmod(i, 8)
            w2e = w2_in[e_].rearrange("(kc p) n -> kc p n", p=128)
            src = w2e[k // 2][:, (k % 2) * 512:(k % 2) * 512 + 512]
            if i >= 2:
                waitr("gpsimd", WCE2, i - 1)
            E.op("gpsimd", lambda e, s=src, ii=i: e.dma_start(
                wstE2[ii % 2][:, :], s), inc=LWE2s[i % 2].bump(16))

    def emit_w2_casts(i0, i1):
        for i in range(i0, i1):
            e_, k = divmod(i, 8)
            lw2n[i % 2] += 16
            E.wait("scalar", LWE2s[i % 2], lw2n[i % 2])
            col = e_ * 4096 + (k // 2) * 1024 + (k % 2) * 512
            E.op("scalar", lambda e, cc=col, ii=i: e.activation(
                w2_bf[:, cc:cc + 512], wstE2[ii % 2][:, :], AF.Copy),
                inc=WCE2.bump())

    emit_w2_loads(0, 64)

    for t in range(2):
        dma("sync", lambda e, tt=t: e.dma_start(
            s2E_ring[tt % 2][:, :], s_s2[tt, :, :]), LS2s[t % 2],
            label=("s2e", t))
    emit_w1_loads(0, 2)
    emit_w1_casts(0, 2)
    emit_w2_casts(0, 2)

    for t in range(NT_OWN):
        if t + 2 < NT_OWN:
            waitr("sync", SU3, t + 1)
            dma("sync", lambda e, tt=t + 2: e.dma_start(
                s2E_ring[tt % 2][:, :], s_s2[tt, :, :]), LS2s[t % 2],
                label=("s2e", t + 2))
        emit_w1_loads(2 + 4 * t, min(2 + 4 * (t + 1), 64))
        wdma("scalar", ("s2e", t))
        waitr("scalar", SR2, t)
        E.op("scalar", lambda e, tt=t: e.activation(
            u3_buf[:, :], s2E_ring[tt % 2][:, :], AF.Identity,
            scale=stats_s2[:, 2 * tt:2 * tt + 1],
            bias=stats_s2[:, 2 * tt + 1:2 * tt + 2]), inc=SU3.bump())
        E.wait("vector", SU3, SU3.n)
        if t == 0:
            E.wait("vector", REV, repl_e_done)
        waitr("vector", SR2, t)
        waitr("vector", SH2T, t)
        E.op("vector", lambda e: e.scalar_tensor_tensor(
            h1E_buf[:, :], u3_buf[:, :], 1.0, rs(1), ALU.mult, ALU.mult),
            inc=CHE.bump())
        E.wait("vector", CHE, CHE.n)
        E.op("vector", lambda e: e.scalar_tensor_tensor(
            h2bf_buf[:, :], h1E_buf[:, :], 1.0, rs(0), ALU.mult, ALU.add),
            inc=SH2.bump())
        E.wait("scalar", SH2, SH2.n)
        E.op("scalar", lambda e, tt=t: e.activation(
            u3_buf[:, :], h2bf_buf[:, :], AF.Square,
            accum_out=r2_all[:, tt:tt + 1]), inc=SR2.bump())
        E.wait("tensor", SH2, SH2.n)
        waitr("tensor", SH2E, t - 1)
        for c in range(8):
            E.op("tensor", lambda e, cc=c, bk=t % 2: e.transpose(
                bb(bk)[:, ts(cc, 128)], h2bf_buf[:, ts(cc, 128)],
                ident_bf[:, :]), inc=SH2T.bump() if c == 7 else None)
        E.wait("vector", SH2T, SH2T.n)
        wdma("vector", ("sth2", t - 2))
        waitr("vector", SLG, t - 1)
        E.op("vector", lambda e, bk=t % 2, st=h2st[t % 2]: e.tensor_copy(
            st[:, :], bb(bk)[:, 0:D]), inc=SH2E.bump())
        E.wait("scalar", SH2E, SH2E.n)
        dma("scalar", lambda e, tt=t, st=h2st[t % 2]: e.dma_start(
            s_h2T[tt, :, :], st[:, :]), STH2s[t % 2], label=("sth2", t))
        E.wait("tensor", SH2E, SH2E.n)
        waitr("tensor", SLGE, t)
        for k in range(8):
            E.op("tensor", lambda e, kk=k, st=h2st[t % 2]: e.matmul(
                bf(2)[:, 0:8], st[:, ts(kk, 128)], cnT_bf[:, kk * 8:(kk + 1) * 8],
                start=(kk == 0), stop=(kk == 7)))
        for k in range(8):
            E.op("tensor", lambda e, kk=k, st=h2st[t % 2]: e.matmul(
                bf(2)[:, 256:264], st[:, ts(kk, 128)],
                wrT_bf[:, kk * 8:(kk + 1) * 8], start=(kk == 0), stop=(kk == 7)),
                inc=SLG.bump() if k == 7 else None)
        E.wait("vector", SLG, SLG.n)
        E.op("vector", lambda e, tt=t: e.tensor_copy(
            cos_all[:, tt * 8:(tt + 1) * 8], bf(2)[:, 0:8]))
        E.op("vector", lambda e, tt=t: e.tensor_copy(
            route_all[:, tt * 8:(tt + 1) * 8], bf(2)[:, 256:264]),
            inc=SLGE.bump())
        emit_w1_casts(2 + 4 * t, min(2 + 4 * (t + 1), 64))
        emit_w2_casts(2 + 4 * t, min(2 + 4 * (t + 1), 64))

    E.wait("sync", RMM, RMM.n)
    for q in range(4):
        if q > 0:
            E.wait("sync", WCE, WCE.n)
        E.op("sync", lambda e, qq=q: e.dma_start(
            rowstE[0:1, :], b1_row[0:1, qq * 1024:(qq + 1) * 1024]),
            inc=LROWE.bump(16))
        E.wait("vector", LROWE, LROWE.n)
        E.op("vector", lambda e, qq=q: e.tensor_copy(
            b1_bf[0:1, qq * 1024:(qq + 1) * 1024], rowstE[0:1, :]),
            inc=WCE.bump())
    wce_total = WCE.n
    wce2_total = WCE2.n

    def rtg(eng="vector"):
        E.wait(eng, SGA, SGA.n)

    E.wait("scalar", SR2, SR2.n)
    E.op("scalar", lambda e: e.activation(g16["m1"], r2_all[:, 0:NT_OWN],
                                          AF.Ln), inc=SGA.bump())
    rtg("scalar")
    E.op("scalar", lambda e: e.activation(g16["r2s"], g16["m1"], AF.Exp,
                                          scale=0.5), inc=SGA.bump())
    rtg()
    E.op("vector", lambda e: e.tensor_scalar_max(g16["r2s"], g16["r2s"],
                                                 1e-12), inc=SGA.bump())
    rtg()
    E.op("vector", lambda e: e.reciprocal(g16["rinv"], g16["r2s"]),
         inc=SGA.bump())
    rtg()
    E.wait("vector", SLGE, SLGE.n)
    E.op("vector", lambda e: e.tensor_tensor(
        g3(cos_all), g3(cos_all), bc16(g16["rinv"]), ALU.mult),
        inc=SGA.bump())
    rtg()
    E.op("vector", lambda e: e.tensor_add(cos_all[:, :], cos_all[:, :],
                                          route_all[:, :]), inc=SGA.bump())
    rtg()
    E.op("vector", lambda e: e.reduce_max(c16(g16["m1"]), g3(cos_all),
                                          axis=AX.X), inc=SGA.bump())
    rtg()
    E.op("vector", lambda e: e.tensor_tensor(
        g3(u_all), g3(cos_all), bc16(g16["m1"]), ALU.subtract),
        inc=SGA.bump())
    E.op("vector", lambda e: e.tensor_tensor(
        g3(gmask_all), g3(cos_all), bc16(g16["m1"]), ALU.is_ge),
        inc=SGA.bump())
    rtg("scalar")
    E.op("scalar", lambda e: e.activation(gE_all[:, :], u_all[:, :], AF.Exp),
         inc=SGA.bump())
    rtg()
    E.op("vector", lambda e: e.scalar_tensor_tensor(
        gtub_all[:, :], gmask_all[:, :], -1e30, cos_all[:, :],
        ALU.mult, ALU.add), inc=SGA.bump())
    rtg()
    E.op("vector", lambda e: e.reduce_max(c16(g16["m2"]), g3(gtub_all),
                                          axis=AX.X), inc=SGA.bump())
    rtg()
    E.op("vector", lambda e: e.tensor_tensor(
        g3(gmask_all), g3(cos_all), bc16(g16["m2"]), ALU.is_ge),
        inc=SGA.bump())
    rtg()
    E.op("vector", lambda e: e.tensor_mul(gE_all[:, :], gE_all[:, :],
                                          gmask_all[:, :]), inc=SGA.bump())
    rtg()
    E.op("vector", lambda e: e.reduce_sum(c16(g16["rsum"]), g3(gE_all),
                                          axis=AX.X), inc=SGA.bump())
    rtg()
    E.op("vector", lambda e: e.reciprocal(g16["rinv"], g16["rsum"]),
         inc=SGA.bump())
    rtg()
    E.op("vector", lambda e: e.tensor_tensor(
        g3(gates[:, 0:NEXP * NT_OWN]), g3(gE_all), bc16(g16["rinv"]),
        ALU.mult), inc=SGDONE.bump())
    E.wait("tensor", SGDONE, SGDONE.n)
    for b in range(4):
        waitr("tensor", SGT, 2 * b)
        for ti in range(4):
            E.op("tensor", lambda e, tt=b * 4 + ti, cc=ti: e.transpose(
                bf(3)[0:8, cc * 128:(cc + 1) * 128],
                gates[:, tt * 8:(tt + 1) * 8], ident32[:, :]),
                inc=SGT.bump() if ti == 3 else None)
        E.wait("scalar", SGT, SGT.n)
        E.op("scalar", lambda e, bb_=b: e.activation(
            gatesT_bf[0:8, bb_ * 512:(bb_ + 1) * 512], bf(3)[0:8, 0:512],
            AF.Copy), inc=SGT.bump())

    for s in STH2s:
        E.wait("scalar", s, s.n)
    barrier(clear=False)
    if nphases == 5:
        return finalize()

    A.off = ck5
    h2T_ring = [A.alloc(f"h2Tr{i}", [128, D], BF16) for i in range(2)]
    s2b_ring = [A.alloc(f"s2b{i}", [128, D], F32) for i in range(2)]
    tE_f = A.alloc("tE_f", [128, DEXP], F32)
    tEg_bf = [A.alloc(f"tEg{i}", [128, DEXP], BF16) for i in range(2)]
    tT_bf = [A.alloc(f"tTbf{i}", [128, DEXP], BF16) for i in range(3)]
    yo_t1 = A.alloc("yo_t1", [128, D], F32)
    yo_sb = [A.alloc(f"yo{i}", [128, D], F32) for i in range(2)]
    assert A.off <= ck5 + 33 * 1024, f"5b scratch too big: {A.off - ck5}"

    LH2Bs = [mkdma("LH2Ba"), mkdma("LH2Bb")]
    LS2Bs = [mkdma("LS2Ba"), mkdma("LS2Bb")]
    STE = mksem("STE")
    SGEL = mksem("SGEL")
    SSCL = mksem("SSCL")
    STT = mksem("STT")
    STTE = mksem("STTE")
    SOUT = mksem("SOUT")
    SYEV = mksem("SYEV")
    STYs = [mkstore("STYa"), mkstore("STYb")]

    for t in range(2):
        dma("sync", lambda e, tt=t: e.dma_start(
            h2T_ring[tt % 2][:, :], s_h2T[tt, :, :]), LH2Bs[t % 2],
            label=("h2b", t))
        dma("sync", lambda e, tt=t: e.dma_start(
            s2b_ring[tt % 2][:, :], s_s2[tt, :, :]), LS2Bs[t % 2],
            label=("s2b", t))

    E.wait("tensor", WCE, wce_total)
    E.wait("tensor", WCE2, wce2_total)
    for t in range(NT_OWN):
        if t + 2 < NT_OWN:
            waitr("sync", STE, NEXP * (t + 1))
            waitr("sync", SYEV, t + 1)
            dma("sync", lambda e, tt=t + 2: e.dma_start(
                h2T_ring[tt % 2][:, :], s_h2T[tt, :, :]), LH2Bs[t % 2],
                label=("h2b", t + 2))
            dma("sync", lambda e, tt=t + 2: e.dma_start(
                s2b_ring[tt % 2][:, :], s_s2[tt, :, :]), LS2Bs[t % 2],
                label=("s2b", t + 2))
        obks = (4, 5) if (t % 2 == 0) else (6, 7)

        def tail_trc(idx):
            waitr("tensor", SSCL, idx + 1)
            waitr("tensor", STTE, idx - 1)
            for c in range(4):
                E.op("tensor", lambda e, cc=c, ii=idx: e.transpose(
                    bb(2 + ii % 2)[:, ts(cc, 128)],
                    tEg_bf[ii % 2][:, ts(cc, 128)], ident_bf[:, :]),
                    inc=STT.bump() if c == 3 else None)
            E.wait("scalar", STT, STT.n)
            E.op("scalar", lambda e, ii=idx: e.activation(
                tT_bf[ii % 3][:, :], bb(2 + ii % 2)[:, 0:512], AF.Copy),
                inc=STTE.bump())

        def tail_fc2(idx):
            ee = idx % NEXP
            waitr("tensor", STTE, idx + 1)
            if ee == 0:
                waitr("tensor", SYEV, t - 1)
            for n in range(2):
                for k in range(4):
                    E.op("tensor", lambda e, kk=k, nn=n, ee2=ee, ii=idx,
                         b=obks[n]: e.matmul(
                             bf(b)[:, :], tT_bf[ii % 3][:, ts(kk, 128)],
                             w2_bf[:, (ee2 * 4 + kk) * 1024 + nn * 512:
                                   (ee2 * 4 + kk) * 1024 + nn * 512 + 512],
                             start=(ee2 == 0 and kk == 0), stop=False))

        for e_ in range(NEXP):
            idx = t * NEXP + e_
            bk = idx % 2
            if e_ == 0:
                wdma("tensor", ("h2b", t))
            waitr("tensor", SGEL, idx - 1)
            for k in range(8):
                E.op("tensor", lambda e, kk=k, ee=e_, b=bk, ht=h2T_ring[t % 2]:
                     e.matmul(bf(b)[:, :], ht[:, ts(kk, 128)],
                              w1_bf[:, (ee * 8 + kk) * 512:(ee * 8 + kk) * 512 + 512],
                              start=(kk == 0), stop=False))
            E.op("tensor", lambda e, ee=e_, b=bk: e.matmul(
                bf(b)[:, :], ones_bf[0:1, 0:128],
                b1_bf[0:1, ee * 512:(ee + 1) * 512], start=False, stop=True),
                inc=STE.bump())
            E.wait("scalar", STE, STE.n)
            waitr("scalar", SSCL, idx)
            E.op("scalar", lambda e, b=bk: e.activation(
                tE_f[:, :], bf(b)[:, :], AF.Gelu), inc=SGEL.bump())
            E.wait("vector", SGEL, SGEL.n)
            waitr("vector", STT, idx - 1)
            E.op("vector", lambda e, tt=t, ee=e_, ii=idx: e.tensor_scalar(
                tEg_bf[ii % 2][:, :], tE_f[:, :],
                gates[:, tt * 8 + ee: tt * 8 + ee + 1], None, ALU.mult),
                inc=SSCL.bump())
            if e_ > 0:
                tail_trc(idx - 1)
            if e_ > 1:
                tail_fc2(idx - 2)
        tail_trc(t * NEXP + NEXP - 1)
        tail_fc2(t * NEXP + NEXP - 2)
        tail_fc2(t * NEXP + NEXP - 1)
        for n in range(2):
            E.op("tensor", lambda e, nn=n, tt=t, b=obks[n]: e.matmul(
                bf(b)[:, :], gatesT_bf[0:8, tt * 128:(tt + 1) * 128],
                b2_bf[0:8, nn * 512:(nn + 1) * 512], start=False, stop=True),
                inc=SOUT.bump() if n == 1 else None)
        E.wait("vector", SOUT, SOUT.n)
        wdma("vector", ("s2b", t))
        wdma("vector", ("sty", t - 2))
        E.op("vector", lambda e, b=obks[0]: e.scalar_tensor_tensor(
            yo_t1[:, 0:512], bf(b)[:, :], 1.0, rs(2)[:, 0:512],
            ALU.mult, ALU.mult), inc=CHE.bump())
        E.op("vector", lambda e, b=obks[1]: e.scalar_tensor_tensor(
            yo_t1[:, 512:1024], bf(b)[:, :], 1.0, rs(2)[:, 512:1024],
            ALU.mult, ALU.mult), inc=CHE.bump())
        E.wait("vector", CHE, CHE.n)
        E.op("vector", lambda e, tt=t: e.tensor_add(
            yo_sb[tt % 2][:, :], yo_t1[:, :], s2b_ring[tt % 2][:, :]),
            inc=SYEV.bump())
        E.wait("scalar", SYEV, SYEV.n)
        dma("scalar", lambda e, tt=t: e.dma_start(
            y_out[tt, :, :], yo_sb[tt % 2][:, :]), STYs[t % 2],
            label=("sty", t))

    for s in STYs:
        E.wait("scalar", s, s.n)
    return finalize()


def _stage_inputs(inputs):
    f32 = np.float32
    x = np.asarray(inputs["L0_attn_x"], f32)
    c = np.asarray(inputs["L0_c"], f32)

    shared = {
        "wqkvT": np.ascontiguousarray(np.asarray(inputs["L0_attn_qkv_w"], f32).T),
        "wprojT": np.ascontiguousarray(np.asarray(inputs["L0_attn_proj_w"], f32).T),
        "wfc1T": np.ascontiguousarray(np.asarray(inputs["L0_attn_fc1_w"], f32).T),
        "wfc2T": np.ascontiguousarray(np.asarray(inputs["L0_attn_fc2_w"], f32).T),
        "w1_in": np.ascontiguousarray(np.asarray(inputs["L0_ffn_w1"], f32)),
        "w2_in": np.ascontiguousarray(np.asarray(inputs["L0_ffn_w2"], f32)),
        "wmod": np.ascontiguousarray(np.asarray(inputs["L0_wmod"], f32)),
        "wrouteT": np.ascontiguousarray(
            np.asarray(inputs["L0_ffn_w_route"], f32).T),
        "cent_in": np.ascontiguousarray(
            np.asarray(inputs["L0_ffn_centroids"], f32)),
        "b2_in": np.ascontiguousarray(np.asarray(inputs["L0_ffn_b2"], f32)),
        "qkvb_row": np.asarray(inputs["L0_attn_qkv_b"], f32)[None, :].copy(),
        "projb_row": np.asarray(inputs["L0_attn_proj_b"], f32)[None, :].copy(),
        "fc1b_fm_in": np.ascontiguousarray(
            np.asarray(inputs["L0_attn_fc1_b"], f32).reshape(32, 128).T),
        "fc2b_row": np.asarray(inputs["L0_attn_fc2_b"], f32)[None, :].copy(),
        "b1_row": np.asarray(inputs["L0_ffn_b1"], f32).reshape(1, -1).copy(),
        "lnrow": np.concatenate([
            np.asarray(inputs["L0_attn_ln1_w"], f32),
            np.asarray(inputs["L0_attn_ln1_b"], f32),
            np.asarray(inputs["L0_attn_ln2_w"], f32),
            np.asarray(inputs["L0_attn_ln2_b"], f32)])[None, :].copy(),
        "ident_bf_in": np.eye(128, dtype=ml_dtypes.bfloat16),
        "ident32_in": np.eye(128, dtype=f32),
    }

    in_maps = []
    for core in range(8):
        b = core // 2
        half = core % 2
        s0 = half * 2048
        ext = np.zeros((NT_EXT * TOK, D), f32)
        ext[128:128 + 2048] = x[b, s0:s0 + 2048]
        if half == 0:
            ext[128 + 2048:] = x[b, 2048:2048 + 128]
        else:
            ext[0:128] = x[b, 1920:2048]
        mask = np.zeros((1, NT_OWN * WIN), f32)
        if half == 0:
            mask[0, 0:128] = -1e30
        else:
            mask[0, 15 * WIN + 256:16 * WIN] = -1e30
        m = dict(shared)
        m["x_ext"] = ext
        m["maskrow"] = mask
        m["c_row8"] = np.ascontiguousarray(c[b].reshape(8, 128))
        in_maps.append(m)
    return in_maps


_prog_cache = {}


def _get_program(taps=()):
    key = tuple(taps)
    if key not in _prog_cache:
        _prog_cache[key] = build_program(taps)
    return _prog_cache[key]


def kernel(**inputs):
    in_maps = _stage_inputs(inputs)
    nc = _get_program()
    res = run_bass_kernel_spmd(nc, in_maps, core_ids=list(range(8)))
    out = np.zeros((4, 4096, 1024), np.float32)
    for core in range(8):
        b = core // 2
        half = core % 2
        y = np.asarray(res.results[core]["y"]).reshape(2048, 1024)
        out[b, half * 2048:(half + 1) * 2048] = y
    return out
